# version 01
# baseline (speedup 1.0000x reference)
import numpy as np
import ml_dtypes

import concourse.bass as bass
import concourse.mybir as mybir
from concourse.bass_utils import run_bass_kernel_spmd

F32 = mybir.dt.float32
BF16 = mybir.dt.bfloat16
ALU = mybir.AluOpType
ACT = mybir.ActivationFunctionType
AX = mybir.AxisListType

P = 128
T = 2048
NT = 16
D = 1024
DI = 2048
DS = 16
LN_EPS = 1e-5
NEG = -1.0e30
BF = ml_dtypes.bfloat16

ENGS = ["ten", "vec", "sca", "pol", "sp"]
DMA_SEM = {"sp": "dsp", "sca": "dact", "pol": "dpol"}
NDMASEM = 6


def f32(x):
    return np.ascontiguousarray(x, dtype=np.float32)


class Prog:

    def __init__(self, nc, sems, cnt, mk=None):
        self.nc = nc
        self.ops = []
        self.sems = sems
        self.cnt = cnt
        self.mk = mk

    def op(self, eng, fn, waits=(), tag=""):
        sem = eng
        prev = self.cnt[sem]
        self.cnt[sem] += 1
        tok = (sem, self.cnt[sem])
        w = self._clean(waits, eng)
        if prev > 0:
            w.append((sem, prev))
        self.ops.append((eng, fn, w, [(sem, 1)], tag))
        return tok

    def _dma_sem(self, eng):
        q = DMA_SEM[eng]
        idx = self.mk.dma_rr[q]
        self.mk.dma_rr[q] = (idx + 1) % NDMASEM
        return f"{q}{idx}"

    def dma(self, eng, dst, src, waits=(), tag=""):
        sem = self._dma_sem(eng)
        prev = self.cnt[sem]
        self.cnt[sem] += 16
        tok = (sem, self.cnt[sem])
        waits = list(waits) + [(sem, prev)]
        if self.cnt.get(eng, 0):
            waits.append((eng, self.cnt[eng]))

        def _dma(e, d=dst, s=src, t=tag):
            try:
                return e.dma_start(d, s)
            except Exception:
                print(f"DMA FAIL tag={t} dst={d} src={s}")
                raise
        self.ops.append((eng, _dma, self._clean(waits, eng), [(sem, 16)], tag))
        return tok

    def dma_hv(self, eng, dst_fn, src_fn, waits=(), tag=""):
        sem = self._dma_sem(eng)
        prev = self.cnt[sem]
        self.cnt[sem] += 16
        tok = (sem, self.cnt[sem])
        waits = list(waits) + [(sem, prev)]
        if self.cnt.get(eng, 0):
            waits.append((eng, self.cnt[eng]))
        mk = self.mk
        s_h = self.sems[sem]

        def _dma(e, t=tag):
            hreg = mk.half_reg(e, eng)
            try:
                with e.If_eq(hreg, 0):
                    e.dma_start(dst_fn(0), src_fn(0)).then_inc(s_h, 16).annotate(t + "_h0")
                with e.Else():
                    e.dma_start(dst_fn(1), src_fn(1)).then_inc(s_h, 16).annotate(t + "_h1")
            except Exception:
                print(f"DMAHV FAIL tag={t}")
                raise
            return None
        self.ops.append((eng, _dma, self._clean(waits, eng), [], tag))
        return tok

    def _clean(self, waits, eng):
        m = {}
        for w in waits:
            if w is None:
                continue
            s, v = w
            if s == eng:
                continue
            if m.get(s, -1) < v:
                m[s] = v
        return list(m.items())

    def barrier_all(self):
        snap = [(s, v) for s, v in self.cnt.items() if v > 0]
        toks = []
        for e in ENGS:
            toks.append(self.op(e, lambda eng: eng.nop(), waits=snap, tag="pbar"))
        return toks

    def emit(self):
        nc = self.nc
        with nc.Block() as blk:
            def mk_runner(ename):
                def run(e):
                    for eng, fn, waits, incs, tag in self.ops:
                        if eng != ename:
                            continue
                        for s, v in waits:
                            e.wait_ge(self.sems[s], v)
                        inst = fn(e)
                        if inst is None and incs:
                            inst = e.nop()
                        for s, n in incs:
                            inst.then_inc(self.sems[s], n)
                return run
            blk.tensor(mk_runner("ten"))
            blk.vector(mk_runner("vec"))
            blk.scalar(mk_runner("sca"))
            blk.gpsimd(mk_runner("pol"))
            blk.sync(mk_runner("sp"))


class Arena:
    def __init__(self, ap_f32, ncols_f32):
        self.base = ap_f32
        self.n = ncols_f32
        self.off = 0

    def reset(self):
        self.off = 0

    def take(self, ncols, dtype=F32):
        if dtype == F32:
            a = self.base[:, self.off:self.off + ncols]
            self.off += ncols
        else:
            nf = (ncols + 1) // 2
            a = self.base[:, self.off:self.off + nf].bitcast(BF16)
            self.off += nf
        assert self.off <= self.n, f"arena overflow {self.off} > {self.n}"
        return a


def prep_host(inputs):
    x = np.asarray(inputs["L0_attn_x"], dtype=np.float32)
    c = {}
    c["w_in_T"] = f32(np.asarray(inputs["L0_attn_in_w"]).T)
    c["in_b_pc"] = f32(np.asarray(inputs["L0_attn_in_b"]).reshape(32, P).T)
    cw = np.asarray(inputs["L0_attn_conv_w"])[:, 0, :]
    c["conv_w_pc"] = f32(cw.reshape(16, P, 4).transpose(1, 0, 2).reshape(P, 64))
    c["conv_b_pc"] = f32(np.asarray(inputs["L0_attn_conv_b"]).reshape(16, P).T)
    c["xproj_T"] = f32(np.asarray(inputs["L0_attn_xproj_w"]).T)
    c["dt_w_row"] = f32(np.asarray(inputs["L0_attn_dt_w"])[:, 0].reshape(1, DI))
    c["dt_b_pc"] = f32(np.asarray(inputs["L0_attn_dt_b"]).reshape(16, P).T)
    c["out_w_T"] = f32(np.asarray(inputs["L0_attn_out_w"]).T)

    ipw = np.asarray(inputs["L0_ffn_in_proj_w"])
    ipb = np.asarray(inputs["L0_ffn_in_proj_b"])
    c["wq0_T"] = f32(ipw[0:D].T)
    c["wk0_T"] = f32(ipw[D:2 * D].T)
    c["wv0_T"] = f32(ipw[2 * D:3 * D].T)
    c["bq0_pc"] = f32(ipb[0:D].reshape(8, P).T)
    c["bk0_pc"] = f32(ipb[D:2 * D].reshape(8, P).T)
    c["bv0_pc"] = f32(ipb[2 * D:3 * D].reshape(8, P).T)
    c["wo0_T"] = f32(np.asarray(inputs["L0_ffn_out_proj_w"]).T)
    c["bo0_bc"] = f32(np.broadcast_to(np.asarray(inputs["L0_ffn_out_proj_b"]), (P, D)))
    c["gate_T"] = f32(np.asarray(inputs["L0_ffn_gate_w"]).T)
    c["gate_b_bc"] = f32(np.broadcast_to(np.asarray(inputs["L0_ffn_gate_b"]), (P, 8)))
    c["ew_T"] = f32(np.asarray(inputs["L0_ffn_expert_w"]).transpose(0, 2, 1))
    c["eb"] = f32(np.asarray(inputs["L0_ffn_expert_b"]))
    c["ln1_w_bc"] = f32(np.broadcast_to(np.asarray(inputs["L0_ffn_ln1_w"]), (P, D)))
    c["ln1_b_bc"] = f32(np.broadcast_to(np.asarray(inputs["L0_ffn_ln1_b"]), (P, D)))
    c["ln2_w_bc"] = f32(np.broadcast_to(np.asarray(inputs["L0_ffn_ln2_w"]), (P, D)))
    c["ln2_b_bc"] = f32(np.broadcast_to(np.asarray(inputs["L0_ffn_ln2_b"]), (P, D)))

    c["wq1"] = f32(np.asarray(inputs["L1_attn_Wq"]).reshape(D, D))
    c["wk1"] = f32(np.asarray(inputs["L1_attn_Wk"]).reshape(D, D))
    c["wv1"] = f32(np.asarray(inputs["L1_attn_Wv"]).reshape(D, D))
    c["wo1"] = f32(np.asarray(inputs["L1_attn_Wout"]).reshape(D, D))

    c["w_router"] = f32(np.asarray(inputs["L1_ffn_w_router"]))
    c["w1"] = f32(np.asarray(inputs["L1_ffn_w1"]))
    c["w3"] = f32(np.asarray(inputs["L1_ffn_w3"]))
    c["w2"] = f32(np.asarray(inputs["L1_ffn_w2"]))
    c["sw1"] = f32(np.asarray(inputs["L1_ffn_sw1"]))
    c["sw3"] = f32(np.asarray(inputs["L1_ffn_sw3"]))
    c["sw2"] = f32(np.asarray(inputs["L1_ffn_sw2"]))

    c["ident"] = f32(np.eye(P))
    c["eps_pc"] = np.full((P, 1), LN_EPS, np.float32)
    c["ones_row"] = np.ones((1, P), np.float32)
    mq = np.arange(P)[:, None]
    mk_ = np.arange(P)[None, :]
    c["diag_mask"] = f32(np.where(mk_ <= mq, 0.0, NEG))
    mp = np.arange(P)[:, None]
    mf = np.arange(512)[None, :]
    c["maskT"] = f32(np.concatenate(
        [np.where(mp + j * P > mf, NEG, 0.0) for j in range(4)], axis=1))

    in_maps = []
    for core in range(8):
        b, h = core // 2, core % 2
        d = dict(c)
        d["x_in"] = f32(x[b, h * T:(h + 1) * T])
        if h == 0:
            d["x_halo"] = np.zeros((16, D), np.float32)
            d["halo_b_pc"] = np.zeros((P, 16), np.float32)
        else:
            d["x_halo"] = f32(x[b, h * T - 16:h * T])
            d["halo_b_pc"] = f32(np.asarray(inputs["L0_attn_in_b"])[:DI].reshape(16, P).T)
        A = np.asarray(inputs["L0_attn_A_log"], dtype=np.float32)
        Asl = -np.exp(A[h * 1024:(h + 1) * 1024])
        d["A_sc"] = f32(Asl.reshape(8, P, DS).transpose(1, 0, 2).reshape(P, 8 * DS))
        Dv = np.asarray(inputs["L0_attn_D"], dtype=np.float32)[h * 1024:(h + 1) * 1024]
        d["D_sc"] = f32(Dv.reshape(8, P).T)
        pos = np.arange(h * T, (h + 1) * T, dtype=np.float32)
        inv = 1.0 / (10000.0 ** (np.arange(0, 64, 2, dtype=np.float32) / 64.0))
        fr = pos[:, None] * inv[None, :]
        d["rope_cos"] = f32(np.cos(fr).reshape(NT, P, 32).transpose(1, 0, 2).reshape(P, NT * 32))
        d["rope_sin"] = f32(np.sin(fr).reshape(NT, P, 32).transpose(1, 0, 2).reshape(P, NT * 32))
        in_maps.append(d)
    return in_maps


IN_SHAPES = {
    "x_in": (T, D), "x_halo": (16, D), "halo_b_pc": (P, 16),
    "w_in_T": (D, 2 * DI), "in_b_pc": (P, 32),
    "conv_w_pc": (P, 64), "conv_b_pc": (P, 16),
    "xproj_T": (DI, 33), "dt_w_row": (1, DI), "dt_b_pc": (P, 16),
    "out_w_T": (DI, D),
    "A_sc": (P, 128), "D_sc": (P, 8),
    "wq0_T": (D, D), "wk0_T": (D, D), "wv0_T": (D, D),
    "bq0_pc": (P, 8), "bk0_pc": (P, 8), "bv0_pc": (P, 8),
    "wo0_T": (D, D), "bo0_bc": (P, D),
    "gate_T": (D, 8), "gate_b_bc": (P, 8),
    "ew_T": (8, D, D), "eb": (8, D),
    "ln1_w_bc": (P, D), "ln1_b_bc": (P, D),
    "ln2_w_bc": (P, D), "ln2_b_bc": (P, D),
    "wq1": (D, D), "wk1": (D, D), "wv1": (D, D), "wo1": (D, D),
    "w_router": (D, 8), "w1": (8, D, DI), "w3": (8, D, DI),
    "w2": (8, DI, D), "sw1": (D, DI), "sw3": (D, DI), "sw2": (DI, D),
    "ident": (P, P), "diag_mask": (P, P), "maskT": (P, 4 * 512), "eps_pc": (P, 1),
    "ones_row": (1, P),
    "rope_cos": (P, NT * 32), "rope_sin": (P, NT * 32),
}
CONST_NAMES = ["eps_pc", "in_b_pc", "conv_w_pc", "conv_b_pc", "dt_b_pc", "halo_b_pc",
               "A_sc", "D_sc", "ident", "diag_mask", "bq0_pc", "bk0_pc",
               "bv0_pc", "gate_b_bc", "dt_w_row", "ones_row"]


def dslice(th, row0, nrows, ncols_total, col0, ncols):
    return bass.AP(th, row0 * ncols_total + col0,
                   [[ncols_total, nrows], [1, ncols]])


class MK:
    def __init__(self, taps=(), stages=("produce", "scan", "out", "l0ffn", "l1")):
        self.stages = set(stages)
        self.taps = set(taps)
        self.nc = bass.Bass("TRN2", target_bir_lowering=False, num_devices=8)
        self.inp = {}
        self.out = {}
        self.sems = {}
        self.cnt = {}
        names = ["ten", "vec", "sca", "pol", "sp"]
        for q in ["dsp", "dact", "dpol"]:
            names += [f"{q}{i}" for i in range(NDMASEM)]
        for n in names:
            self.sems[n] = self.nc.alloc_semaphore("s_" + n)
            self.cnt[n] = 0
        self.dma_rr = {"dsp": 0, "dact": 0, "dpol": 0}

    def newprog(self):
        return Prog(self.nc, self.sems, self.cnt, mk=self)

    def half_reg(self, e, eng):
        if not hasattr(self, "_eregs"):
            self._eregs = {}
        r = self._eregs.get(eng)
        if r is None:
            r = e.to_reg(e.compute_val(self.half))
            self._eregs[eng] = r
        return r

    def din(self, name, shape, dtype=F32):
        h = self.nc.dram_tensor(name, list(shape), dtype, kind="ExternalInput")
        self.inp[name] = h
        return h

    def dout(self, name, shape, dtype=F32):
        h = self.nc.dram_tensor(name, list(shape), dtype, kind="ExternalOutput")
        self.out[name] = h
        return h

    def tap(self, pr, name, sb_ap, waits):
        if name not in self.taps:
            return None
        shape = list(sb_ap.shape)
        h = self.dout("tap_" + name, shape, sb_ap.dtype)
        return pr.dma("sp", h.ap(), sb_ap, waits=waits, tag="tap_" + name)

    def build(self):
        nc = self.nc
        for n, s in IN_SHAPES.items():
            self.din(n, s, F32)
        self.dout("y", (T, D), F32)

        sh = lambda name, shape, dt: nc.dram_tensor(name, list(shape), dt,
                                                    kind="Internal", addr_space="Shared")
        self.XCb = sh("XCb", (2, DI, T), BF16)
        self.DLb = sh("DLb", (2, DI, T), F32)
        self.Bb = sh("Bbuf", (2, DS, T), F32)
        self.Cb = sh("Cbuf", (2, DS, T), F32)
        self.Yb = sh("Ybuf", (2, 1024, 2 * T), F32)
        self.Qb = [sh(f"Qbuf{l}", (2, D, T), BF16) for l in range(2)]
        self.Kb = [sh(f"Kbuf{l}", (2, D, T), BF16) for l in range(2)]
        self.Vb = [sh(f"Vbuf{l}", (2, D, T), BF16) for l in range(2)]
        self.Ob = [sh(f"Obuf{l}", (D, 2 * T), BF16) for l in range(2)]

        self.RES = nc.alloc_sbuf_tensor("RES", [P, NT * D], F32)
        self.CONST = {}
        for n in CONST_NAMES:
            self.CONST[n] = nc.alloc_sbuf_tensor("c_" + n, list(IN_SHAPES[n]), F32)
        self.IDB = nc.alloc_sbuf_tensor("IDB", [P, P], BF16)
        self.SML = nc.alloc_sbuf_tensor("SML", [P, 1024], F32)
        AR = nc.alloc_sbuf_tensor("AR", [P, 32 * 1024], F32)
        self.AR = Arena(AR[:, :], 32 * 1024)
        self.PB = [nc.alloc_psum_tensor(f"pb{i}", [P, 512], F32) for i in range(7)]
        self.PBF = nc.alloc_psum_tensor("pbf", [P, 1024], BF16)

        pid = nc.partition_id()
        self.half = pid % 2
        self.conds = [1 - self.half, self.half]

        pr = self.newprog()
        ld = []
        for n in CONST_NAMES:
            ld.append(pr.dma("sp", self.CONST[n][:, :], self.inp[n].ap()))
        ld.append(pr.dma("pol", self.IDB[:, :], self.inp["ident"].ap()))
        xin = self.inp["x_in"].ap().rearrange("(nt p) d -> nt p d", p=P)
        res_ld = [pr.dma("sp", self.RES[:, t * D:(t + 1) * D], xin[t])
                  for t in range(NT)]
        self.ld_const = ld
        self.mamba_produce(pr, res_ld)
        pr.emit()
        nc.all_core_barrier()

        if "scan" in self.stages:
            pr = self.newprog()
            self.mamba_scan(pr)
            pr.emit()
            nc.all_core_barrier()

        pr = self.newprog()
        if "out" in self.stages:
            self.mamba_out(pr)
        if "l0ffn" in self.stages or "l0heads" in self.stages or "l0prod" in self.stages:
            self.attn_produce(pr, 0)
            pr.emit()
            nc.all_core_barrier()
            pr = self.newprog()
            if "l0ffn" in self.stages or "l0heads" in self.stages:
                self.attn_heads(pr, 0)
                pr.emit()
                nc.all_core_barrier()
                pr = self.newprog()
            if "l0ffn" in self.stages:
                self.attn_out(pr, 0)
                self.moe_l0(pr)
        if "l1" in self.stages:
            self.attn_produce(pr, 1)
            pr.emit()
            nc.all_core_barrier()
            pr = self.newprog()
            self.attn_heads(pr, 1)
            pr.emit()
            nc.all_core_barrier()
            pr = self.newprog()
            self.attn_out(pr, 1)
            self.ffn_l1(pr)

        if "stage" in self.taps:
            d_xc = self.dout("tap_xc", (DI, T), BF16)
            d_dl = self.dout("tap_dl", (DI, T), F32)
            d_b = self.dout("tap_b", (DS, T), F32)
            d_c = self.dout("tap_c", (DS, T), F32)
            d_y = (self.dout("tap_y", (1024, 2 * T), F32)
                   if "scan" in self.stages else None)
            bar0 = pr.barrier_all()
            pr.dma_hv("pol", lambda h: d_xc.ap(),
                      lambda h: dslice(self.XCb, h * DI, DI, T, 0, T),
                      waits=bar0, tag="txc")
            pr.dma_hv("pol", lambda h: d_dl.ap(),
                      lambda h: dslice(self.DLb, h * DI, DI, T, 0, T), tag="tdl")
            pr.dma_hv("pol", lambda h: d_b.ap(),
                      lambda h: dslice(self.Bb, h * DS, DS, T, 0, T), tag="tb")
            pr.dma_hv("pol", lambda h: d_c.ap(),
                      lambda h: dslice(self.Cb, h * DS, DS, T, 0, T), tag="tc")
            if "scan" in self.stages:
                pr.dma_hv("pol", lambda h: d_y.ap(),
                          lambda h: dslice(self.Yb, h * 1024, 1024, 2 * T, 0, 2 * T),
                          tag="ty")

        fin = pr.barrier_all()
        yo = self.out["y"].ap().rearrange("(nt p) d -> nt p d", p=P)
        st = [pr.dma("sp", yo[t], self.RES[:, t * D:(t + 1) * D], waits=fin)
              for t in range(NT)]
        pr.op("sp", lambda e: None, waits=st, tag="final")
        pr.emit()
        return nc

    def ln_one(self, pr, x_ap, out_ap, tmp, st, wb=None, waits=(), rows=P,
               width=D, war=()):
        mean, var, inv, sumsq, sd = (st[:, 0:1], st[:, 1:2], st[:, 2:3],
                                     st[:, 3:4], st[:, 4:5])
        w_in = list(waits)
        t_sq = pr.op("sca", lambda e: e.activation(tmp, x_ap, ACT.Square,
                     accum_out=sumsq), waits=w_in + list(war), tag="ln_sq")
        pr.op("vec", lambda e: e.reduce_sum(mean, x_ap, axis=AX.X),
              waits=w_in, tag="ln_sum")
        pr.op("vec", lambda e: e.tensor_scalar(mean, mean, 1.0 / width, None,
              ALU.mult), tag="ln_mean")
        pr.op("vec", lambda e: e.tensor_tensor(sd, mean, mean, ALU.mult), tag="ln_m2")
        pr.op("vec", lambda e: e.tensor_scalar(var, sumsq, 1.0 / width, None,
              ALU.mult), waits=[t_sq], tag="ln_v0")
        t_var = pr.op("vec", lambda e: e.tensor_tensor(var, var, sd, ALU.subtract),
                      tag="ln_var")
        t_sd = pr.op("sca", lambda e: e.activation(sd, var, ACT.Sqrt,
                     bias=self.CONST["eps_pc"][0:var.shape[0], 0:1]),
                     waits=[t_var], tag="ln_sd")
        t_inv = pr.op("vec", lambda e: e.reciprocal(inv, sd), waits=[t_sd], tag="ln_inv")
        if wb is None:
            tk = pr.op("vec", lambda e: e.tensor_scalar(out_ap, x_ap, mean, inv,
                       ALU.subtract, ALU.mult), waits=list(war), tag="ln_app")
        else:
            w_bc, b_bc = wb
            pr.op("vec", lambda e: e.tensor_scalar(tmp, x_ap, mean, inv,
                  ALU.subtract, ALU.mult), waits=[t_sq] + list(war), tag="ln_a1")
            pr.op("vec", lambda e: e.tensor_tensor(tmp, tmp, w_bc, ALU.mult),
                  tag="ln_a2")
            tk = pr.op("vec", lambda e: e.tensor_tensor(out_ap, tmp, b_bc, ALU.add),
                       waits=list(war), tag="ln_a3")
        return tk

    def mamba_produce(self, pr, res_ld):
        nc = self.nc
        C = self.CONST
        A = self.AR
        A.reset()
        PB = self.PB

        ZFM = A.take(8 * T, BF16)
        LNT = A.take(D)
        LNSQ = A.take(D)
        XH = A.take(16 + T)
        REG1 = A.take(3 * T)
        ZCH = REG1[:, 0:T]
        SZ = REG1[:, T:2 * T]
        CA = REG1[:, 2 * T:3 * T]
        DLT = ZCH
        PRJ = SZ
        XCC = [A.take(T, BF16), A.take(T, BF16)]
        WIN = [A.take(2 * D, BF16), A.take(2 * D, BF16)]
        XPJ = A.take(16 * 33 + 16, BF16)
        HFM = A.take(8 * 16, BF16)
        HX = LNT

        half = self.half
        ident = C["ident"][:, :]

        zfm_ev = []
        for t in range(NT):
            tk = self.ln_one(pr, self.RES[:, t * D:(t + 1) * D], LNT, LNSQ,
                             self.SML[:, 8 * t:8 * t + 8],
                             waits=[res_ld[t]] + self.ld_const[:2],
                             war=[zfm_ev[-1]] if zfm_ev else [])
            for c in range(8):
                pb = PB[5 + (c % 2)]
                t_tr = pr.op("ten", lambda e, cc=c, b=pb: e.transpose(
                    b[:, 0:128], LNT[:, cc * 128:(cc + 1) * 128], ident),
                    waits=[tk, zfm_ev[-2] if len(zfm_ev) >= 2 else None], tag="zt")
                t_ev = pr.op("sca", lambda e, b=pb, cc=c, tt=t: e.copy(
                    ZFM[:, cc * T + tt * P: cc * T + (tt + 1) * P], b[:, 0:128]),
                    waits=[t_tr], tag="zte")
                zfm_ev.append(t_ev)
        zfm_done = zfm_ev[-1]

        t_hx = pr.dma("sp", HX[0:16, :], self.inp["x_halo"].ap(),
                      waits=[zfm_ev[-1]], tag="halo_ld")
        st = self.SML[0:16, 8 * NT:8 * NT + 8]
        t_hln = self.ln_one(pr, HX[0:16, :], HX[0:16, :], LNSQ[0:16, :], st,
                            waits=[t_hx], rows=16, war=[zfm_done])
        halo_fm = []
        for c in range(8):
            pb = PB[5 + c % 2]
            t_tr = pr.op("ten", lambda e, cc=c, b=pb: e.transpose(
                b[:, 0:16], HX[0:16, cc * 128:(cc + 1) * 128], ident[0:16, 0:16]),
                waits=[t_hln, zfm_ev[-1], halo_fm[-1] if halo_fm else None], tag="htr")
            t_ev = pr.op("sca", lambda e, b=pb, cc=c: e.copy(
                HFM[:, cc * 16:(cc + 1) * 16], b[:, 0:16]), waits=[t_tr], tag="htr_ev")
            halo_fm.append(t_ev)

        xp = self.inp["xproj_T"].ap().rearrange("(c p) n -> c p n", p=P)
        xpj_toks = [pr.dma("pol", XPJ[:, cc * 33:(cc + 1) * 33], xp[cc], tag="xpj")
                    for cc in range(16)]

        win_ap = self.inp["w_in_T"].ap().rearrange("(kc p) n -> p kc n", p=P)

        bank_tok = {4: zfm_ev[-1], 5: zfm_ev[-1], 6: halo_fm[-1], 7: None}
        prev_use = [None, None]
        xc_war = [None, None]
        xh_war = None
        sz_war = None
        xc_st = []
        for co in range(16):
            wbuf = WIN[co % 2]
            t_w1 = pr.dma("pol", wbuf[:, 0:1024].rearrange("p (kc n) -> p kc n", n=128),
                          win_ap[:, :, co * 128:(co + 1) * 128],
                          waits=[prev_use[co % 2]], tag="w_ld_x")
            t_w2 = pr.dma("pol", wbuf[:, 1024:2048].rearrange("p (kc n) -> p kc n", n=128),
                          win_ap[:, :, (16 + co) * 128:(17 + co) * 128], tag="w_ld_z")

            mm = None
            for kc in range(8):
                mm = pr.op("ten", lambda e, k=kc, wb=wbuf: e.matmul(
                    PB[6][:, 0:16], wb[:, k * 128:(k + 1) * 128],
                    HFM[:, k * 16:(k + 1) * 16], start=(k == 0), stop=(k == 7)),
                    waits=[t_w1, halo_fm[-1], bank_tok[6], xh_war], tag="hmm")
            t_he = pr.op("sca", lambda e, c_=co: e.activation(
                XH[:, 0:16], PB[6][:, 0:16], ACT.Identity,
                bias=C["halo_b_pc"][:, c_:c_ + 1]), waits=[mm], tag="hev")
            bank_tok[6] = t_he

            last_ev = {0: None, 1: None}
            for which, woff in ((0, 0), (1, 1024)):
                dst = XH if which == 0 else ZCH
                for tt in range(4):
                    bk = 4 + tt % 2
                    mm = None
                    for kc in range(8):
                        mm = pr.op("ten", lambda e, k=kc, wb=wbuf, o=woff, t_=tt, b=bk:
                                   e.matmul(PB[b][:, :],
                                            wb[:, o + k * 128:o + (k + 1) * 128],
                                            ZFM[:, k * T + t_ * 512:k * T + (t_ + 1) * 512],
                                            start=(k == 0), stop=(k == 7)),
                                   waits=[t_w1 if which == 0 else t_w2, zfm_done,
                                          bank_tok[bk], xh_war, sz_war], tag="xzmm")
                    off = (16 + tt * 512) if which == 0 else tt * 512
                    t_ev = pr.op("sca", lambda e, d=dst, o=off, c_=co, wh=which, b=bk:
                                 e.activation(d[:, o:o + 512], PB[b][:, :], ACT.Identity,
                                 bias=C["in_b_pc"][:, c_ + 16 * wh:c_ + 16 * wh + 1]),
                                 waits=[mm], tag="xzev")
                    bank_tok[bk] = t_ev
                    last_ev[which] = t_ev
            t_xh, t_z = last_ev[0], last_ev[1]
            prev_use[co % 2] = t_z

            t_c = pr.op("vec", lambda e, c_=co: e.tensor_scalar(
                CA, XH[:, 13:13 + T], C["conv_w_pc"][:, 4 * c_:4 * c_ + 1], None,
                ALU.mult), waits=[t_xh, t_he], tag="cv0")
            for k in range(1, 4):
                t_c = pr.op("vec", lambda e, kk=k, c_=co:
                            e.scalar_tensor_tensor(CA, XH[:, 13 + kk:13 + kk + T],
                            C["conv_w_pc"][:, 4 * c_ + kk:4 * c_ + kk + 1], CA,
                            ALU.mult, ALU.add), tag=f"cv{k}")
            t_c = pr.op("vec", lambda e, c_=co: e.tensor_scalar(
                CA, CA, C["conv_b_pc"][:, c_:c_ + 1], None, ALU.add), tag="cvb")
            t_sz = pr.op("sca", lambda e: e.activation(
                SZ, ZCH, ACT.Silu), waits=[t_z, sz_war], tag="silu")
            t_xc = pr.op("vec", lambda e, o_=XCC[co % 2]:
                         e.tensor_tensor(o_, CA, SZ, ALU.mult),
                         waits=[t_c, t_sz] + list(xc_war[co % 2] or ()), tag="xcm")
            xh_war = t_xc
            sz_war = t_xc

            t_st = pr.dma_hv("pol",
                lambda h, c_=co: dslice(self.XCb, h * DI + c_ * P, P, T, 0, T),
                lambda h, c_=co: XCC[c_ % 2],
                waits=[t_xc], tag="xcst")
            xc_st.append(t_st)

            pj = None
            for tt in range(4):
                pj = pr.op("ten", lambda e, c_=co, t_=tt: e.matmul(
                    PB[t_][0:33, :], XPJ[:, c_ * 33:(c_ + 1) * 33],
                    XCC[c_ % 2][:, t_ * 512:(t_ + 1) * 512],
                    start=(c_ == 0), stop=(c_ == 15)),
                    waits=[t_xc, xpj_toks[co]], tag="pjmm")
            xc_war[co % 2] = (pj, t_st)

        prj_toks = []
        for tt in range(4):
            tk = pr.op("vec", lambda e, t_=tt: e.tensor_copy(
                PRJ[0:33, t_ * 512:(t_ + 1) * 512], PB[t_][0:33, :]),
                waits=[("ten", self.cnt["ten"])], tag="pjev")
            prj_toks.append(tk)
        tB = pr.dma_hv("pol", lambda h: dslice(self.Bb, h * DS, DS, T, 0, T),
                       lambda h: PRJ[1:17, :], waits=prj_toks, tag="Bst")
        tC = pr.dma_hv("pol", lambda h: dslice(self.Cb, h * DS, DS, T, 0, T),
                       lambda h: PRJ[17:33, :], waits=prj_toks, tag="Cst")

        bank_tok2 = {4: bank_tok[4], 5: bank_tok[5]}
        dst_tok = None
        for co in range(16):
            dts = []
            for tt in range(4):
                bk = 4 + tt % 2
                t_mm = pr.op("ten", lambda e, c_=co, t_=tt, b=bk: e.matmul(
                    PB[b][:, :], C["dt_w_row"][0:1, c_ * 128:(c_ + 1) * 128],
                    PRJ[0:1, t_ * 512:(t_ + 1) * 512], start=True, stop=True),
                    waits=prj_toks + [bank_tok2[bk]], tag="dtmm")
                t_e = pr.op("sca", lambda e, c_=co, t_=tt, b=bk:
                            e.activation(DLT[:, t_ * 512:(t_ + 1) * 512], PB[b][:, :],
                            ACT.Exp, bias=C["dt_b_pc"][:, c_:c_ + 1]),
                            waits=[t_mm, dst_tok], tag="dtex")
                bank_tok2[bk] = t_e
                dts.append(t_e)
            t_p1 = pr.op("vec", lambda e: e.tensor_scalar(
                DLT, DLT, 1.0, None, ALU.add), waits=dts, tag="dtp1")
            dst_tok = pr.dma_hv("pol",
                lambda h, c_=co: dslice(self.DLb, h * DI + c_ * P, P, T, 0, T),
                lambda h: DLT, waits=[t_p1], tag="dlst")

        self.tap(pr, "zfm", ZFM, [zfm_done])
        pr.barrier_all()

    def mamba_scan(self, pr):
        C = self.CONST
        A = self.AR
        A.reset()
        half = self.half
        TC = 1024

        DLs = A.take(8 * TC)
        XCs = A.take(8 * TC, BF16)
        Ws = A.take(8 * TC, BF16)
        Ya = A.take(8 * TC, BF16)
        BR = [A.take(TC, BF16), A.take(TC, BF16)]
        CR = [A.take(TC, BF16), A.take(TC, BF16)]
        AT = [A.take(TC), A.take(TC)]
        UT = [A.take(TC, BF16), A.take(TC, BF16)]
        HT = [A.take(TC, BF16), A.take(TC, BF16)]
        GT = [A.take(TC, BF16), A.take(TC, BF16)]
        YD = [A.take(TC), A.take(TC)]
        HC = self.SML[:, 0:128]

        dsl = lambda i: DLs[:, i * TC:(i + 1) * TC]
        xsl = lambda i: XCs[:, i * TC:(i + 1) * TC]
        wsl = lambda i: Ws[:, i * TC:(i + 1) * TC]
        ysl = lambda i: Ya[:, i * TC:(i + 1) * TC]

        bar = pr.op("pol", lambda e: e.nop(), tag="postbar")
        t_hc0 = pr.op("vec", lambda e: e.memset(HC, 0.0), tag="hc0")
        prev_store = [None, None]
        prev_scan = [None, None]
        prev_g = [None, None]
        prev_u_par = {0: None, 1: None}
        prev_g_par = {0: None, 1: None}
        yd_tok = [None] * 8
        ld_d_prev = [bar] * 8
        ld_x_prev = [None] * 8
        for tc in range(4):
            hseg = tc // 2
            coff = (tc % 2) * TC
            ld_d, ld_x = [], []
            for i in range(8):
                ld_d.append(pr.dma_hv("pol",
                    lambda h, ii=i: dsl(ii),
                    lambda h, ii=i, hs=hseg, cf=coff: dslice(
                        self.DLb, hs * DI + h * 1024 + ii * P, P, T, cf, TC),
                    waits=[ld_d_prev[i]], tag="dld"))
                ld_x.append(pr.dma_hv("pol",
                    lambda h, ii=i: xsl(ii),
                    lambda h, ii=i, hs=hseg, cf=coff: dslice(
                        self.XCb, hs * DI + h * 1024 + ii * P, P, T, cf, TC),
                    waits=[ld_x_prev[i]], tag="xld"))
            w_toks = []
            for i in range(8):
                w_toks.append(pr.op("vec", lambda e, ii=i: e.tensor_tensor(
                    wsl(ii), dsl(ii), xsl(ii), ALU.mult),
                    waits=[ld_d[i], ld_x[i], prev_store[i % 2]], tag="wmul"))
            y0 = [pr.op("pol", lambda e, ii=i: e.memset(ysl(ii), 0.0),
                        waits=[yd_tok[i]], tag="y0")
                  for i in range(8)]
            j = 0
            for s in range(DS):
                sp = s % 2
                t_br = pr.dma("pol", BR[sp],
                              dslice(self.Bb, hseg * DS + s, 1, T, coff, TC)
                              .to_broadcast((P, TC)),
                              waits=[prev_u_par[sp]], tag="brep")
                t_cr = pr.dma("pol", CR[sp],
                              dslice(self.Cb, hseg * DS + s, 1, T, coff, TC)
                              .to_broadcast((P, TC)),
                              waits=[prev_g_par[sp]], tag="crep")
                for i in range(8):
                    p_ = j % 2
                    col = i * DS + s
                    t_a = pr.op("sca", lambda e, ii=i, cc=col, pp=p_: e.activation(
                        AT[pp], dsl(ii), ACT.Exp, scale=C["A_sc"][:, cc:cc + 1]),
                        waits=[ld_d[i], prev_scan[p_]], tag="aexp")
                    t_u = pr.op("vec", lambda e, ii=i, pp=p_, ss=sp: e.tensor_tensor(
                        UT[pp], wsl(ii), BR[ss], ALU.mult),
                        waits=[w_toks[i], t_br], tag="umul")
                    t_h = pr.op("vec", lambda e, cc=col, pp=p_: e.tensor_tensor_scan(
                        HT[pp], AT[pp], UT[pp], HC[:, cc:cc + 1], ALU.mult, ALU.add),
                        waits=[t_a, t_u, prev_g[p_], t_hc0], tag="scan")
                    prev_scan[p_] = t_h
                    pr.op("vec", lambda e, cc=col, pp=p_: e.tensor_copy(
                        HC[:, cc:cc + 1], HT[pp][:, TC - 1:TC]), tag="carry")
                    t_g = pr.op("pol", lambda e, pp=p_, ss=sp: e.tensor_tensor(
                        GT[pp], HT[pp], CR[ss], ALU.mult),
                        waits=[t_h, t_cr, y0[i]], tag="gmul")
                    t_y = pr.op("pol", lambda e, ii=i, pp=p_: e.tensor_tensor(
                        ysl(ii), ysl(ii), GT[pp], ALU.add), tag="yacc")
                    y0[i] = t_y
                    prev_g[p_] = t_g
                    prev_u_par[sp] = t_u
                    prev_g_par[sp] = t_g
                    j += 1
            for i in range(8):
                t_yd = pr.op("vec", lambda e, ii=i: e.scalar_tensor_tensor(
                    YD[ii % 2], xsl(ii), C["D_sc"][:, ii:ii + 1], ysl(ii),
                    ALU.mult, ALU.add),
                    waits=[y0[i], ld_x[i], prev_store[i % 2]], tag="ytot")
                yd_tok[i] = t_yd
                prev_store[i % 2] = pr.dma_hv("pol",
                    lambda h, ii=i, t_=tc: dslice(
                        self.Yb, h * 1024 + ii * P, P, 2 * T, t_ * TC, TC),
                    lambda h, ii=i: YD[ii % 2], waits=[t_yd], tag=f"yst{tc}_{i}")
                ld_d_prev[i] = prev_store[i % 2]
                ld_x_prev[i] = prev_store[i % 2]
        pr.barrier_all()

    def mamba_out(self, pr):
        C = self.CONST
        A = self.AR
        A.reset()
        half = self.half
        YF = A.take(16 * T, BF16)
        WOB = A.take(16 * D, BF16)

        bar = pr.op("pol", lambda e: e.nop(), tag="postbar")
        ybuf = lambda kc: YF[:, kc * T:(kc + 1) * T]
        ld = []
        for kc in range(16):
            owner = kc // 8
            row = (kc % 8) * P
            ld.append(pr.dma_hv("pol",
                lambda h, k=kc: ybuf(k),
                lambda h, o=owner, r=row: dslice(
                    self.Yb, o * 1024 + r, P, 2 * T, h * T, T),
                waits=[bar], tag="yld"))
        ow = self.inp["out_w_T"].ap().rearrange("(kc p) n -> kc p n", p=P)
        lw = [pr.dma("pol", WOB[:, kc * D:(kc + 1) * D], ow[kc], tag="owld")
              for kc in range(16)]

        bank_tok = {0: None, 1: None}
        for tt in range(NT):
            for dc in range(2):
                mm = None
                for kc in range(16):
                    mm = pr.op("ten", lambda e, k=kc, t_=tt, d_=dc: e.matmul(
                        self.PB[d_][:, :], ybuf(k)[:, t_ * P:(t_ + 1) * P],
                        WOB[:, k * D + d_ * 512:k * D + (d_ + 1) * 512],
                        start=(k == 0), stop=(k == 15)),
                        waits=[ld[kc], lw[kc], bank_tok[dc]], tag="owmm")
                t_add = pr.op("vec", lambda e, t_=tt, d_=dc: e.tensor_tensor(
                    self.RES[:, t_ * D + d_ * 512:t_ * D + (d_ + 1) * 512],
                    self.RES[:, t_ * D + d_ * 512:t_ * D + (d_ + 1) * 512],
                    self.PB[d_][:, :], ALU.add), waits=[mm], tag="radd")
                bank_tok[dc] = t_add
        pr.barrier_all()
        return



    def make_fm(self, pr, zfm, xrf=None, lnt=None, lnsq=None, xrt=None,
                rope_cs=None, rope_toks=(), war=()):
        C = self.CONST
        PB = self.PB
        ident = C["ident"][:, :]
        evs = []
        last_rope = None
        for t in range(NT):
            tk = self.ln_one(pr, self.RES[:, t * D:(t + 1) * D], lnt, lnsq,
                             self.SML[:, 8 * t:8 * t + 8],
                             war=[evs[-1] if evs else None] + list(war))
            srcs = [(zfm, lnt, tk)]
            if xrf is not None:
                cos, sin = rope_cs
                cosb = cos[:, t * 32:(t + 1) * 32].unsqueeze(1).unsqueeze(3)\
                    .broadcast_to((P, 16, 32, 1))
                sinb = sin[:, t * 32:(t + 1) * 32].unsqueeze(1).unsqueeze(3)\
                    .broadcast_to((P, 16, 32, 1))
                zr = lnt.rearrange("p (h i two) -> p h i two", two=2, i=32)
                xr = xrt.rearrange("p (h i two) -> p h i two", two=2, i=32)
                z_e, z_o = zr[:, :, :, 0:1], zr[:, :, :, 1:2]
                x_e, x_o = xr[:, :, :, 0:1], xr[:, :, :, 1:2]
                tmp = self.SML[:, 256:768].rearrange(
                    "p (h i one) -> p h i one", h=16, i=32)
                pr.op("vec", lambda e, a=tmp, b=z_o, c=sinb: e.tensor_tensor(
                      a, b, c, ALU.mult),
                      waits=[tk] + list(rope_toks), tag="rp1")
                pr.op("vec", lambda e, a=x_e, b=z_e, c=cosb: e.tensor_tensor(
                      a, b, c, ALU.mult), waits=[last_rope], tag="rp2")
                pr.op("vec", lambda e, a=x_e, b=tmp: e.tensor_tensor(
                      a, a, b, ALU.subtract), tag="rp3")
                pr.op("vec", lambda e, a=tmp, b=z_o, c=cosb: e.tensor_tensor(
                      a, b, c, ALU.mult), tag="rp4")
                pr.op("vec", lambda e, a=x_o, b=z_e, c=sinb: e.tensor_tensor(
                      a, b, c, ALU.mult), tag="rp5")
                rt = pr.op("vec", lambda e, a=x_o, b=tmp: e.tensor_tensor(
                      a, a, b, ALU.add), tag="rp6")
                srcs.append((xrf, xrt, rt))
            for dst, srcbuf, tok in srcs:
                for c in range(8):
                    pb = PB[5 + (c % 2)]
                    t_tr = pr.op("ten", lambda e, cc=c, b=pb, s=srcbuf: e.transpose(
                        b[:, 0:128], s[:, cc * 128:(cc + 1) * 128], ident),
                        waits=[tok, evs[-2] if len(evs) >= 2 else None], tag="fmt")
                    t_ev = pr.op("sca", lambda e, b=pb, cc=c, tt=t, d=dst: e.copy(
                        d[:, cc * T + tt * P: cc * T + (tt + 1) * P], b[:, 0:128]),
                        waits=[t_tr], tag="fme")
                    evs.append(t_ev)
                if dst is xrf:
                    last_rope = evs[-1]
        return evs[-1]

    def attn_produce(self, pr, l):
        C = self.CONST
        PB = self.PB
        A = self.AR
        A.reset()
        ZFM = A.take(8 * T, BF16)
        XRF = A.take(8 * T, BF16) if l == 0 else None
        LNT = A.take(D)
        LNSQ = A.take(D)
        XRT = A.take(D) if l == 0 else None
        WB = [A.take(8 * 128, BF16), A.take(8 * 128, BF16)]
        EV = A.take(T, BF16)
        bar = pr.op("pol", lambda e: e.nop(), tag="postbar")
        if l == 0:
            RCOS = A.take(NT * 32)
            RSIN = A.take(NT * 32)
            t_rc = pr.dma("sp", RCOS, self.inp["rope_cos"].ap(), waits=[bar],
                          tag="rcld")
            t_rs = pr.dma("sp", RSIN, self.inp["rope_sin"].ap(), tag="rsld")
            war0 = [bar, t_rc, t_rs]
            rope_cs = (RCOS, RSIN)
        else:
            war0 = [bar]
            rope_cs = None
        fm_tok = self.make_fm(pr, ZFM, xrf=XRF, lnt=LNT, lnsq=LNSQ, xrt=XRT,
                              rope_cs=rope_cs, rope_toks=war0, war=war0)
        if l == 0:
            plan = [("wq0_T", self.Qb[0], "bq0_pc", XRF),
                    ("wk0_T", self.Kb[0], "bk0_pc", XRF),
                    ("wv0_T", self.Vb[0], "bv0_pc", ZFM)]
        else:
            plan = [("wq1", self.Qb[1], None, ZFM),
                    ("wk1", self.Kb[1], None, ZFM),
                    ("wv1", self.Vb[1], None, ZFM)]
        wprev = [None, None]
        ev_war = None
        bank_tok = {0: None, 1: None}
        for w_name, stage_buf, bias_col, src_fm in plan:
            w_ap = self.inp[w_name].ap().rearrange("(kc p) n -> p kc n", p=P)
            for co in range(8):
                wb = WB[co % 2]
                t_w = pr.dma("pol", wb.rearrange("p (kc n) -> p kc n", n=128),
                             w_ap[:, :, co * 128:(co + 1) * 128],
                             waits=[wprev[co % 2]], tag="qkv_wld")
                evs = []
                mm = None
                for tt in range(4):
                    bk = tt % 2
                    for kc in range(8):
                        mm = pr.op("ten", lambda e, k=kc, w=wb, t_=tt, b=bk, s=src_fm:
                                   e.matmul(PB[b][:, :],
                                            w[:, k * 128:(k + 1) * 128],
                                            s[:, k * T + t_ * 512:k * T + (t_ + 1) * 512],
                                            start=(k == 0), stop=(k == 7)),
                                   waits=[t_w, fm_tok, bank_tok[bk]], tag="qkv_mm")
                    bias = C[bias_col][:, co:co + 1] if bias_col else 0.0
                    t_ev = pr.op("sca", lambda e, b=bk, t_=tt, bi=bias: e.activation(
                        EV[:, t_ * 512:(t_ + 1) * 512], PB[b][:, :], ACT.Identity,
                        bias=bi), waits=[mm, ev_war], tag="qkv_ev")
                    bank_tok[bk] = t_ev
                    evs.append(t_ev)
                wprev[co % 2] = mm
                t_st = pr.dma_hv("pol",
                    lambda h, c_=co, sb=stage_buf: dslice(sb, h * D + c_ * P, P, T, 0, T),
                    lambda h: EV[:, 0:T], waits=evs, tag="qkv_st")
                ev_war = t_st
        pr.barrier_all()

    def attn_heads(self, pr, l):
        C = self.CONST
        PB, PBF = self.PB, self.PBF
        A = self.AR
        A.reset()
        Q2 = A.take(4096, BF16)
        K2 = A.take(4096, BF16)
        QH = A.take(4096, BF16)
        VF = A.take(4096, BF16)
        VT2 = A.take(32 * 65 + 15, BF16)
        ON = A.take(4096, BF16)
        PT = [A.take(512, BF16), A.take(512, BF16)]
        OE = A.take(512, BF16)
        MSKT = A.take(4 * 512)
        CQ = self.SML[:, 128:132]
        MA = self.SML[:, 132:133]
        MT = self.SML[:, 133:134]
        CT = self.SML[0:4, 136:264]
        RR = self.SML[0:1, 300:812]
        Qb, Kb, Vb, Ob = self.Qb[l], self.Kb[l], self.Vb[l], self.Ob[l]

        bar = pr.op("pol", lambda e: e.nop(), tag="postbar")
        t_msk = pr.dma("sp", MSKT, self.inp["maskT"].ap(), waits=[bar], tag="mskld")
        t_o1 = pr.op("vec", lambda e: e.memset(K2[64:65, :], 1.0), waits=[bar],
                     tag="ones1")
        t_o2 = pr.op("vec", lambda e: e.memset(
            VT2[:, 0:2080].rearrange("p (kt c) -> p kt c", c=65)[:, :, 64:65], 1.0),
            tag="ones2")
        w_k2, w_qh, w_vf = [t_o1], [None], [None]
        w_q2, w_vt, w_on = [None], [t_o2], [None]
        p1bank = {0: None, 1: None}
        p2bank = {2: None, 3: None}
        exp_war = [None, None]
        av_start_war = []
        pb5_war = None
        ct_tok = None
        ctr_tok = None
        for hl in range(8):
            tk2, tqh, tvf = [], [], []
            for seg in range(2):
                tk2.append(pr.dma_hv("pol",
                    lambda h, s=seg: K2[0:64, s * T:(s + 1) * T],
                    lambda h, s=seg, hh=hl: dslice(Kb, s * D + h * 512 + hh * 64, 64, T, 0, T),
                    waits=w_k2, tag="k2ld"))
                tqh.append(pr.dma_hv("pol",
                    lambda h, s=seg: QH[0:64, s * T:(s + 1) * T],
                    lambda h, s=seg, hh=hl: dslice(Qb, s * D + h * 512 + hh * 64, 64, T, 0, T),
                    waits=w_qh, tag="qhld"))
                tvf.append(pr.dma_hv("pol",
                    lambda h, s=seg: VF[0:64, s * T:(s + 1) * T],
                    lambda h, s=seg, hh=hl: dslice(Vb, s * D + h * 512 + hh * 64, 64, T, 0, T),
                    waits=w_vf, tag="vfld"))
            tq2 = pr.op("sca", lambda e: e.activation(
                Q2[0:64, :], QH[0:64, :], ACT.Identity, scale=0.125),
                waits=tqh + w_q2, tag="qscale")
            tvt = None
            for kt in range(32):
                t_tr = pr.op("ten", lambda e, k=kt: e.transpose(
                    PBF[:, 0:64], VF[0:64, k * 128:(k + 1) * 128],
                    self.IDB[0:64, 0:64]),
                    waits=tvf + w_vt + [tvt], tag="vtr")
                tvt = pr.op("vec", lambda e, k=kt: e.tensor_copy(
                    VT2[:, k * 65:k * 65 + 64], PBF[:, 0:64]),
                    waits=[t_tr], tag="vte")

            av_tok = None
            st_last = None
            t_exp = None
            ons = []
            for qc in range(8):
                negm_toks = []
                for qt in range(4):
                    qg = qc * 4 + qt
                    kext = (qg + 1) * 128
                    nch = (kext + 511) // 512
                    red_tok = None
                    for kcH in range(nch):
                        n = min(512, kext - kcH * 512)
                        bk = kcH % 2
                        mm = pr.op("ten", lambda e, k=kcH, q_=qg, b=bk, nn=n:
                                   e.matmul(PB[b][:, 0:nn],
                                            QH[0:64, q_ * 128:(q_ + 1) * 128],
                                            K2[0:64, k * 512:k * 512 + nn],
                                            start=True, stop=True),
                                   waits=tk2 + tqh + [p1bank[bk]], tag="s_mm")
                        if (kcH + 1) * 512 >= kext:
                            dcol = qg * 128 - kcH * 512
                            mm = pr.op("vec", lambda e, b=bk, dc=dcol: e.tensor_tensor(
                                PB[b][:, dc:dc + 128], PB[b][:, dc:dc + 128],
                                C["diag_mask"][:, :], ALU.add),
                                waits=[mm], tag="s_mask")
                        dst = MA if kcH == 0 else MT
                        rt = pr.op("vec", lambda e, b=bk, d=dst, nn=n: e.reduce_max(
                            d, PB[b][:, 0:nn], axis=AX.X),
                            waits=[mm, red_tok], tag="s_max")
                        if kcH > 0:
                            rt = pr.op("vec", lambda e: e.tensor_max(MA, MA, MT),
                                       tag="s_maxacc")
                        p1bank[bk] = rt
                        red_tok = rt
                    negm_toks.append(pr.op("vec", lambda e, q_=qt: e.tensor_scalar(
                        CQ[:, q_:q_ + 1], MA, -0.125, None, ALU.mult),
                        waits=[red_tok, ctr_tok], tag="negm"))
                ctr_tok = pr.op("ten", lambda e: e.transpose(
                    PB[4][0:4, 0:128], CQ[:, 0:4], C["ident"][:, :]),
                    waits=negm_toks + av_start_war, tag="ctr")
                t_cte = pr.op("vec", lambda e: e.tensor_copy(
                    CT, PB[4][0:4, 0:128]), waits=[ctr_tok], tag="cte")
                t_fl = pr.dma("pol",
                              Q2[64:65, qc * 512:(qc + 1) * 512]
                              .rearrange("p (a b) -> p a b", a=4),
                              CT.unsqueeze(1),
                              waits=[t_cte, ct_tok, st_last], tag="cflat")
                ct_tok = t_fl
                nkt = (qc + 1) * 4
                av_prev = list(av_start_war)
                for kt in range(nkt):
                    bk = 2 + kt % 2
                    mm = pr.op("ten", lambda e, k=kt, q_=qc, b=bk: e.matmul(
                        PB[b][:, :], K2[0:65, k * 128:(k + 1) * 128],
                        Q2[0:65, q_ * 512:(q_ + 1) * 512], start=True, stop=True),
                        waits=tk2 + [tq2, t_fl, p2bank[bk]], tag="st_mm")
                    st_last = mm
                    if kt >= qc * 4:
                        j = kt - qc * 4
                        mm = pr.op("vec", lambda e, b=bk, jj=j: e.tensor_tensor(
                            PB[b][:, :], PB[b][:, :],
                            MSKT[:, jj * 512:(jj + 1) * 512], ALU.add),
                            waits=[mm, t_msk], tag="st_mask")
                    pp = kt % 2
                    t_exp = pr.op("sca", lambda e, b=bk, p_=pp: e.activation(
                        PT[p_][:, :], PB[b][:, :], ACT.Exp),
                        waits=[mm, exp_war[pp]], tag="st_exp")
                    p2bank[bk] = t_exp
                    av_tok = pr.op("ten", lambda e, k=kt, p_=pp, la=(kt == nkt - 1):
                                   e.matmul(PB[4][0:65, :],
                                            VT2[:, k * 65:(k + 1) * 65],
                                            PT[p_][:, :], start=(k == 0), stop=la),
                                   waits=[t_exp, tvt] + (av_prev if kt == 0 else []),
                                   tag="av_mm")
                    exp_war[pp] = av_tok
                t_rr = pr.op("vec", lambda e: e.reciprocal(RR, PB[4][64:65, :]),
                             waits=[av_tok], tag="rrow")
                t_rb = pr.op("ten", lambda e: e.matmul(
                    PB[5][0:64, :], C["ones_row"][0:1, 0:64], RR,
                    start=True, stop=True), waits=[t_rr, pb5_war], tag="rbc")
                t_oe = pr.op("sca", lambda e: e.copy(OE[0:64, 0:512],
                             PB[4][0:64, :]), waits=[av_tok, t_rr], tag="o_ev")
                t_on = pr.op("vec", lambda e, q_=qc: e.tensor_tensor(
                    ON[0:64, q_ * 512:(q_ + 1) * 512], OE[0:64, 0:512],
                    PB[5][0:64, :], ALU.mult),
                    waits=[t_oe, t_rb] + w_on, tag="o_norm")
                ons.append(t_on)
                pb5_war = t_on
                av_start_war = [t_oe, t_rr]
            t_ost = pr.dma_hv("pol",
                lambda h, hh=hl: dslice(Ob, h * 512 + hh * 64, 64, 2 * T, 0, 2 * T),
                lambda h: ON[0:64, :], waits=ons, tag="onst")
            w_k2 = [st_last]
            w_qh = [st_last]
            w_q2 = [t_exp]
            w_vf = [av_tok]
            w_vt = [av_tok]
            w_on = [t_ost]
        pr.barrier_all()

    def attn_out(self, pr, l):
        C = self.CONST
        PB = self.PB
        A = self.AR
        A.reset()
        if l == 0:
            self.X1 = A.take(NT * D)
        OALL = A.take(8 * T, BF16)
        WOB = A.take(8 * 1024, BF16)
        Z0 = A.take(D)
        bar = pr.op("pol", lambda e: e.nop(), tag="postbar")
        cl = []
        if l == 0:
            BO = A.take(D)
            LW1 = A.take(D)
            LB1 = A.take(D)
            cl.append(pr.dma("sp", BO, self.inp["bo0_bc"].ap(), waits=[bar], tag="bold"))
            cl.append(pr.dma("sp", LW1, self.inp["ln1_w_bc"].ap(), tag="lw1"))
            cl.append(pr.dma("sp", LB1, self.inp["ln1_b_bc"].ap(), tag="lb1"))
        w_name = "wo0_T" if l == 0 else "wo1"
        Ob = self.Ob[l]
        ld = []
        for r in range(8):
            ld.append(pr.dma_hv("pol",
                lambda h, rr=r: OALL[:, rr * T:(rr + 1) * T],
                lambda h, rr=r: dslice(Ob, rr * P, P, 2 * T, h * T, T),
                waits=[bar], tag="oall"))
        w_ap = self.inp[w_name].ap().rearrange("(kc p) n -> kc p n", p=P)
        lw = [pr.dma("pol", WOB[:, kc * 1024:(kc + 1) * 1024], w_ap[kc],
                     tag="wold") for kc in range(8)]
        bank_tok = {0: None, 1: None}
        ln_war = None
        for tt in range(NT):
            outtoks = []
            for dc in range(2):
                mm = None
                for kc in range(8):
                    mm = pr.op("ten", lambda e, k=kc, t_=tt, d_=dc: e.matmul(
                        PB[d_][:, :], OALL[:, k * T + t_ * P:k * T + (t_ + 1) * P],
                        WOB[:, k * 1024 + d_ * 512:k * 1024 + (d_ + 1) * 512],
                        start=(k == 0), stop=(k == 7)),
                        waits=[ld[kc], lw[kc], bank_tok[dc]], tag="wo_mm")
                outtoks.append(mm)
            if l == 1:
                for dc in range(2):
                    t_add = pr.op("vec", lambda e, t_=tt, d_=dc: e.tensor_tensor(
                        self.RES[:, t_ * D + d_ * 512:t_ * D + (d_ + 1) * 512],
                        self.RES[:, t_ * D + d_ * 512:t_ * D + (d_ + 1) * 512],
                        PB[d_][:, :], ALU.add), waits=[outtoks[dc]], tag="res1")
                    bank_tok[dc] = t_add
            else:
                tz = self.ln_one(pr, self.RES[:, tt * D:(tt + 1) * D], Z0, Z0,
                                 self.SML[:, 8 * tt:8 * tt + 8],
                                 waits=cl, war=[ln_war])
                for dc in range(2):
                    t1 = pr.op("vec", lambda e, d_=dc: e.tensor_tensor(
                        Z0[:, d_ * 512:(d_ + 1) * 512],
                        Z0[:, d_ * 512:(d_ + 1) * 512], PB[d_][:, :], ALU.add),
                        waits=[outtoks[dc], tz], tag="z0a")
                    t2 = pr.op("vec", lambda e, d_=dc: e.tensor_tensor(
                        Z0[:, d_ * 512:(d_ + 1) * 512],
                        Z0[:, d_ * 512:(d_ + 1) * 512],
                        BO[:, d_ * 512:(d_ + 1) * 512], ALU.add), tag="boa")
                    bank_tok[dc] = t2
                ln_war = self.ln_one(
                    pr, Z0, self.X1[:, tt * D:(tt + 1) * D],
                    self.X1[:, tt * D:(tt + 1) * D],
                    self.SML[:, 264:272], wb=(LW1, LB1),
                    waits=[bank_tok[0], bank_tok[1]], war=[])
                bank_tok[0] = ln_war
                bank_tok[1] = ln_war
        pr.barrier_all()

    def moe_l0(self, pr):
        C = self.CONST
        PB = self.PB
        A = self.AR
        A.reset()
        A.off = NT * D
        X1 = self.X1
        X1FM = A.take(8 * 512, BF16)
        MACC = A.take(4 * 1024)
        EW = A.take(8 * 1024, BF16)
        GTW = A.take(64, BF16)
        EB = A.take(1024)
        W8 = A.take(NT * 8)
        W8T = A.take(4 * 128)
        LW2 = A.take(D)
        LB2 = A.take(D)
        TMP = A.take(D)
        G8 = self.SML[:, 0:8]
        T8 = self.SML[:, 8:16]
        M8 = self.SML[:, 16:24]
        SU = self.SML[:, 24:32]
        ident = C["ident"][:, :]

        bar = pr.op("pol", lambda e: e.nop(), tag="postbar")
        g_ap = self.inp["gate_T"].ap().rearrange("(kc p) n -> p kc n", p=P)
        t_gw = pr.dma("pol", GTW.rearrange("p (kc n) -> p kc n", n=8), g_ap,
                      waits=[bar], tag="gwld")
        t_eb = pr.dma("sp", EB[0:8, :], self.inp["eb"].ap(), waits=[bar], tag="ebld")
        t_l2 = [pr.dma("sp", LW2, self.inp["ln2_w_bc"].ap(), tag="lw2"),
                pr.dma("sp", LB2, self.inp["ln2_b_bc"].ap(), tag="lb2")]

        macc_war = [None] * 4
        res_done = []
        gexp_war = None
        ebank = {0: None, 1: None}
        ew_prev = None
        pb6_tok = None
        for half_t in range(4):
            t0 = half_t * 4
            fm_toks = []
            for ti in range(4):
                for c in range(8):
                    pb = PB[5 + (c % 2)]
                    t_tr = pr.op("ten", lambda e, cc=c, b=pb, t_=t0 + ti: e.transpose(
                        b[:, 0:128], X1[:, t_ * D + cc * 128:t_ * D + (cc + 1) * 128],
                        ident),
                        waits=[fm_toks[-2] if len(fm_toks) >= 2 else None,
                               ("vec", self.cnt["vec"]), ("sca", self.cnt["sca"]),
                               bar], tag="x1t")
                    t_ev = pr.op("sca", lambda e, b=pb, cc=c, ti_=ti: e.copy(
                        X1FM[:, cc * 512 + ti_ * P:cc * 512 + (ti_ + 1) * P],
                        b[:, 0:128]), waits=[t_tr], tag="x1e")
                    fm_toks.append(t_ev)
            fm_tok = fm_toks[-1]
            w8_toks = []
            gbank = None
            for ti in range(4):
                tt = t0 + ti
                mm = None
                for kc in range(8):
                    mm = pr.op("ten", lambda e, k=kc, ti_=ti: e.matmul(
                        PB[2][:, 0:8],
                        X1FM[:, k * 512 + ti_ * P:k * 512 + (ti_ + 1) * P],
                        GTW[:, k * 8:(k + 1) * 8], start=(k == 0), stop=(k == 7)),
                        waits=[t_gw, fm_tok, gbank], tag="gmm")
                t_b = pr.op("vec", lambda e: e.tensor_tensor(
                    G8, PB[2][:, 0:8], C["gate_b_bc"][:, 0:8], ALU.add),
                    waits=[mm, gexp_war], tag="gbias")
                gbank = t_b
                t_m = pr.op("vec", lambda e: e.reduce_max(SU[:, 0:1], G8, axis=AX.X),
                            tag="gmax")
                t_nm = pr.op("vec", lambda e: e.tensor_scalar(
                    SU[:, 1:2], SU[:, 0:1], -1.0, None, ALU.mult), tag="gneg")
                t_e = pr.op("sca", lambda e: e.activation(
                    T8, G8, ACT.Exp, bias=SU[:, 1:2], accum_out=SU[:, 2:3]),
                    waits=[t_nm], tag="gexp")
                gexp_war = t_e
                t_r = pr.op("vec", lambda e: e.reciprocal(SU[:, 3:4], SU[:, 2:3]),
                            waits=[t_e], tag="grec")
                t_g = pr.op("vec", lambda e: e.tensor_scalar(
                    G8, T8, SU[:, 3:4], None, ALU.mult), tag="gnorm")
                t_m1 = pr.op("vec", lambda e: e.reduce_max(SU[:, 0:1], G8, axis=AX.X),
                             tag="gm1")
                pr.op("vec", lambda e: e.tensor_scalar(
                    M8, G8, SU[:, 0:1], None, ALU.is_lt), tag="gm2b")
                pr.op("vec", lambda e: e.tensor_tensor(T8, G8, M8, ALU.mult),
                      tag="gm2c")
                t_m2 = pr.op("vec", lambda e: e.reduce_max(SU[:, 1:2], T8,
                             axis=AX.X), tag="gm2")
                t_msk2 = pr.op("vec", lambda e: e.tensor_scalar(
                    M8, G8, SU[:, 1:2], None, ALU.is_ge), tag="gge")
                t_w8 = pr.op("vec", lambda e, tt_=tt: e.tensor_tensor(
                    W8[:, tt_ * 8:(tt_ + 1) * 8], G8, M8, ALU.mult), tag="gw8")
                w8_toks.append(t_w8)
            w8t = []
            for ti in range(4):
                tt = t0 + ti
                t_tr = pr.op("ten", lambda e, tt_=tt: e.transpose(
                    PB[3][0:8, 0:128], W8[:, tt_ * 8:(tt_ + 1) * 8], ident),
                    waits=[w8_toks[ti], w8t[-1] if w8t else None], tag="w8t")
                t_ev = pr.op("vec", lambda e, ti_=ti: e.tensor_copy(
                    W8T[0:8, ti_ * 128:(ti_ + 1) * 128], PB[3][0:8, 0:128]),
                    waits=[t_tr], tag="w8te")
                w8t.append(t_ev)
            for ti in range(4):
                for dc in range(2):
                    t_bm = pr.op("ten", lambda e, ti_=ti, d_=dc: e.matmul(
                        PB[6][:, :], W8T[0:8, ti_ * 128:(ti_ + 1) * 128],
                        EB[0:8, d_ * 512:(d_ + 1) * 512], start=True, stop=True),
                        waits=[w8t[ti], t_eb, fm_tok, macc_war[ti], pb6_tok],
                        tag="biasmm")
                    t_mc = pr.op("vec", lambda e, ti_=ti, d_=dc: e.tensor_copy(
                        MACC[:, ti_ * 1024 + d_ * 512:ti_ * 1024 + (d_ + 1) * 512],
                        PB[6][:, :]), waits=[t_bm, macc_war[ti]], tag="macc0")
                    macc_war[ti] = t_mc
                    pb6_tok = t_mc
            ew_ap = self.inp["ew_T"].ap()
            mm = None
            for e_ in range(8):
                t_ew = pr.dma("pol", EW.rearrange("p (kc n) -> p kc n", n=1024),
                              ew_ap[e_].rearrange("(kc p) n -> p kc n", p=P),
                              waits=[ew_prev], tag="ewld")
                for ti in range(4):
                    tt = t0 + ti
                    for dc in range(2):
                        bk = dc
                        for kc in range(8):
                            mm = pr.op("ten", lambda e, k=kc, ti_=ti, d_=dc, b=bk:
                                       e.matmul(PB[b][:, :],
                                                X1FM[:, k * 512 + ti_ * P:
                                                     k * 512 + (ti_ + 1) * P],
                                                EW[:, k * 1024 + d_ * 512:
                                                   k * 1024 + (d_ + 1) * 512],
                                                start=(k == 0), stop=(k == 7)),
                                       waits=[t_ew, fm_tok, ebank[bk]], tag="emm")
                        t_acc = pr.op("vec", lambda e, ti_=ti, d_=dc, b=bk, ee=e_,
                                      tt_=tt: e.scalar_tensor_tensor(
                            MACC[:, ti_ * 1024 + d_ * 512:ti_ * 1024 + (d_ + 1) * 512],
                            PB[b][:, :], W8[:, tt_ * 8 + ee:tt_ * 8 + ee + 1],
                            MACC[:, ti_ * 1024 + d_ * 512:ti_ * 1024 + (d_ + 1) * 512],
                            ALU.mult, ALU.add),
                            waits=[mm, macc_war[ti]], tag="eacc")
                        ebank[bk] = t_acc
                        macc_war[ti] = t_acc
                ew_prev = mm
            for ti in range(4):
                tt = t0 + ti
                t_s = pr.op("vec", lambda e, ti_=ti, tt_=tt: e.tensor_tensor(
                    MACC[:, ti_ * 1024:(ti_ + 1) * 1024],
                    MACC[:, ti_ * 1024:(ti_ + 1) * 1024],
                    X1[:, tt_ * D:(tt_ + 1) * D], ALU.add),
                    waits=[macc_war[ti]], tag="x2sum")
                t_ln = self.ln_one(pr, MACC[:, ti * 1024:(ti + 1) * 1024], TMP,
                                   TMP, self.SML[:, 272:280], wb=(LW2, LB2),
                                   waits=t_l2 + [t_s],
                                   war=[res_done[-1] if res_done else None])
                t_res = pr.op("vec", lambda e, tt_=tt: e.tensor_tensor(
                    self.RES[:, tt_ * D:(tt_ + 1) * D],
                    self.RES[:, tt_ * D:(tt_ + 1) * D], TMP, ALU.add),
                    waits=[t_ln], tag="res2")
                res_done.append(t_res)
            macc_war = [res_done[-1]] * 4
        pr.barrier_all()

    def ffn_l1(self, pr):
        C = self.CONST
        PB = self.PB
        A = self.AR
        A.reset()
        ZFM = A.take(8 * T, BF16)
        GEF = A.take(8 * 1024, BF16)
        MACC = A.take(8 * 1024)
        W8 = A.take(NT * 8)
        WRT = A.take(64, BF16)
        W12 = [A.take(8 * 128, BF16), A.take(8 * 128, BF16)]
        W2F = A.take(8 * 1024, BF16)
        SG = [A.take(512, BF16), A.take(512, BF16)]
        LNT = A.take(D)
        LNSQ = A.take(D)
        G8 = self.SML[:, 0:8]
        T8 = self.SML[:, 8:16]
        M8 = self.SML[:, 16:24]
        SU = self.SML[:, 24:32]

        bar = pr.op("pol", lambda e: e.nop(), tag="postbar")
        fm_tok = self.make_fm(pr, ZFM, lnt=LNT, lnsq=LNSQ, war=[bar])
        wr_ap = self.inp["w_router"].ap().rearrange("(kc p) n -> p kc n", p=P)
        t_wr = pr.dma("pol", WRT.rearrange("p (kc n) -> p kc n", n=8), wr_ap,
                      waits=[bar], tag="wrld")
        w8_toks = []
        rbank = None
        rexp_war = None
        for tt in range(NT):
            mm = None
            for kc in range(8):
                mm = pr.op("ten", lambda e, k=kc, t_=tt: e.matmul(
                    PB[6][:, 0:8], ZFM[:, k * T + t_ * P:k * T + (t_ + 1) * P],
                    WRT[:, k * 8:(k + 1) * 8], start=(k == 0), stop=(k == 7)),
                    waits=[t_wr, fm_tok, rbank], tag="rmm")
            t_m = pr.op("vec", lambda e: e.reduce_max(SU[:, 0:1], PB[6][:, 0:8],
                        axis=AX.X), waits=[mm], tag="rmax")
            t_nm = pr.op("vec", lambda e: e.tensor_scalar(
                SU[:, 1:2], SU[:, 0:1], -1.0, None, ALU.mult), tag="rneg")
            t_e = pr.op("sca", lambda e: e.activation(
                T8, PB[6][:, 0:8], ACT.Exp, bias=SU[:, 1:2], accum_out=SU[:, 2:3]),
                waits=[t_nm, rexp_war], tag="rexp")
            rexp_war = t_e
            rbank = t_e
            t_r = pr.op("vec", lambda e: e.reciprocal(SU[:, 3:4], SU[:, 2:3]),
                        waits=[t_e], tag="rrec")
            t_g = pr.op("vec", lambda e: e.tensor_scalar(
                G8, T8, SU[:, 3:4], None, ALU.mult), tag="rnorm")
            t_m1 = pr.op("vec", lambda e: e.reduce_max(SU[:, 0:1], G8, axis=AX.X),
                         tag="rm1")
            pr.op("vec", lambda e: e.tensor_scalar(
                M8, G8, SU[:, 0:1], None, ALU.is_lt), tag="rm2b")
            pr.op("vec", lambda e: e.tensor_tensor(T8, G8, M8, ALU.mult),
                  tag="rm2c")
            t_m2 = pr.op("vec", lambda e: e.reduce_max(SU[:, 1:2], T8,
                         axis=AX.X), tag="rm2")
            t_sum = pr.op("vec", lambda e: e.tensor_tensor(
                SU[:, 4:5], SU[:, 0:1], SU[:, 1:2], ALU.add), tag="rsum12")
            t_rc = pr.op("vec", lambda e: e.reciprocal(SU[:, 5:6], SU[:, 4:5]),
                         tag="rrec2")
            t_msk2 = pr.op("vec", lambda e: e.tensor_scalar(
                M8, G8, SU[:, 1:2], None, ALU.is_ge), tag="rge")
            t_w = pr.op("vec", lambda e: e.tensor_tensor(M8, M8, G8, ALU.mult),
                        tag="rw")
            t_w8 = pr.op("vec", lambda e, tt_=tt: e.tensor_scalar(
                W8[:, tt_ * 8:(tt_ + 1) * 8], M8, SU[:, 5:6], None, ALU.mult),
                tag="rw8")
            w8_toks.append(t_w8)
        w8_done = w8_toks[-1]

        macc_war = [None] * 8
        res_last = None
        for tb in range(2):
            tb0 = tb * 8
            first_acc = True
            for e_ in [8] + list(range(8)):
                shared = (e_ == 8)
                if shared:
                    w1ap, w3ap, w2ap = (self.inp["sw1"].ap(), self.inp["sw3"].ap(),
                                        self.inp["sw2"].ap())
                else:
                    w1ap = self.inp["w1"].ap()[e_]
                    w3ap = self.inp["w3"].ap()[e_]
                    w2ap = self.inp["w2"].ap()[e_]
                for hg in range(2):
                    hg0 = hg * 8
                    g_toks = []
                    wld_prev = [None, None]
                    hbank = {0: None, 1: None, 2: None, 3: None}
                    gm_war = [None, None]
                    for hc8 in range(8):
                        hc = hg0 + hc8
                        t_w1 = pr.dma("pol",
                            W12[0].rearrange("p (kc n) -> p kc n", n=128),
                            w1ap.rearrange("(kc p) n -> p kc n", p=P)
                            [:, :, hc * 128:(hc + 1) * 128],
                            waits=[wld_prev[0]], tag="w1ld")
                        t_w3 = pr.dma("pol",
                            W12[1].rearrange("p (kc n) -> p kc n", n=128),
                            w3ap.rearrange("(kc p) n -> p kc n", p=P)
                            [:, :, hc * 128:(hc + 1) * 128],
                            waits=[wld_prev[1]], tag="w3ld")
                        mm1 = mm3 = None
                        for tc2 in range(2):
                            b1, b3 = tc2, 2 + tc2
                            for kc in range(8):
                                col = kc * T + tb0 * P + tc2 * 512
                                mm1 = pr.op("ten", lambda e, k=kc, cl=col, b=b1:
                                            e.matmul(PB[b][:, :],
                                                     W12[0][:, k * 128:(k + 1) * 128],
                                                     ZFM[:, cl:cl + 512],
                                                     start=(k == 0), stop=(k == 7)),
                                            waits=[t_w1, fm_tok, hbank[b1]],
                                            tag="h1mm")
                            for kc in range(8):
                                col = kc * T + tb0 * P + tc2 * 512
                                mm3 = pr.op("ten", lambda e, k=kc, cl=col, b=b3:
                                            e.matmul(PB[b][:, :],
                                                     W12[1][:, k * 128:(k + 1) * 128],
                                                     ZFM[:, cl:cl + 512],
                                                     start=(k == 0), stop=(k == 7)),
                                            waits=[t_w3, fm_tok, hbank[b3]],
                                            tag="h3mm")
                            t_si = pr.op("sca", lambda e, b=b1, t2=tc2: e.activation(
                                SG[t2], PB[b][:, :], ACT.Silu),
                                waits=[mm1, gm_war[tc2]], tag="silu1")
                            t_gm = pr.op("vec", lambda e, b=b3, h_=hc8, t2=tc2:
                                         e.tensor_tensor(
                                GEF[:, h_ * 1024 + t2 * 512:h_ * 1024 + (t2 + 1) * 512],
                                SG[t2], PB[b][:, :], ALU.mult),
                                waits=[t_si, mm3], tag="gmul2")
                            hbank[b1] = t_gm
                            hbank[b3] = t_gm
                            gm_war[tc2] = t_gm
                            g_toks.append(t_gm)
                        wld_prev = [mm1, mm3]
                    g_done = g_toks[-1]
                    t_w2 = pr.dma("pol", W2F.rearrange("p (hc n) -> p hc n", n=1024),
                                  w2ap.rearrange("(hcp p) n -> p hcp n", p=P)
                                  [:, hg0:hg0 + 8, :],
                                  waits=[wld_prev[0], wld_prev[1]], tag="w2ld")
                    ybank = {4: None, 5: None}
                    for ti in range(8):
                        tt = tb0 + ti
                        for dc in range(2):
                            bk = 4 + dc
                            mm = None
                            for hc8 in range(8):
                                mm = pr.op("ten", lambda e, h_=hc8, ti_=ti, d_=dc,
                                           b=bk: e.matmul(
                                    PB[b][:, :],
                                    GEF[:, h_ * 1024 + ti_ * P:h_ * 1024 + (ti_ + 1) * P],
                                    W2F[:, h_ * 1024 + d_ * 512:h_ * 1024 + (d_ + 1) * 512],
                                    start=(h_ == 0), stop=(h_ == 7)),
                                    waits=[t_w2, g_done, ybank[bk]], tag="ymm")
                            if first_acc:
                                t_acc = pr.op("vec", lambda e, ti_=ti, d_=dc, b=bk:
                                              e.tensor_copy(
                                    MACC[:, ti_ * 1024 + d_ * 512:
                                         ti_ * 1024 + (d_ + 1) * 512], PB[b][:, :]),
                                    waits=[mm, macc_war[ti]], tag="yinit")
                            elif shared:
                                t_acc = pr.op("vec", lambda e, ti_=ti, d_=dc, b=bk:
                                              e.tensor_tensor(
                                    MACC[:, ti_ * 1024 + d_ * 512:
                                         ti_ * 1024 + (d_ + 1) * 512],
                                    MACC[:, ti_ * 1024 + d_ * 512:
                                         ti_ * 1024 + (d_ + 1) * 512],
                                    PB[b][:, :], ALU.add),
                                    waits=[mm, macc_war[ti]], tag="yshacc")
                            else:
                                t_acc = pr.op("vec", lambda e, ti_=ti, d_=dc, b=bk,
                                              ee=e_, tt_=tt: e.scalar_tensor_tensor(
                                    MACC[:, ti_ * 1024 + d_ * 512:
                                         ti_ * 1024 + (d_ + 1) * 512],
                                    PB[b][:, :], W8[:, tt_ * 8 + ee:tt_ * 8 + ee + 1],
                                    MACC[:, ti_ * 1024 + d_ * 512:
                                         ti_ * 1024 + (d_ + 1) * 512],
                                    ALU.mult, ALU.add),
                                    waits=[mm, macc_war[ti], w8_done], tag="yacc2")
                            ybank[bk] = t_acc
                            macc_war[ti] = t_acc
                    first_acc = False
            for ti in range(8):
                tt = tb0 + ti
                t_res = pr.op("vec", lambda e, ti_=ti, tt_=tt: e.tensor_tensor(
                    self.RES[:, tt_ * D:(tt_ + 1) * D],
                    self.RES[:, tt_ * D:(tt_ + 1) * D],
                    MACC[:, ti_ * 1024:(ti_ + 1) * 1024], ALU.add),
                    waits=[macc_war[ti], res_last], tag="res3")
                macc_war[ti] = t_res
                res_last = t_res
        pr.barrier_all()


def kernel(taps=(), **inputs):
    in_maps = prep_host(inputs)
    mk = MK(taps=taps)
    nc = mk.build()
    res = run_bass_kernel_spmd(nc, in_maps, core_ids=list(range(8)))
    out = np.empty((4, 4096, 1024), np.float32)
    for c in range(8):
        b, h = c // 2, c % 2
        out[b, h * T:(h + 1) * T] = res.results[c]["y"]
    if taps:
        return out, res.results
    return out


# version 02
# speedup vs baseline: 1.3482x; 1.3482x over previous
import numpy as np
import ml_dtypes

import concourse.bass as bass
import concourse.mybir as mybir
from concourse.bass_utils import run_bass_kernel_spmd

F32 = mybir.dt.float32
BF16 = mybir.dt.bfloat16
ALU = mybir.AluOpType
ACT = mybir.ActivationFunctionType
AX = mybir.AxisListType

P = 128
T = 2048
NT = 16
D = 1024
DI = 2048
DS = 16
LN_EPS = 1e-5
NEG = -1.0e30
BF = ml_dtypes.bfloat16

ENGS = ["ten", "vec", "sca", "pol", "sp"]
DMA_SEM = {"sp": "dsp", "sca": "dact", "pol": "dpol"}
NDMASEM = 6


def f32(x):
    return np.ascontiguousarray(x, dtype=np.float32)


class Prog:

    def __init__(self, nc, sems, cnt, mk=None):
        self.nc = nc
        self.ops = []
        self.sems = sems
        self.cnt = cnt
        self.mk = mk

    CHAIN = {"ten": False, "vec": True, "sca": True, "pol": True, "sp": True}

    def op(self, eng, fn, waits=(), tag=""):
        sem = eng
        prev = self.cnt[sem]
        self.cnt[sem] += 1
        tok = (sem, self.cnt[sem])
        w = self._clean(waits, eng)
        if prev > 0 and self.CHAIN[eng]:
            w.append((sem, prev))
        self.ops.append((eng, fn, w, [(sem, 1)], tag))
        return tok

    def _dma_sem(self, eng):
        q = DMA_SEM[eng]
        idx = self.mk.dma_rr[q]
        self.mk.dma_rr[q] = (idx + 1) % NDMASEM
        return f"{q}{idx}"

    def dma(self, eng, dst, src, waits=(), tag=""):
        sem = self._dma_sem(eng)
        prev = self.cnt[sem]
        self.cnt[sem] += 16
        tok = (sem, self.cnt[sem])
        waits = list(waits) + [(sem, prev)]
        if self.cnt.get(eng, 0):
            waits.append((eng, self.cnt[eng]))

        def _dma(e, d=dst, s=src, t=tag):
            try:
                return e.dma_start(d, s)
            except Exception:
                print(f"DMA FAIL tag={t} dst={d} src={s}")
                raise
        self.ops.append((eng, _dma, self._clean(waits, eng), [(sem, 16)], tag))
        return tok

    def dma_hv(self, eng, dst_fn, src_fn, waits=(), tag=""):
        sem = self._dma_sem(eng)
        prev = self.cnt[sem]
        self.cnt[sem] += 16
        tok = (sem, self.cnt[sem])
        waits = list(waits) + [(sem, prev)]
        if self.cnt.get(eng, 0):
            waits.append((eng, self.cnt[eng]))
        mk = self.mk
        s_h = self.sems[sem]

        def _dma(e, t=tag):
            hreg = mk.half_reg(e, eng)
            try:
                with e.If_eq(hreg, 0):
                    e.dma_start(dst_fn(0), src_fn(0)).then_inc(s_h, 16).annotate(t + "_h0")
                with e.Else():
                    e.dma_start(dst_fn(1), src_fn(1)).then_inc(s_h, 16).annotate(t + "_h1")
            except Exception:
                print(f"DMAHV FAIL tag={t}")
                raise
            return None
        self.ops.append((eng, _dma, self._clean(waits, eng), [], tag))
        return tok

    def _clean(self, waits, eng):
        m = {}
        for w in waits:
            if w is None:
                continue
            s, v = w
            if s == eng:
                continue
            if m.get(s, -1) < v:
                m[s] = v
        return list(m.items())

    def barrier_all(self):
        snap = [(s, v) for s, v in self.cnt.items() if v > 0]
        toks = []
        for e in ENGS:
            toks.append(self.op(e, lambda eng: eng.nop(), waits=snap, tag="pbar"))
        return toks

    def emit(self):
        nc = self.nc
        with nc.Block() as blk:
            def mk_runner(ename):
                def run(e):
                    for eng, fn, waits, incs, tag in self.ops:
                        if eng != ename:
                            continue
                        for s, v in waits:
                            e.wait_ge(self.sems[s], v)
                        try:
                            inst = fn(e)
                        except Exception as ex:
                            import traceback; traceback.print_exc()
                            raise RuntimeError(f"emit fail tag={tag}") from ex
                        if inst is None and incs:
                            inst = e.nop()
                        for s, n in incs:
                            inst.then_inc(self.sems[s], n)
                return run
            blk.tensor(mk_runner("ten"))
            blk.vector(mk_runner("vec"))
            blk.scalar(mk_runner("sca"))
            blk.gpsimd(mk_runner("pol"))
            blk.sync(mk_runner("sp"))


class Arena:
    def __init__(self, ap_f32, ncols_f32):
        self.base = ap_f32
        self.n = ncols_f32
        self.off = 0

    def reset(self):
        self.off = 0

    def take(self, ncols, dtype=F32):
        if dtype == F32:
            a = self.base[:, self.off:self.off + ncols]
            self.off += ncols
        else:
            nf = (ncols + 1) // 2
            a = self.base[:, self.off:self.off + nf].bitcast(BF16)
            self.off += nf
        assert self.off <= self.n, f"arena overflow {self.off} > {self.n}"
        return a


def prep_host(inputs):
    x = np.asarray(inputs["L0_attn_x"], dtype=np.float32)
    c = {}
    c["w_in_T"] = f32(np.asarray(inputs["L0_attn_in_w"]).T)
    c["in_b_pc"] = f32(np.asarray(inputs["L0_attn_in_b"]).reshape(32, P).T)
    cw = np.asarray(inputs["L0_attn_conv_w"])[:, 0, :]
    c["conv_w_pc"] = f32(cw.reshape(16, P, 4).transpose(1, 0, 2).reshape(P, 64))
    c["conv_b_pc"] = f32(np.asarray(inputs["L0_attn_conv_b"]).reshape(16, P).T)
    c["xproj_T"] = f32(np.asarray(inputs["L0_attn_xproj_w"]).T)
    c["dt_w_row"] = f32(np.asarray(inputs["L0_attn_dt_w"])[:, 0].reshape(1, DI))
    c["dt_b_pc"] = f32(np.asarray(inputs["L0_attn_dt_b"]).reshape(16, P).T)
    c["out_w_T"] = f32(np.asarray(inputs["L0_attn_out_w"]).T)

    ipw = np.asarray(inputs["L0_ffn_in_proj_w"])
    ipb = np.asarray(inputs["L0_ffn_in_proj_b"])
    c["wq0_T"] = f32(ipw[0:D].T)
    c["wk0_T"] = f32(ipw[D:2 * D].T)
    c["wv0_T"] = f32(ipw[2 * D:3 * D].T)
    c["bq0_pc"] = f32(ipb[0:D].reshape(8, P).T)
    c["bk0_pc"] = f32(ipb[D:2 * D].reshape(8, P).T)
    c["bv0_pc"] = f32(ipb[2 * D:3 * D].reshape(8, P).T)
    c["wo0_T"] = f32(np.asarray(inputs["L0_ffn_out_proj_w"]).T)
    c["bo0_bc"] = f32(np.broadcast_to(np.asarray(inputs["L0_ffn_out_proj_b"]), (P, D)))
    c["gate_T"] = f32(np.asarray(inputs["L0_ffn_gate_w"]).T)
    c["gate_b_bc"] = f32(np.broadcast_to(np.asarray(inputs["L0_ffn_gate_b"]), (P, 8)))
    c["ew_T"] = f32(np.asarray(inputs["L0_ffn_expert_w"]).transpose(0, 2, 1))
    c["eb"] = f32(np.asarray(inputs["L0_ffn_expert_b"]))
    c["ln1_w_bc"] = f32(np.broadcast_to(np.asarray(inputs["L0_ffn_ln1_w"]), (P, D)))
    c["ln1_b_bc"] = f32(np.broadcast_to(np.asarray(inputs["L0_ffn_ln1_b"]), (P, D)))
    c["ln2_w_bc"] = f32(np.broadcast_to(np.asarray(inputs["L0_ffn_ln2_w"]), (P, D)))
    c["ln2_b_bc"] = f32(np.broadcast_to(np.asarray(inputs["L0_ffn_ln2_b"]), (P, D)))

    c["wq1"] = f32(np.asarray(inputs["L1_attn_Wq"]).reshape(D, D))
    c["wk1"] = f32(np.asarray(inputs["L1_attn_Wk"]).reshape(D, D))
    c["wv1"] = f32(np.asarray(inputs["L1_attn_Wv"]).reshape(D, D))
    c["wo1"] = f32(np.asarray(inputs["L1_attn_Wout"]).reshape(D, D))

    c["w_router"] = f32(np.asarray(inputs["L1_ffn_w_router"]))
    c["w1"] = f32(np.asarray(inputs["L1_ffn_w1"]))
    c["w3"] = f32(np.asarray(inputs["L1_ffn_w3"]))
    c["w2"] = f32(np.asarray(inputs["L1_ffn_w2"]))
    c["sw1"] = f32(np.asarray(inputs["L1_ffn_sw1"]))
    c["sw3"] = f32(np.asarray(inputs["L1_ffn_sw3"]))
    c["sw2"] = f32(np.asarray(inputs["L1_ffn_sw2"]))

    c["ident"] = f32(np.eye(P))
    c["eps_pc"] = np.full((P, 1), LN_EPS, np.float32)
    c["ones_row"] = np.ones((1, P), np.float32)
    mq = np.arange(P)[:, None]
    mk_ = np.arange(P)[None, :]
    c["diag_mask"] = f32(np.where(mk_ <= mq, 0.0, NEG))
    mp = np.arange(P)[:, None]
    mf = np.arange(512)[None, :]
    c["maskT"] = f32(np.concatenate(
        [np.where(mp + j * P > mf, NEG, 0.0) for j in range(4)], axis=1))

    in_maps = []
    for core in range(8):
        b, h = core // 2, core % 2
        d = dict(c)
        d["x_in"] = f32(x[b, h * T:(h + 1) * T])
        if h == 0:
            d["x_halo"] = np.zeros((16, D), np.float32)
            d["halo_b_pc"] = np.zeros((P, 16), np.float32)
        else:
            d["x_halo"] = f32(x[b, h * T - 16:h * T])
            d["halo_b_pc"] = f32(np.asarray(inputs["L0_attn_in_b"])[:DI].reshape(16, P).T)
        A = np.asarray(inputs["L0_attn_A_log"], dtype=np.float32)
        Asl = -np.exp(A[h * 1024:(h + 1) * 1024])
        d["A_sc"] = f32(Asl.reshape(8, P, DS).transpose(1, 0, 2).reshape(P, 8 * DS))
        Dv = np.asarray(inputs["L0_attn_D"], dtype=np.float32)[h * 1024:(h + 1) * 1024]
        d["D_sc"] = f32(Dv.reshape(8, P).T)
        pos = np.arange(h * T, (h + 1) * T, dtype=np.float32)
        inv = 1.0 / (10000.0 ** (np.arange(0, 64, 2, dtype=np.float32) / 64.0))
        fr = pos[:, None] * inv[None, :]
        d["rope_cos"] = f32(np.cos(fr).reshape(NT, P, 32).transpose(1, 0, 2).reshape(P, NT * 32))
        d["rope_sin"] = f32(np.sin(fr).reshape(NT, P, 32).transpose(1, 0, 2).reshape(P, NT * 32))
        in_maps.append(d)
    return in_maps


IN_SHAPES = {
    "x_in": (T, D), "x_halo": (16, D), "halo_b_pc": (P, 16),
    "w_in_T": (D, 2 * DI), "in_b_pc": (P, 32),
    "conv_w_pc": (P, 64), "conv_b_pc": (P, 16),
    "xproj_T": (DI, 33), "dt_w_row": (1, DI), "dt_b_pc": (P, 16),
    "out_w_T": (DI, D),
    "A_sc": (P, 128), "D_sc": (P, 8),
    "wq0_T": (D, D), "wk0_T": (D, D), "wv0_T": (D, D),
    "bq0_pc": (P, 8), "bk0_pc": (P, 8), "bv0_pc": (P, 8),
    "wo0_T": (D, D), "bo0_bc": (P, D),
    "gate_T": (D, 8), "gate_b_bc": (P, 8),
    "ew_T": (8, D, D), "eb": (8, D),
    "ln1_w_bc": (P, D), "ln1_b_bc": (P, D),
    "ln2_w_bc": (P, D), "ln2_b_bc": (P, D),
    "wq1": (D, D), "wk1": (D, D), "wv1": (D, D), "wo1": (D, D),
    "w_router": (D, 8), "w1": (8, D, DI), "w3": (8, D, DI),
    "w2": (8, DI, D), "sw1": (D, DI), "sw3": (D, DI), "sw2": (DI, D),
    "ident": (P, P), "diag_mask": (P, P), "maskT": (P, 4 * 512), "eps_pc": (P, 1),
    "ones_row": (1, P),
    "rope_cos": (P, NT * 32), "rope_sin": (P, NT * 32),
}
CONST_NAMES = ["eps_pc", "in_b_pc", "conv_w_pc", "conv_b_pc", "dt_b_pc", "halo_b_pc",
               "A_sc", "D_sc", "ident", "diag_mask", "bq0_pc", "bk0_pc",
               "bv0_pc", "gate_b_bc", "dt_w_row", "ones_row"]


def dslice(th, row0, nrows, ncols_total, col0, ncols):
    return bass.AP(th, row0 * ncols_total + col0,
                   [[ncols_total, nrows], [1, ncols]])


class MK:
    def __init__(self, taps=(), stages=("produce", "scan", "out", "l0ffn", "l1")):
        self.stages = set(stages)
        self.taps = set(taps)
        self.nc = bass.Bass("TRN2", target_bir_lowering=False, num_devices=8)
        self.inp = {}
        self.out = {}
        self.sems = {}
        self.cnt = {}
        names = ["ten", "vec", "sca", "pol", "sp"]
        for q in ["dsp", "dact", "dpol"]:
            names += [f"{q}{i}" for i in range(NDMASEM)]
        for n in names:
            self.sems[n] = self.nc.alloc_semaphore("s_" + n)
            self.cnt[n] = 0
        self.dma_rr = {"dsp": 0, "dact": 0, "dpol": 0}

    def newprog(self):
        return Prog(self.nc, self.sems, self.cnt, mk=self)

    def half_reg(self, e, eng):
        if not hasattr(self, "_eregs"):
            self._eregs = {}
        r = self._eregs.get(eng)
        if r is None:
            r = e.to_reg(e.compute_val(self.half))
            self._eregs[eng] = r
        return r

    def din(self, name, shape, dtype=F32):
        h = self.nc.dram_tensor(name, list(shape), dtype, kind="ExternalInput")
        self.inp[name] = h
        return h

    def dout(self, name, shape, dtype=F32):
        h = self.nc.dram_tensor(name, list(shape), dtype, kind="ExternalOutput")
        self.out[name] = h
        return h

    def tap(self, pr, name, sb_ap, waits):
        if name not in self.taps:
            return None
        shape = list(sb_ap.shape)
        h = self.dout("tap_" + name, shape, sb_ap.dtype)
        return pr.dma("sp", h.ap(), sb_ap, waits=waits, tag="tap_" + name)

    def build(self):
        nc = self.nc
        for n, s in IN_SHAPES.items():
            self.din(n, s, F32)
        self.dout("y", (T, D), F32)

        sh = lambda name, shape, dt: nc.dram_tensor(name, list(shape), dt,
                                                    kind="Internal", addr_space="Shared")
        self.XCb = sh("XCb", (2, DI, T), BF16)
        self.DLb = sh("DLb", (2, DI, T), F32)
        self.Bb = sh("Bbuf", (2, DS, T), F32)
        self.Cb = sh("Cbuf", (2, DS, T), F32)
        self.Yb = sh("Ybuf", (2, 1024, 2 * T), F32)
        self.Qb = [sh(f"Qbuf{l}", (2, D, T), BF16) for l in range(2)]
        self.Kb = [sh(f"Kbuf{l}", (2, D, T), BF16) for l in range(2)]
        self.Vb = [sh(f"Vbuf{l}", (2, D, T), BF16) for l in range(2)]
        self.Ob = [sh(f"Obuf{l}", (D, 2 * T), BF16) for l in range(2)]

        self.RES = nc.alloc_sbuf_tensor("RES", [P, NT * D], F32)
        self.CONST = {}
        for n in CONST_NAMES:
            self.CONST[n] = nc.alloc_sbuf_tensor("c_" + n, list(IN_SHAPES[n]), F32)
        self.IDB = nc.alloc_sbuf_tensor("IDB", [P, P], BF16)
        self.SML = nc.alloc_sbuf_tensor("SML", [P, 1024], F32)
        AR = nc.alloc_sbuf_tensor("AR", [P, 32 * 1024], F32)
        self.AR = Arena(AR[:, :], 32 * 1024)
        self.PB = [nc.alloc_psum_tensor(f"pb{i}", [P, 512], F32) for i in range(7)]
        self.PBF = nc.alloc_psum_tensor("pbf", [P, 1024], BF16)

        pid = nc.partition_id()
        self.half = pid % 2
        self.conds = [1 - self.half, self.half]

        pr = self.newprog()
        ld = []
        for n in CONST_NAMES:
            ld.append(pr.dma("sp", self.CONST[n][:, :], self.inp[n].ap()))
        ld.append(pr.dma("pol", self.IDB[:, :], self.inp["ident"].ap()))
        xin = self.inp["x_in"].ap().rearrange("(nt p) d -> nt p d", p=P)
        res_ld = [pr.dma("sp", self.RES[:, t * D:(t + 1) * D], xin[t])
                  for t in range(NT)]
        self.ld_const = ld
        self.mamba_produce(pr, res_ld)
        pr.emit()
        nc.all_core_barrier()

        if "scan" in self.stages:
            pr = self.newprog()
            self.mamba_scan(pr)
            pr.emit()
            nc.all_core_barrier()

        pr = self.newprog()
        if "out" in self.stages:
            self.mamba_out(pr)
        if "l0ffn" in self.stages or "l0heads" in self.stages or "l0prod" in self.stages:
            self.attn_produce(pr, 0)
            pr.emit()
            nc.all_core_barrier()
            pr = self.newprog()
            if "l0ffn" in self.stages or "l0heads" in self.stages:
                self.attn_heads(pr, 0)
                pr.emit()
                nc.all_core_barrier()
                pr = self.newprog()
            if "l0ffn" in self.stages:
                self.attn_out(pr, 0)
                self.moe_l0(pr)
        if "l1" in self.stages:
            self.attn_produce(pr, 1)
            pr.emit()
            nc.all_core_barrier()
            pr = self.newprog()
            self.attn_heads(pr, 1)
            pr.emit()
            nc.all_core_barrier()
            pr = self.newprog()
            self.attn_out(pr, 1)
            self.ffn_l1(pr)

        if "stage" in self.taps:
            d_xc = self.dout("tap_xc", (DI, T), BF16)
            d_dl = self.dout("tap_dl", (DI, T), F32)
            d_b = self.dout("tap_b", (DS, T), F32)
            d_c = self.dout("tap_c", (DS, T), F32)
            d_y = (self.dout("tap_y", (1024, 2 * T), F32)
                   if "scan" in self.stages else None)
            bar0 = pr.barrier_all()
            pr.dma_hv("pol", lambda h: d_xc.ap(),
                      lambda h: dslice(self.XCb, h * DI, DI, T, 0, T),
                      waits=bar0, tag="txc")
            pr.dma_hv("pol", lambda h: d_dl.ap(),
                      lambda h: dslice(self.DLb, h * DI, DI, T, 0, T), tag="tdl")
            pr.dma_hv("pol", lambda h: d_b.ap(),
                      lambda h: dslice(self.Bb, h * DS, DS, T, 0, T), tag="tb")
            pr.dma_hv("pol", lambda h: d_c.ap(),
                      lambda h: dslice(self.Cb, h * DS, DS, T, 0, T), tag="tc")
            if "scan" in self.stages:
                pr.dma_hv("pol", lambda h: d_y.ap(),
                          lambda h: dslice(self.Yb, h * 1024, 1024, 2 * T, 0, 2 * T),
                          tag="ty")

        fin = pr.barrier_all()
        yo = self.out["y"].ap().rearrange("(nt p) d -> nt p d", p=P)
        st = [pr.dma("sp", yo[t], self.RES[:, t * D:(t + 1) * D], waits=fin)
              for t in range(NT)]
        pr.op("sp", lambda e: None, waits=st, tag="final")
        pr.emit()
        return nc

    def ln_one(self, pr, x_ap, out_ap, tmp, st, wb=None, waits=(), rows=P,
               width=D, war=()):
        mean, var, inv, sumsq, sd = (st[:, 0:1], st[:, 1:2], st[:, 2:3],
                                     st[:, 3:4], st[:, 4:5])
        w_in = list(waits)
        t_sq = pr.op("sca", lambda e: e.activation(tmp, x_ap, ACT.Square,
                     accum_out=sumsq), waits=w_in + list(war), tag="ln_sq")
        pr.op("vec", lambda e: e.reduce_sum(mean, x_ap, axis=AX.X),
              waits=w_in, tag="ln_sum")
        pr.op("vec", lambda e: e.tensor_scalar(mean, mean, 1.0 / width, None,
              ALU.mult), tag="ln_mean")
        pr.op("vec", lambda e: e.tensor_tensor(sd, mean, mean, ALU.mult), tag="ln_m2")
        pr.op("vec", lambda e: e.tensor_scalar(var, sumsq, 1.0 / width, None,
              ALU.mult), waits=[t_sq], tag="ln_v0")
        t_var = pr.op("vec", lambda e: e.tensor_tensor(var, var, sd, ALU.subtract),
                      tag="ln_var")
        t_sd = pr.op("sca", lambda e: e.activation(sd, var, ACT.Sqrt,
                     bias=self.CONST["eps_pc"][0:var.shape[0], 0:1]),
                     waits=[t_var], tag="ln_sd")
        t_inv = pr.op("vec", lambda e: e.reciprocal(inv, sd), waits=[t_sd], tag="ln_inv")
        if wb is None:
            tk = pr.op("vec", lambda e: e.tensor_scalar(out_ap, x_ap, mean, inv,
                       ALU.subtract, ALU.mult), waits=list(war), tag="ln_app")
        else:
            w_bc, b_bc = wb
            pr.op("vec", lambda e: e.tensor_scalar(tmp, x_ap, mean, inv,
                  ALU.subtract, ALU.mult), waits=[t_sq] + list(war), tag="ln_a1")
            pr.op("vec", lambda e: e.tensor_tensor(tmp, tmp, w_bc, ALU.mult),
                  tag="ln_a2")
            tk = pr.op("vec", lambda e: e.tensor_tensor(out_ap, tmp, b_bc, ALU.add),
                       waits=list(war), tag="ln_a3")
        return tk

    def mamba_produce(self, pr, res_ld):
        nc = self.nc
        C = self.CONST
        A = self.AR
        A.reset()
        PB = self.PB

        ZFM = A.take(8 * T, BF16)
        LNT = A.take(D)
        LNSQ = A.take(D)
        XH = A.take(16 + T)
        REG1 = A.take(3 * T)
        ZCH = REG1[:, 0:T]
        SZ = REG1[:, T:2 * T]
        CA = REG1[:, 2 * T:3 * T]
        DLT = ZCH
        PRJ = SZ
        XCC = [A.take(T, BF16), A.take(T, BF16)]
        WIN = [A.take(2 * D, BF16), A.take(2 * D, BF16)]
        XPJ = A.take(16 * 33 + 16, BF16)
        HFM = A.take(8 * 16, BF16)
        HX = LNT

        half = self.half
        ident = C["ident"][:, :]

        zfm_ev = []
        for t in range(NT):
            tk = self.ln_one(pr, self.RES[:, t * D:(t + 1) * D], LNT, LNSQ,
                             self.SML[:, 8 * t:8 * t + 8],
                             waits=[res_ld[t]] + self.ld_const[:2],
                             war=[zfm_ev[-1]] if zfm_ev else [])
            for c in range(8):
                pb = PB[5 + (c % 2)]
                t_tr = pr.op("ten", lambda e, cc=c, b=pb: e.transpose(
                    b[:, 0:128], LNT[:, cc * 128:(cc + 1) * 128], ident),
                    waits=[tk, zfm_ev[-2] if len(zfm_ev) >= 2 else None], tag="zt")
                t_ev = pr.op("sca", lambda e, b=pb, cc=c, tt=t: e.copy(
                    ZFM[:, cc * T + tt * P: cc * T + (tt + 1) * P], b[:, 0:128]),
                    waits=[t_tr], tag="zte")
                zfm_ev.append(t_ev)
        zfm_done = zfm_ev[-1]

        t_hx = pr.dma("sp", HX[0:16, :], self.inp["x_halo"].ap(),
                      waits=[zfm_ev[-1]], tag="halo_ld")
        st = self.SML[0:16, 8 * NT:8 * NT + 8]
        t_hln = self.ln_one(pr, HX[0:16, :], HX[0:16, :], LNSQ[0:16, :], st,
                            waits=[t_hx], rows=16, war=[zfm_done])
        halo_fm = []
        for c in range(8):
            pb = PB[5 + c % 2]
            t_tr = pr.op("ten", lambda e, cc=c, b=pb: e.transpose(
                b[:, 0:16], HX[0:16, cc * 128:(cc + 1) * 128], ident[0:16, 0:16]),
                waits=[t_hln, zfm_ev[-1], halo_fm[-1] if halo_fm else None], tag="htr")
            t_ev = pr.op("sca", lambda e, b=pb, cc=c: e.copy(
                HFM[:, cc * 16:(cc + 1) * 16], b[:, 0:16]), waits=[t_tr], tag="htr_ev")
            halo_fm.append(t_ev)

        xp = self.inp["xproj_T"].ap().rearrange("(c p) n -> c p n", p=P)
        xpj_toks = [pr.dma("pol", XPJ[:, cc * 33:(cc + 1) * 33], xp[cc], tag="xpj")
                    for cc in range(16)]

        win_ap = self.inp["w_in_T"].ap().rearrange("(kc p) n -> p kc n", p=P)

        bank_tok = {4: zfm_ev[-1], 5: zfm_ev[-1], 6: halo_fm[-1], 7: None}
        prev_use = [None, None]
        xc_war = [None, None]
        xh_war = None
        sz_war = None
        xc_st = []
        for co in range(16):
            wbuf = WIN[co % 2]
            t_w1 = pr.dma("pol", wbuf[:, 0:1024].rearrange("p (kc n) -> p kc n", n=128),
                          win_ap[:, :, co * 128:(co + 1) * 128],
                          waits=[prev_use[co % 2]], tag="w_ld_x")
            t_w2 = pr.dma("pol", wbuf[:, 1024:2048].rearrange("p (kc n) -> p kc n", n=128),
                          win_ap[:, :, (16 + co) * 128:(17 + co) * 128], tag="w_ld_z")

            mm = None
            for kc in range(8):
                mm = pr.op("ten", lambda e, k=kc, wb=wbuf: e.matmul(
                    PB[6][:, 0:16], wb[:, k * 128:(k + 1) * 128],
                    HFM[:, k * 16:(k + 1) * 16], start=(k == 0), stop=(k == 7)),
                    waits=([t_w1, halo_fm[-1], bank_tok[6], xh_war] if kc == 0
                           else []), tag="hmm")
            t_he = pr.op("sca", lambda e, c_=co: e.activation(
                XH[:, 0:16], PB[6][:, 0:16], ACT.Identity,
                bias=C["halo_b_pc"][:, c_:c_ + 1]), waits=[mm], tag="hev")
            bank_tok[6] = t_he

            last_ev = {0: None, 1: None}
            for which, woff in ((0, 0), (1, 1024)):
                dst = XH if which == 0 else ZCH
                for tt in range(4):
                    bk = 4 + tt % 2
                    mm = None
                    for kc in range(8):
                        mm = pr.op("ten", lambda e, k=kc, wb=wbuf, o=woff, t_=tt, b=bk:
                                   e.matmul(PB[b][:, :],
                                            wb[:, o + k * 128:o + (k + 1) * 128],
                                            ZFM[:, k * T + t_ * 512:k * T + (t_ + 1) * 512],
                                            start=(k == 0), stop=(k == 7)),
                                   waits=([t_w1 if which == 0 else t_w2, zfm_done,
                                          bank_tok[bk], xh_war, sz_war] if kc == 0
                                          else []), tag="xzmm")
                    off = (16 + tt * 512) if which == 0 else tt * 512
                    t_ev = pr.op("sca", lambda e, d=dst, o=off, c_=co, wh=which, b=bk:
                                 e.activation(d[:, o:o + 512], PB[b][:, :], ACT.Identity,
                                 bias=C["in_b_pc"][:, c_ + 16 * wh:c_ + 16 * wh + 1]),
                                 waits=[mm], tag="xzev")
                    bank_tok[bk] = t_ev
                    last_ev[which] = t_ev
            t_xh, t_z = last_ev[0], last_ev[1]
            prev_use[co % 2] = t_z

            t_c = pr.op("vec", lambda e, c_=co: e.tensor_scalar(
                CA, XH[:, 13:13 + T], C["conv_w_pc"][:, 4 * c_:4 * c_ + 1], None,
                ALU.mult), waits=[t_xh, t_he], tag="cv0")
            for k in range(1, 4):
                t_c = pr.op("vec", lambda e, kk=k, c_=co:
                            e.scalar_tensor_tensor(CA, XH[:, 13 + kk:13 + kk + T],
                            C["conv_w_pc"][:, 4 * c_ + kk:4 * c_ + kk + 1], CA,
                            ALU.mult, ALU.add), tag=f"cv{k}")
            t_c = pr.op("vec", lambda e, c_=co: e.tensor_scalar(
                CA, CA, C["conv_b_pc"][:, c_:c_ + 1], None, ALU.add), tag="cvb")
            t_sz = pr.op("sca", lambda e: e.activation(
                SZ, ZCH, ACT.Silu), waits=[t_z, sz_war], tag="silu")
            t_xc = pr.op("vec", lambda e, o_=XCC[co % 2]:
                         e.tensor_tensor(o_, CA, SZ, ALU.mult),
                         waits=[t_c, t_sz] + list(xc_war[co % 2] or ()), tag="xcm")
            xh_war = t_xc
            sz_war = t_xc

            t_st = pr.dma_hv("pol",
                lambda h, c_=co: dslice(self.XCb, h * DI + c_ * P, P, T, 0, T),
                lambda h, c_=co: XCC[c_ % 2],
                waits=[t_xc], tag="xcst")
            xc_st.append(t_st)

            pj = None
            for tt in range(4):
                pj = pr.op("ten", lambda e, c_=co, t_=tt: e.matmul(
                    PB[t_][0:33, :], XPJ[:, c_ * 33:(c_ + 1) * 33],
                    XCC[c_ % 2][:, t_ * 512:(t_ + 1) * 512],
                    start=(c_ == 0), stop=(c_ == 15)),
                    waits=[t_xc, xpj_toks[co]], tag="pjmm")
            xc_war[co % 2] = (pj, t_st)

        prj_toks = []
        for tt in range(4):
            tk = pr.op("vec", lambda e, t_=tt: e.tensor_copy(
                PRJ[0:33, t_ * 512:(t_ + 1) * 512], PB[t_][0:33, :]),
                waits=[("ten", self.cnt["ten"])], tag="pjev")
            prj_toks.append(tk)
        tB = pr.dma_hv("pol", lambda h: dslice(self.Bb, h * DS, DS, T, 0, T),
                       lambda h: PRJ[1:17, :], waits=prj_toks, tag="Bst")
        tC = pr.dma_hv("pol", lambda h: dslice(self.Cb, h * DS, DS, T, 0, T),
                       lambda h: PRJ[17:33, :], waits=prj_toks, tag="Cst")

        bank_tok2 = {4: bank_tok[4], 5: bank_tok[5]}
        dst_tok = None
        for co in range(16):
            dts = []
            for tt in range(4):
                bk = 4 + tt % 2
                t_mm = pr.op("ten", lambda e, c_=co, t_=tt, b=bk: e.matmul(
                    PB[b][:, :], C["dt_w_row"][0:1, c_ * 128:(c_ + 1) * 128],
                    PRJ[0:1, t_ * 512:(t_ + 1) * 512], start=True, stop=True),
                    waits=prj_toks + [bank_tok2[bk]], tag="dtmm")
                t_e = pr.op("sca", lambda e, c_=co, t_=tt, b=bk:
                            e.activation(DLT[:, t_ * 512:(t_ + 1) * 512], PB[b][:, :],
                            ACT.Exp, bias=C["dt_b_pc"][:, c_:c_ + 1]),
                            waits=[t_mm, dst_tok], tag="dtex")
                bank_tok2[bk] = t_e
                dts.append(t_e)
            t_p1 = pr.op("vec", lambda e: e.tensor_scalar(
                DLT, DLT, 1.0, None, ALU.add), waits=dts, tag="dtp1")
            dst_tok = pr.dma_hv("pol",
                lambda h, c_=co: dslice(self.DLb, h * DI + c_ * P, P, T, 0, T),
                lambda h: DLT, waits=[t_p1], tag="dlst")

        self.tap(pr, "zfm", ZFM, [zfm_done])
        pr.barrier_all()

    def mamba_scan(self, pr):
        C = self.CONST
        A = self.AR
        A.reset()
        half = self.half
        TC = 1024

        DLs = A.take(8 * TC)
        XCs = A.take(8 * TC, BF16)
        Ws = A.take(8 * TC, BF16)
        Ya = A.take(8 * TC, BF16)
        BR = [A.take(TC, BF16), A.take(TC, BF16)]
        CR = [A.take(TC, BF16), A.take(TC, BF16)]
        AT = [A.take(TC), A.take(TC)]
        UT = [A.take(TC, BF16), A.take(TC, BF16)]
        HT = [A.take(TC, BF16), A.take(TC, BF16)]
        GT = [A.take(TC, BF16), A.take(TC, BF16)]
        YD = [A.take(TC), A.take(TC)]
        HC = self.SML[:, 0:128]

        dsl = lambda i: DLs[:, i * TC:(i + 1) * TC]
        xsl = lambda i: XCs[:, i * TC:(i + 1) * TC]
        wsl = lambda i: Ws[:, i * TC:(i + 1) * TC]
        ysl = lambda i: Ya[:, i * TC:(i + 1) * TC]

        bar = pr.op("pol", lambda e: e.nop(), tag="postbar")
        t_hc0 = pr.op("vec", lambda e: e.memset(HC, 0.0), tag="hc0")
        prev_store = [None, None]
        prev_scan = [None, None]
        prev_g = [None, None]
        prev_u_par = {0: None, 1: None}
        prev_g_par = {0: None, 1: None}
        yd_tok = [None] * 8
        ld_d_prev = [bar] * 8
        ld_x_prev = [None] * 8
        for tc in range(4):
            hseg = tc // 2
            coff = (tc % 2) * TC
            ld_d, ld_x = [], []
            for i in range(8):
                ld_d.append(pr.dma_hv("pol",
                    lambda h, ii=i: dsl(ii),
                    lambda h, ii=i, hs=hseg, cf=coff: dslice(
                        self.DLb, hs * DI + h * 1024 + ii * P, P, T, cf, TC),
                    waits=[ld_d_prev[i]], tag="dld"))
                ld_x.append(pr.dma_hv("pol",
                    lambda h, ii=i: xsl(ii),
                    lambda h, ii=i, hs=hseg, cf=coff: dslice(
                        self.XCb, hs * DI + h * 1024 + ii * P, P, T, cf, TC),
                    waits=[ld_x_prev[i]], tag="xld"))
            w_toks = []
            for i in range(8):
                w_toks.append(pr.op("vec", lambda e, ii=i: e.tensor_tensor(
                    wsl(ii), dsl(ii), xsl(ii), ALU.mult),
                    waits=[ld_d[i], ld_x[i], prev_store[i % 2]], tag="wmul"))
            y0 = [pr.op("pol", lambda e, ii=i: e.memset(ysl(ii), 0.0),
                        waits=[yd_tok[i]], tag="y0")
                  for i in range(8)]
            j = 0
            for s in range(DS):
                sp = s % 2
                t_br = pr.dma("pol", BR[sp],
                              dslice(self.Bb, hseg * DS + s, 1, T, coff, TC)
                              .to_broadcast((P, TC)),
                              waits=[prev_u_par[sp]], tag="brep")
                t_cr = pr.dma("pol", CR[sp],
                              dslice(self.Cb, hseg * DS + s, 1, T, coff, TC)
                              .to_broadcast((P, TC)),
                              waits=[prev_g_par[sp]], tag="crep")
                for i in range(8):
                    p_ = j % 2
                    col = i * DS + s
                    t_a = pr.op("sca", lambda e, ii=i, cc=col, pp=p_: e.activation(
                        AT[pp], dsl(ii), ACT.Exp, scale=C["A_sc"][:, cc:cc + 1]),
                        waits=[ld_d[i], prev_scan[p_]], tag="aexp")
                    t_u = pr.op("vec", lambda e, ii=i, pp=p_, ss=sp: e.tensor_tensor(
                        UT[pp], wsl(ii), BR[ss], ALU.mult),
                        waits=[w_toks[i], t_br], tag="umul")
                    t_h = pr.op("vec", lambda e, cc=col, pp=p_: e.tensor_tensor_scan(
                        HT[pp], AT[pp], UT[pp], HC[:, cc:cc + 1], ALU.mult, ALU.add),
                        waits=[t_a, t_u, prev_g[p_], t_hc0], tag="scan")
                    prev_scan[p_] = t_h
                    pr.op("vec", lambda e, cc=col, pp=p_: e.tensor_copy(
                        HC[:, cc:cc + 1], HT[pp][:, TC - 1:TC]), tag="carry")
                    t_g = pr.op("pol", lambda e, pp=p_, ss=sp: e.tensor_tensor(
                        GT[pp], HT[pp], CR[ss], ALU.mult),
                        waits=[t_h, t_cr, y0[i]], tag="gmul")
                    t_y = pr.op("pol", lambda e, ii=i, pp=p_: e.tensor_tensor(
                        ysl(ii), ysl(ii), GT[pp], ALU.add), tag="yacc")
                    y0[i] = t_y
                    prev_g[p_] = t_g
                    prev_u_par[sp] = t_u
                    prev_g_par[sp] = t_g
                    j += 1
            for i in range(8):
                t_yd = pr.op("vec", lambda e, ii=i: e.scalar_tensor_tensor(
                    YD[ii % 2], xsl(ii), C["D_sc"][:, ii:ii + 1], ysl(ii),
                    ALU.mult, ALU.add),
                    waits=[y0[i], ld_x[i], prev_store[i % 2]], tag="ytot")
                yd_tok[i] = t_yd
                prev_store[i % 2] = pr.dma_hv("pol",
                    lambda h, ii=i, t_=tc: dslice(
                        self.Yb, h * 1024 + ii * P, P, 2 * T, t_ * TC, TC),
                    lambda h, ii=i: YD[ii % 2], waits=[t_yd], tag=f"yst{tc}_{i}")
                ld_d_prev[i] = prev_store[i % 2]
                ld_x_prev[i] = prev_store[i % 2]
        pr.barrier_all()

    def mamba_out(self, pr):
        C = self.CONST
        A = self.AR
        A.reset()
        half = self.half
        YF = A.take(16 * T, BF16)
        WOB = A.take(16 * D, BF16)

        bar = pr.op("pol", lambda e: e.nop(), tag="postbar")
        ybuf = lambda kc: YF[:, kc * T:(kc + 1) * T]
        ld = []
        for kc in range(16):
            owner = kc // 8
            row = (kc % 8) * P
            ld.append(pr.dma_hv("pol",
                lambda h, k=kc: ybuf(k),
                lambda h, o=owner, r=row: dslice(
                    self.Yb, o * 1024 + r, P, 2 * T, h * T, T),
                waits=[bar], tag="yld"))
        ow = self.inp["out_w_T"].ap().rearrange("(kc p) n -> kc p n", p=P)
        lw = [pr.dma("pol", WOB[:, kc * D:(kc + 1) * D], ow[kc], tag="owld")
              for kc in range(16)]

        bank_tok = {0: None, 1: None}
        for tt in range(NT):
            for dc in range(2):
                mm = None
                for kc in range(16):
                    mm = pr.op("ten", lambda e, k=kc, t_=tt, d_=dc: e.matmul(
                        self.PB[d_][:, :], ybuf(k)[:, t_ * P:(t_ + 1) * P],
                        WOB[:, k * D + d_ * 512:k * D + (d_ + 1) * 512],
                        start=(k == 0), stop=(k == 15)),
                        waits=(ld + lw + [bank_tok[dc]] if kc == 0 else []),
                        tag="owmm")
                t_add = pr.op("vec", lambda e, t_=tt, d_=dc: e.tensor_tensor(
                    self.RES[:, t_ * D + d_ * 512:t_ * D + (d_ + 1) * 512],
                    self.RES[:, t_ * D + d_ * 512:t_ * D + (d_ + 1) * 512],
                    self.PB[d_][:, :], ALU.add), waits=[mm], tag="radd")
                bank_tok[dc] = t_add
        pr.barrier_all()
        return



    def make_fm(self, pr, zfm, xrf=None, lnt=None, lnsq=None, xrt=None,
                rope_cs=None, rope_toks=(), war=()):
        C = self.CONST
        PB = self.PB
        ident = C["ident"][:, :]
        evs = []
        last_rope = None
        for t in range(NT):
            tk = self.ln_one(pr, self.RES[:, t * D:(t + 1) * D], lnt, lnsq,
                             self.SML[:, 8 * t:8 * t + 8],
                             war=[evs[-1] if evs else None] + list(war))
            srcs = [(zfm, lnt, tk)]
            if xrf is not None:
                cos, sin = rope_cs
                cosb = cos[:, t * 32:(t + 1) * 32].unsqueeze(1).unsqueeze(3)\
                    .broadcast_to((P, 16, 32, 1))
                sinb = sin[:, t * 32:(t + 1) * 32].unsqueeze(1).unsqueeze(3)\
                    .broadcast_to((P, 16, 32, 1))
                zr = lnt.rearrange("p (h i two) -> p h i two", two=2, i=32)
                xr = xrt.rearrange("p (h i two) -> p h i two", two=2, i=32)
                z_e, z_o = zr[:, :, :, 0:1], zr[:, :, :, 1:2]
                x_e, x_o = xr[:, :, :, 0:1], xr[:, :, :, 1:2]
                tmp = self.SML[:, 256:768].rearrange(
                    "p (h i one) -> p h i one", h=16, i=32)
                pr.op("vec", lambda e, a=tmp, b=z_o, c=sinb: e.tensor_tensor(
                      a, b, c, ALU.mult),
                      waits=[tk] + list(rope_toks), tag="rp1")
                pr.op("vec", lambda e, a=x_e, b=z_e, c=cosb: e.tensor_tensor(
                      a, b, c, ALU.mult), waits=[last_rope], tag="rp2")
                pr.op("vec", lambda e, a=x_e, b=tmp: e.tensor_tensor(
                      a, a, b, ALU.subtract), tag="rp3")
                pr.op("vec", lambda e, a=tmp, b=z_o, c=cosb: e.tensor_tensor(
                      a, b, c, ALU.mult), tag="rp4")
                pr.op("vec", lambda e, a=x_o, b=z_e, c=sinb: e.tensor_tensor(
                      a, b, c, ALU.mult), tag="rp5")
                rt = pr.op("vec", lambda e, a=x_o, b=tmp: e.tensor_tensor(
                      a, a, b, ALU.add), tag="rp6")
                srcs.append((xrf, xrt, rt))
            for dst, srcbuf, tok in srcs:
                for c in range(8):
                    pb = PB[5 + (c % 2)]
                    t_tr = pr.op("ten", lambda e, cc=c, b=pb, s=srcbuf: e.transpose(
                        b[:, 0:128], s[:, cc * 128:(cc + 1) * 128], ident),
                        waits=[tok, evs[-2] if len(evs) >= 2 else None], tag="fmt")
                    t_ev = pr.op("sca", lambda e, b=pb, cc=c, tt=t, d=dst: e.copy(
                        d[:, cc * T + tt * P: cc * T + (tt + 1) * P], b[:, 0:128]),
                        waits=[t_tr], tag="fme")
                    evs.append(t_ev)
                if dst is xrf:
                    last_rope = evs[-1]
        return evs[-1]

    def attn_produce(self, pr, l):
        C = self.CONST
        PB = self.PB
        A = self.AR
        A.reset()
        ZFM = A.take(8 * T, BF16)
        XRF = A.take(8 * T, BF16) if l == 0 else None
        LNT = A.take(D)
        LNSQ = A.take(D)
        XRT = A.take(D) if l == 0 else None
        WB = [A.take(8 * 128, BF16), A.take(8 * 128, BF16)]
        EV = A.take(T, BF16)
        bar = pr.op("pol", lambda e: e.nop(), tag="postbar")
        if l == 0:
            RCOS = A.take(NT * 32)
            RSIN = A.take(NT * 32)
            t_rc = pr.dma("sp", RCOS, self.inp["rope_cos"].ap(), waits=[bar],
                          tag="rcld")
            t_rs = pr.dma("sp", RSIN, self.inp["rope_sin"].ap(), tag="rsld")
            war0 = [bar, t_rc, t_rs]
            rope_cs = (RCOS, RSIN)
        else:
            war0 = [bar]
            rope_cs = None
        fm_tok = self.make_fm(pr, ZFM, xrf=XRF, lnt=LNT, lnsq=LNSQ, xrt=XRT,
                              rope_cs=rope_cs, rope_toks=war0, war=war0)
        if l == 0:
            plan = [("wq0_T", self.Qb[0], "bq0_pc", XRF),
                    ("wk0_T", self.Kb[0], "bk0_pc", XRF),
                    ("wv0_T", self.Vb[0], "bv0_pc", ZFM)]
        else:
            plan = [("wq1", self.Qb[1], None, ZFM),
                    ("wk1", self.Kb[1], None, ZFM),
                    ("wv1", self.Vb[1], None, ZFM)]
        wprev = [None, None]
        ev_war = None
        bank_tok = {0: None, 1: None}
        for w_name, stage_buf, bias_col, src_fm in plan:
            w_ap = self.inp[w_name].ap().rearrange("(kc p) n -> p kc n", p=P)
            for co in range(8):
                wb = WB[co % 2]
                t_w = pr.dma("pol", wb.rearrange("p (kc n) -> p kc n", n=128),
                             w_ap[:, :, co * 128:(co + 1) * 128],
                             waits=[wprev[co % 2]], tag="qkv_wld")
                evs = []
                mm = None
                for tt in range(4):
                    bk = tt % 2
                    for kc in range(8):
                        mm = pr.op("ten", lambda e, k=kc, w=wb, t_=tt, b=bk, s=src_fm:
                                   e.matmul(PB[b][:, :],
                                            w[:, k * 128:(k + 1) * 128],
                                            s[:, k * T + t_ * 512:k * T + (t_ + 1) * 512],
                                            start=(k == 0), stop=(k == 7)),
                                   waits=([t_w, fm_tok, bank_tok[bk]] if kc == 0
                                          else []), tag="qkv_mm")
                    bias = C[bias_col][:, co:co + 1] if bias_col else 0.0
                    t_ev = pr.op("sca", lambda e, b=bk, t_=tt, bi=bias: e.activation(
                        EV[:, t_ * 512:(t_ + 1) * 512], PB[b][:, :], ACT.Identity,
                        bias=bi), waits=[mm, ev_war], tag="qkv_ev")
                    bank_tok[bk] = t_ev
                    evs.append(t_ev)
                wprev[co % 2] = mm
                t_st = pr.dma_hv("pol",
                    lambda h, c_=co, sb=stage_buf: dslice(sb, h * D + c_ * P, P, T, 0, T),
                    lambda h: EV[:, 0:T], waits=evs, tag="qkv_st")
                ev_war = t_st
        pr.barrier_all()

    def attn_heads(self, pr, l):
        C = self.CONST
        PB, PBF = self.PB, self.PBF
        A = self.AR
        A.reset()
        Q2 = A.take(4096, BF16)
        K2 = A.take(4096, BF16)
        QH = A.take(4096, BF16)
        VF = A.take(4096, BF16)
        VT2 = A.take(32 * 65 + 15, BF16)
        ON = A.take(4096, BF16)
        PT = [A.take(512, BF16), A.take(512, BF16)]
        OE = A.take(512, BF16)
        MSKT = A.take(4 * 512)
        CQ = self.SML[:, 128:132]
        MA = self.SML[:, 132:133]
        MT = self.SML[:, 133:134]
        CT = self.SML[0:4, 136:264]
        RR = self.SML[0:1, 300:812]
        Qb, Kb, Vb, Ob = self.Qb[l], self.Kb[l], self.Vb[l], self.Ob[l]

        bar = pr.op("pol", lambda e: e.nop(), tag="postbar")
        t_msk = pr.dma("sp", MSKT, self.inp["maskT"].ap(), waits=[bar], tag="mskld")
        t_o1 = pr.op("vec", lambda e: e.memset(K2[64:65, :], 1.0), waits=[bar],
                     tag="ones1")
        t_o2 = pr.op("vec", lambda e: e.memset(
            VT2[:, 0:2080].rearrange("p (kt c) -> p kt c", c=65)[:, :, 64:65], 1.0),
            tag="ones2")
        w_k2, w_qh, w_vf = [t_o1], [None], [None]
        w_q2, w_vt, w_on = [None], [t_o2], [None]
        p1bank = {0: None, 1: None}
        p2bank = {2: None, 3: None}
        exp_war = [None, None]
        av_start_war = []
        pb5_war = None
        ct_tok = None
        ctr_tok = None
        for hl in range(8):
            tk2, tqh, tvf = [], [], []
            for seg in range(2):
                tk2.append(pr.dma_hv("pol",
                    lambda h, s=seg: K2[0:64, s * T:(s + 1) * T],
                    lambda h, s=seg, hh=hl: dslice(Kb, s * D + h * 512 + hh * 64, 64, T, 0, T),
                    waits=w_k2, tag="k2ld"))
                tqh.append(pr.dma_hv("pol",
                    lambda h, s=seg: QH[0:64, s * T:(s + 1) * T],
                    lambda h, s=seg, hh=hl: dslice(Qb, s * D + h * 512 + hh * 64, 64, T, 0, T),
                    waits=w_qh, tag="qhld"))
                tvf.append(pr.dma_hv("pol",
                    lambda h, s=seg: VF[0:64, s * T:(s + 1) * T],
                    lambda h, s=seg, hh=hl: dslice(Vb, s * D + h * 512 + hh * 64, 64, T, 0, T),
                    waits=w_vf, tag="vfld"))
            tq2 = pr.op("sca", lambda e: e.activation(
                Q2[0:64, :], QH[0:64, :], ACT.Identity, scale=0.125),
                waits=tqh + w_q2, tag="qscale")
            tvt = None
            for kt in range(32):
                t_tr = pr.op("ten", lambda e, k=kt: e.transpose(
                    PBF[:, 0:64], VF[0:64, k * 128:(k + 1) * 128],
                    self.IDB[0:64, 0:64]),
                    waits=tvf + w_vt + [tvt], tag="vtr")
                tvt = pr.op("vec", lambda e, k=kt: e.tensor_copy(
                    VT2[:, k * 65:k * 65 + 64], PBF[:, 0:64]),
                    waits=[t_tr], tag="vte")

            av_tok = None
            st_last = None
            t_exp = None
            ons = []
            for qc in range(8):
                negm_toks = []
                for qt in range(4):
                    qg = qc * 4 + qt
                    kext = (qg + 1) * 128
                    nch = (kext + 511) // 512
                    red_tok = None
                    for kcH in range(nch):
                        n = min(512, kext - kcH * 512)
                        bk = kcH % 2
                        mm = pr.op("ten", lambda e, k=kcH, q_=qg, b=bk, nn=n:
                                   e.matmul(PB[b][:, 0:nn],
                                            QH[0:64, q_ * 128:(q_ + 1) * 128],
                                            K2[0:64, k * 512:k * 512 + nn],
                                            start=True, stop=True),
                                   waits=tk2 + tqh + [p1bank[bk]], tag="s_mm")
                        if (kcH + 1) * 512 >= kext:
                            dcol = qg * 128 - kcH * 512
                            mm = pr.op("vec", lambda e, b=bk, dc=dcol: e.tensor_tensor(
                                PB[b][:, dc:dc + 128], PB[b][:, dc:dc + 128],
                                C["diag_mask"][:, :], ALU.add),
                                waits=[mm], tag="s_mask")
                        dst = MA if kcH == 0 else MT
                        rt = pr.op("vec", lambda e, b=bk, d=dst, nn=n: e.reduce_max(
                            d, PB[b][:, 0:nn], axis=AX.X),
                            waits=[mm, red_tok], tag="s_max")
                        if kcH > 0:
                            rt = pr.op("vec", lambda e: e.tensor_max(MA, MA, MT),
                                       tag="s_maxacc")
                        p1bank[bk] = rt
                        red_tok = rt
                    negm_toks.append(pr.op("vec", lambda e, q_=qt: e.tensor_scalar(
                        CQ[:, q_:q_ + 1], MA, -0.125, None, ALU.mult),
                        waits=[red_tok, ctr_tok], tag="negm"))
                ctr_tok = pr.op("ten", lambda e: e.transpose(
                    PB[4][0:4, 0:128], CQ[:, 0:4], C["ident"][:, :]),
                    waits=negm_toks + av_start_war, tag="ctr")
                t_cte = pr.op("vec", lambda e: e.tensor_copy(
                    CT, PB[4][0:4, 0:128]), waits=[ctr_tok], tag="cte")
                t_fl = pr.dma("pol",
                              Q2[64:65, qc * 512:(qc + 1) * 512]
                              .rearrange("p (a b) -> p a b", a=4),
                              CT.unsqueeze(1),
                              waits=[t_cte, ct_tok, st_last], tag="cflat")
                ct_tok = t_fl
                nkt = (qc + 1) * 4
                av_prev = list(av_start_war)
                for kt in range(nkt):
                    bk = 2 + kt % 2
                    mm = pr.op("ten", lambda e, k=kt, q_=qc, b=bk: e.matmul(
                        PB[b][:, :], K2[0:65, k * 128:(k + 1) * 128],
                        Q2[0:65, q_ * 512:(q_ + 1) * 512], start=True, stop=True),
                        waits=tk2 + [tq2, t_fl, p2bank[bk]], tag="st_mm")
                    st_last = mm
                    if kt >= qc * 4:
                        j = kt - qc * 4
                        mm = pr.op("vec", lambda e, b=bk, jj=j: e.tensor_tensor(
                            PB[b][:, :], PB[b][:, :],
                            MSKT[:, jj * 512:(jj + 1) * 512], ALU.add),
                            waits=[mm, t_msk], tag="st_mask")
                    pp = kt % 2
                    t_exp = pr.op("sca", lambda e, b=bk, p_=pp: e.activation(
                        PT[p_][:, :], PB[b][:, :], ACT.Exp),
                        waits=[mm, exp_war[pp]], tag="st_exp")
                    p2bank[bk] = t_exp
                    av_tok = pr.op("ten", lambda e, k=kt, p_=pp, la=(kt == nkt - 1):
                                   e.matmul(PB[4][0:65, :],
                                            VT2[:, k * 65:(k + 1) * 65],
                                            PT[p_][:, :], start=(k == 0), stop=la),
                                   waits=[t_exp, tvt] + (av_prev if kt == 0 else []),
                                   tag="av_mm")
                    exp_war[pp] = av_tok
                t_rr = pr.op("vec", lambda e: e.reciprocal(RR, PB[4][64:65, :]),
                             waits=[av_tok], tag="rrow")
                t_rb = pr.op("ten", lambda e: e.matmul(
                    PB[5][0:64, :], C["ones_row"][0:1, 0:64], RR,
                    start=True, stop=True), waits=[t_rr, pb5_war], tag="rbc")
                t_oe = pr.op("sca", lambda e: e.copy(OE[0:64, 0:512],
                             PB[4][0:64, :]), waits=[av_tok, t_rr], tag="o_ev")
                t_on = pr.op("vec", lambda e, q_=qc: e.tensor_tensor(
                    ON[0:64, q_ * 512:(q_ + 1) * 512], OE[0:64, 0:512],
                    PB[5][0:64, :], ALU.mult),
                    waits=[t_oe, t_rb] + w_on, tag="o_norm")
                ons.append(t_on)
                pb5_war = t_on
                av_start_war = [t_oe, t_rr]
            t_ost = pr.dma_hv("pol",
                lambda h, hh=hl: dslice(Ob, h * 512 + hh * 64, 64, 2 * T, 0, 2 * T),
                lambda h: ON[0:64, :], waits=ons, tag="onst")
            w_k2 = [st_last]
            w_qh = [st_last]
            w_q2 = [t_exp]
            w_vf = [av_tok]
            w_vt = [av_tok]
            w_on = [t_ost]
        pr.barrier_all()

    def attn_out(self, pr, l):
        C = self.CONST
        PB = self.PB
        A = self.AR
        A.reset()
        if l == 0:
            self.X1 = A.take(NT * D)
        OALL = A.take(8 * T, BF16)
        WOB = A.take(8 * 1024, BF16)
        Z0 = A.take(D)
        bar = pr.op("pol", lambda e: e.nop(), tag="postbar")
        cl = []
        if l == 0:
            BO = A.take(D)
            LW1 = A.take(D)
            LB1 = A.take(D)
            cl.append(pr.dma("sp", BO, self.inp["bo0_bc"].ap(), waits=[bar], tag="bold"))
            cl.append(pr.dma("sp", LW1, self.inp["ln1_w_bc"].ap(), tag="lw1"))
            cl.append(pr.dma("sp", LB1, self.inp["ln1_b_bc"].ap(), tag="lb1"))
        w_name = "wo0_T" if l == 0 else "wo1"
        Ob = self.Ob[l]
        ld = []
        for r in range(8):
            ld.append(pr.dma_hv("pol",
                lambda h, rr=r: OALL[:, rr * T:(rr + 1) * T],
                lambda h, rr=r: dslice(Ob, rr * P, P, 2 * T, h * T, T),
                waits=[bar], tag="oall"))
        w_ap = self.inp[w_name].ap().rearrange("(kc p) n -> kc p n", p=P)
        lw = [pr.dma("pol", WOB[:, kc * 1024:(kc + 1) * 1024], w_ap[kc],
                     tag="wold") for kc in range(8)]
        bank_tok = {0: None, 1: None}
        ln_war = None
        for tt in range(NT):
            outtoks = []
            for dc in range(2):
                mm = None
                for kc in range(8):
                    mm = pr.op("ten", lambda e, k=kc, t_=tt, d_=dc: e.matmul(
                        PB[d_][:, :], OALL[:, k * T + t_ * P:k * T + (t_ + 1) * P],
                        WOB[:, k * 1024 + d_ * 512:k * 1024 + (d_ + 1) * 512],
                        start=(k == 0), stop=(k == 7)),
                        waits=(ld + lw + [bank_tok[dc]] if kc == 0 else []),
                        tag="wo_mm")
                outtoks.append(mm)
            if l == 1:
                for dc in range(2):
                    t_add = pr.op("vec", lambda e, t_=tt, d_=dc: e.tensor_tensor(
                        self.RES[:, t_ * D + d_ * 512:t_ * D + (d_ + 1) * 512],
                        self.RES[:, t_ * D + d_ * 512:t_ * D + (d_ + 1) * 512],
                        PB[d_][:, :], ALU.add), waits=[outtoks[dc]], tag="res1")
                    bank_tok[dc] = t_add
            else:
                tz = self.ln_one(pr, self.RES[:, tt * D:(tt + 1) * D], Z0, Z0,
                                 self.SML[:, 8 * tt:8 * tt + 8],
                                 waits=cl, war=[ln_war])
                for dc in range(2):
                    t1 = pr.op("vec", lambda e, d_=dc: e.tensor_tensor(
                        Z0[:, d_ * 512:(d_ + 1) * 512],
                        Z0[:, d_ * 512:(d_ + 1) * 512], PB[d_][:, :], ALU.add),
                        waits=[outtoks[dc], tz], tag="z0a")
                    t2 = pr.op("vec", lambda e, d_=dc: e.tensor_tensor(
                        Z0[:, d_ * 512:(d_ + 1) * 512],
                        Z0[:, d_ * 512:(d_ + 1) * 512],
                        BO[:, d_ * 512:(d_ + 1) * 512], ALU.add), tag="boa")
                    bank_tok[dc] = t2
                ln_war = self.ln_one(
                    pr, Z0, self.X1[:, tt * D:(tt + 1) * D],
                    self.X1[:, tt * D:(tt + 1) * D],
                    self.SML[:, 264:272], wb=(LW1, LB1),
                    waits=[bank_tok[0], bank_tok[1]], war=[])
                bank_tok[0] = ln_war
                bank_tok[1] = ln_war
        pr.barrier_all()

    def moe_l0(self, pr):
        C = self.CONST
        PB = self.PB
        A = self.AR
        A.reset()
        A.off = NT * D
        X1 = self.X1
        X1FM = A.take(8 * 512, BF16)
        MACC = A.take(4 * 1024)
        EW = A.take(8 * 1024, BF16)
        GTW = A.take(64, BF16)
        EB = A.take(1024)
        W8 = A.take(NT * 8)
        W8T = A.take(4 * 128)
        LW2 = A.take(D)
        LB2 = A.take(D)
        TMP = A.take(D)
        G8 = self.SML[:, 0:8]
        T8 = self.SML[:, 8:16]
        M8 = self.SML[:, 16:24]
        SU = self.SML[:, 24:32]
        ident = C["ident"][:, :]

        bar = pr.op("pol", lambda e: e.nop(), tag="postbar")
        g_ap = self.inp["gate_T"].ap().rearrange("(kc p) n -> p kc n", p=P)
        t_gw = pr.dma("pol", GTW.rearrange("p (kc n) -> p kc n", n=8), g_ap,
                      waits=[bar], tag="gwld")
        t_eb = pr.dma("sp", EB[0:8, :], self.inp["eb"].ap(), waits=[bar], tag="ebld")
        t_l2 = [pr.dma("sp", LW2, self.inp["ln2_w_bc"].ap(), tag="lw2"),
                pr.dma("sp", LB2, self.inp["ln2_b_bc"].ap(), tag="lb2")]

        macc_war = [None] * 4
        res_done = []
        gexp_war = None
        ebank = {0: None, 1: None}
        ew_prev = None
        pb6_tok = None
        for half_t in range(4):
            t0 = half_t * 4
            fm_toks = []
            for ti in range(4):
                for c in range(8):
                    pb = PB[5 + (c % 2)]
                    t_tr = pr.op("ten", lambda e, cc=c, b=pb, t_=t0 + ti: e.transpose(
                        b[:, 0:128], X1[:, t_ * D + cc * 128:t_ * D + (cc + 1) * 128],
                        ident),
                        waits=[fm_toks[-2] if len(fm_toks) >= 2 else None,
                               ("vec", self.cnt["vec"]), ("sca", self.cnt["sca"]),
                               bar], tag="x1t")
                    t_ev = pr.op("sca", lambda e, b=pb, cc=c, ti_=ti: e.copy(
                        X1FM[:, cc * 512 + ti_ * P:cc * 512 + (ti_ + 1) * P],
                        b[:, 0:128]), waits=[t_tr], tag="x1e")
                    fm_toks.append(t_ev)
            fm_tok = fm_toks[-1]
            w8_toks = []
            gbank = None
            for ti in range(4):
                tt = t0 + ti
                mm = None
                for kc in range(8):
                    mm = pr.op("ten", lambda e, k=kc, ti_=ti: e.matmul(
                        PB[2][:, 0:8],
                        X1FM[:, k * 512 + ti_ * P:k * 512 + (ti_ + 1) * P],
                        GTW[:, k * 8:(k + 1) * 8], start=(k == 0), stop=(k == 7)),
                        waits=([t_gw, fm_tok, gbank] if kc == 0 else []),
                        tag="gmm")
                t_b = pr.op("vec", lambda e: e.tensor_tensor(
                    G8, PB[2][:, 0:8], C["gate_b_bc"][:, 0:8], ALU.add),
                    waits=[mm, gexp_war], tag="gbias")
                gbank = t_b
                t_m = pr.op("vec", lambda e: e.reduce_max(SU[:, 0:1], G8, axis=AX.X),
                            tag="gmax")
                t_nm = pr.op("vec", lambda e: e.tensor_scalar(
                    SU[:, 1:2], SU[:, 0:1], -1.0, None, ALU.mult), tag="gneg")
                t_e = pr.op("sca", lambda e: e.activation(
                    T8, G8, ACT.Exp, bias=SU[:, 1:2], accum_out=SU[:, 2:3]),
                    waits=[t_nm], tag="gexp")
                gexp_war = t_e
                t_r = pr.op("vec", lambda e: e.reciprocal(SU[:, 3:4], SU[:, 2:3]),
                            waits=[t_e], tag="grec")
                t_g = pr.op("vec", lambda e: e.tensor_scalar(
                    G8, T8, SU[:, 3:4], None, ALU.mult), tag="gnorm")
                t_m1 = pr.op("vec", lambda e: e.reduce_max(SU[:, 0:1], G8, axis=AX.X),
                             tag="gm1")
                pr.op("vec", lambda e: e.tensor_scalar(
                    M8, G8, SU[:, 0:1], None, ALU.is_lt), tag="gm2b")
                pr.op("vec", lambda e: e.tensor_tensor(T8, G8, M8, ALU.mult),
                      tag="gm2c")
                t_m2 = pr.op("vec", lambda e: e.reduce_max(SU[:, 1:2], T8,
                             axis=AX.X), tag="gm2")
                t_msk2 = pr.op("vec", lambda e: e.tensor_scalar(
                    M8, G8, SU[:, 1:2], None, ALU.is_ge), tag="gge")
                t_w8 = pr.op("vec", lambda e, tt_=tt: e.tensor_tensor(
                    W8[:, tt_ * 8:(tt_ + 1) * 8], G8, M8, ALU.mult), tag="gw8")
                w8_toks.append(t_w8)
            w8t = []
            for ti in range(4):
                tt = t0 + ti
                t_tr = pr.op("ten", lambda e, tt_=tt: e.transpose(
                    PB[3][0:8, 0:128], W8[:, tt_ * 8:(tt_ + 1) * 8], ident),
                    waits=[w8_toks[ti], w8t[-1] if w8t else None], tag="w8t")
                t_ev = pr.op("vec", lambda e, ti_=ti: e.tensor_copy(
                    W8T[0:8, ti_ * 128:(ti_ + 1) * 128], PB[3][0:8, 0:128]),
                    waits=[t_tr], tag="w8te")
                w8t.append(t_ev)
            for ti in range(4):
                for dc in range(2):
                    t_bm = pr.op("ten", lambda e, ti_=ti, d_=dc: e.matmul(
                        PB[6][:, :], W8T[0:8, ti_ * 128:(ti_ + 1) * 128],
                        EB[0:8, d_ * 512:(d_ + 1) * 512], start=True, stop=True),
                        waits=[w8t[ti], t_eb, fm_tok, macc_war[ti], pb6_tok],
                        tag="biasmm")
                    t_mc = pr.op("vec", lambda e, ti_=ti, d_=dc: e.tensor_copy(
                        MACC[:, ti_ * 1024 + d_ * 512:ti_ * 1024 + (d_ + 1) * 512],
                        PB[6][:, :]), waits=[t_bm, macc_war[ti]], tag="macc0")
                    macc_war[ti] = t_mc
                    pb6_tok = t_mc
            ew_ap = self.inp["ew_T"].ap()
            mm = None
            for e_ in range(8):
                t_ew = pr.dma("pol", EW.rearrange("p (kc n) -> p kc n", n=1024),
                              ew_ap[e_].rearrange("(kc p) n -> p kc n", p=P),
                              waits=[ew_prev], tag="ewld")
                for ti in range(4):
                    tt = t0 + ti
                    for dc in range(2):
                        bk = dc
                        for kc in range(8):
                            mm = pr.op("ten", lambda e, k=kc, ti_=ti, d_=dc, b=bk:
                                       e.matmul(PB[b][:, :],
                                                X1FM[:, k * 512 + ti_ * P:
                                                     k * 512 + (ti_ + 1) * P],
                                                EW[:, k * 1024 + d_ * 512:
                                                   k * 1024 + (d_ + 1) * 512],
                                                start=(k == 0), stop=(k == 7)),
                                       waits=([t_ew, fm_tok, ebank[bk]] if kc == 0
                                              else []), tag="emm")
                        t_acc = pr.op("vec", lambda e, ti_=ti, d_=dc, b=bk, ee=e_,
                                      tt_=tt: e.scalar_tensor_tensor(
                            MACC[:, ti_ * 1024 + d_ * 512:ti_ * 1024 + (d_ + 1) * 512],
                            PB[b][:, :], W8[:, tt_ * 8 + ee:tt_ * 8 + ee + 1],
                            MACC[:, ti_ * 1024 + d_ * 512:ti_ * 1024 + (d_ + 1) * 512],
                            ALU.mult, ALU.add),
                            waits=[mm, macc_war[ti]], tag="eacc")
                        ebank[bk] = t_acc
                        macc_war[ti] = t_acc
                ew_prev = mm
            for ti in range(4):
                tt = t0 + ti
                t_s = pr.op("vec", lambda e, ti_=ti, tt_=tt: e.tensor_tensor(
                    MACC[:, ti_ * 1024:(ti_ + 1) * 1024],
                    MACC[:, ti_ * 1024:(ti_ + 1) * 1024],
                    X1[:, tt_ * D:(tt_ + 1) * D], ALU.add),
                    waits=[macc_war[ti]], tag="x2sum")
                t_ln = self.ln_one(pr, MACC[:, ti * 1024:(ti + 1) * 1024], TMP,
                                   TMP, self.SML[:, 272:280], wb=(LW2, LB2),
                                   waits=t_l2 + [t_s],
                                   war=[res_done[-1] if res_done else None])
                t_res = pr.op("vec", lambda e, tt_=tt: e.tensor_tensor(
                    self.RES[:, tt_ * D:(tt_ + 1) * D],
                    self.RES[:, tt_ * D:(tt_ + 1) * D], TMP, ALU.add),
                    waits=[t_ln], tag="res2")
                res_done.append(t_res)
            macc_war = [res_done[-1]] * 4
        pr.barrier_all()

    def ffn_l1(self, pr):
        C = self.CONST
        PB = self.PB
        A = self.AR
        A.reset()
        ZFM = A.take(8 * T, BF16)
        GEF = A.take(8 * 1024, BF16)
        MACC = A.take(8 * 1024)
        W8 = A.take(NT * 8)
        WRT = A.take(64, BF16)
        W12 = [A.take(8 * 128, BF16), A.take(8 * 128, BF16)]
        W2F = A.take(8 * 1024, BF16)
        SG = [A.take(512, BF16), A.take(512, BF16)]
        LNT = A.take(D)
        LNSQ = A.take(D)
        G8 = self.SML[:, 0:8]
        T8 = self.SML[:, 8:16]
        M8 = self.SML[:, 16:24]
        SU = self.SML[:, 24:32]

        bar = pr.op("pol", lambda e: e.nop(), tag="postbar")
        fm_tok = self.make_fm(pr, ZFM, lnt=LNT, lnsq=LNSQ, war=[bar])
        wr_ap = self.inp["w_router"].ap().rearrange("(kc p) n -> p kc n", p=P)
        t_wr = pr.dma("pol", WRT.rearrange("p (kc n) -> p kc n", n=8), wr_ap,
                      waits=[bar], tag="wrld")
        w8_toks = []
        rbank = None
        rexp_war = None
        for tt in range(NT):
            mm = None
            for kc in range(8):
                mm = pr.op("ten", lambda e, k=kc, t_=tt: e.matmul(
                    PB[6][:, 0:8], ZFM[:, k * T + t_ * P:k * T + (t_ + 1) * P],
                    WRT[:, k * 8:(k + 1) * 8], start=(k == 0), stop=(k == 7)),
                    waits=([t_wr, fm_tok, rbank] if kc == 0 else []), tag="rmm")
            t_m = pr.op("vec", lambda e: e.reduce_max(SU[:, 0:1], PB[6][:, 0:8],
                        axis=AX.X), waits=[mm], tag="rmax")
            t_nm = pr.op("vec", lambda e: e.tensor_scalar(
                SU[:, 1:2], SU[:, 0:1], -1.0, None, ALU.mult), tag="rneg")
            t_e = pr.op("sca", lambda e: e.activation(
                T8, PB[6][:, 0:8], ACT.Exp, bias=SU[:, 1:2], accum_out=SU[:, 2:3]),
                waits=[t_nm, rexp_war], tag="rexp")
            rexp_war = t_e
            rbank = t_e
            t_r = pr.op("vec", lambda e: e.reciprocal(SU[:, 3:4], SU[:, 2:3]),
                        waits=[t_e], tag="rrec")
            t_g = pr.op("vec", lambda e: e.tensor_scalar(
                G8, T8, SU[:, 3:4], None, ALU.mult), tag="rnorm")
            t_m1 = pr.op("vec", lambda e: e.reduce_max(SU[:, 0:1], G8, axis=AX.X),
                         tag="rm1")
            pr.op("vec", lambda e: e.tensor_scalar(
                M8, G8, SU[:, 0:1], None, ALU.is_lt), tag="rm2b")
            pr.op("vec", lambda e: e.tensor_tensor(T8, G8, M8, ALU.mult),
                  tag="rm2c")
            t_m2 = pr.op("vec", lambda e: e.reduce_max(SU[:, 1:2], T8,
                         axis=AX.X), tag="rm2")
            t_sum = pr.op("vec", lambda e: e.tensor_tensor(
                SU[:, 4:5], SU[:, 0:1], SU[:, 1:2], ALU.add), tag="rsum12")
            t_rc = pr.op("vec", lambda e: e.reciprocal(SU[:, 5:6], SU[:, 4:5]),
                         tag="rrec2")
            t_msk2 = pr.op("vec", lambda e: e.tensor_scalar(
                M8, G8, SU[:, 1:2], None, ALU.is_ge), tag="rge")
            t_w = pr.op("vec", lambda e: e.tensor_tensor(M8, M8, G8, ALU.mult),
                        tag="rw")
            t_w8 = pr.op("vec", lambda e, tt_=tt: e.tensor_scalar(
                W8[:, tt_ * 8:(tt_ + 1) * 8], M8, SU[:, 5:6], None, ALU.mult),
                tag="rw8")
            w8_toks.append(t_w8)
        w8_done = w8_toks[-1]

        macc_war = [None] * 8
        res_last = None
        for tb in range(2):
            tb0 = tb * 8
            first_acc = True
            for e_ in [8] + list(range(8)):
                shared = (e_ == 8)
                if shared:
                    w1ap, w3ap, w2ap = (self.inp["sw1"].ap(), self.inp["sw3"].ap(),
                                        self.inp["sw2"].ap())
                else:
                    w1ap = self.inp["w1"].ap()[e_]
                    w3ap = self.inp["w3"].ap()[e_]
                    w2ap = self.inp["w2"].ap()[e_]
                for hg in range(2):
                    hg0 = hg * 8
                    g_toks = []
                    wld_prev = [None, None]
                    hbank = {0: None, 1: None, 2: None, 3: None}
                    gm_war = [None, None]
                    for hc8 in range(8):
                        hc = hg0 + hc8
                        t_w1 = pr.dma("pol",
                            W12[0].rearrange("p (kc n) -> p kc n", n=128),
                            w1ap.rearrange("(kc p) n -> p kc n", p=P)
                            [:, :, hc * 128:(hc + 1) * 128],
                            waits=[wld_prev[0]], tag="w1ld")
                        t_w3 = pr.dma("pol",
                            W12[1].rearrange("p (kc n) -> p kc n", n=128),
                            w3ap.rearrange("(kc p) n -> p kc n", p=P)
                            [:, :, hc * 128:(hc + 1) * 128],
                            waits=[wld_prev[1]], tag="w3ld")
                        mm1 = mm3 = None
                        for tc2 in range(2):
                            b1, b3 = tc2, 2 + tc2
                            for kc in range(8):
                                col = kc * T + tb0 * P + tc2 * 512
                                mm1 = pr.op("ten", lambda e, k=kc, cl=col, b=b1:
                                            e.matmul(PB[b][:, :],
                                                     W12[0][:, k * 128:(k + 1) * 128],
                                                     ZFM[:, cl:cl + 512],
                                                     start=(k == 0), stop=(k == 7)),
                                            waits=([t_w1, fm_tok, hbank[b1]]
                                                   if kc == 0 else []),
                                            tag="h1mm")
                            for kc in range(8):
                                col = kc * T + tb0 * P + tc2 * 512
                                mm3 = pr.op("ten", lambda e, k=kc, cl=col, b=b3:
                                            e.matmul(PB[b][:, :],
                                                     W12[1][:, k * 128:(k + 1) * 128],
                                                     ZFM[:, cl:cl + 512],
                                                     start=(k == 0), stop=(k == 7)),
                                            waits=([t_w3, fm_tok, hbank[b3]]
                                                   if kc == 0 else []),
                                            tag="h3mm")
                            t_si = pr.op("sca", lambda e, b=b1, t2=tc2: e.activation(
                                SG[t2], PB[b][:, :], ACT.Silu),
                                waits=[mm1, gm_war[tc2]], tag="silu1")
                            t_gm = pr.op("vec", lambda e, b=b3, h_=hc8, t2=tc2:
                                         e.tensor_tensor(
                                GEF[:, h_ * 1024 + t2 * 512:h_ * 1024 + (t2 + 1) * 512],
                                SG[t2], PB[b][:, :], ALU.mult),
                                waits=[t_si, mm3], tag="gmul2")
                            hbank[b1] = t_gm
                            hbank[b3] = t_gm
                            gm_war[tc2] = t_gm
                            g_toks.append(t_gm)
                        wld_prev = [mm1, mm3]
                    g_done = g_toks[-1]
                    t_w2 = pr.dma("pol", W2F.rearrange("p (hc n) -> p hc n", n=1024),
                                  w2ap.rearrange("(hcp p) n -> p hcp n", p=P)
                                  [:, hg0:hg0 + 8, :],
                                  waits=[wld_prev[0], wld_prev[1]], tag="w2ld")
                    ybank = {4: None, 5: None}
                    for ti in range(8):
                        tt = tb0 + ti
                        for dc in range(2):
                            bk = 4 + dc
                            mm = None
                            for hc8 in range(8):
                                mm = pr.op("ten", lambda e, h_=hc8, ti_=ti, d_=dc,
                                           b=bk: e.matmul(
                                    PB[b][:, :],
                                    GEF[:, h_ * 1024 + ti_ * P:h_ * 1024 + (ti_ + 1) * P],
                                    W2F[:, h_ * 1024 + d_ * 512:h_ * 1024 + (d_ + 1) * 512],
                                    start=(h_ == 0), stop=(h_ == 7)),
                                    waits=([t_w2, g_done, ybank[bk]] if hc8 == 0
                                           else []), tag="ymm")
                            if first_acc:
                                t_acc = pr.op("vec", lambda e, ti_=ti, d_=dc, b=bk:
                                              e.tensor_copy(
                                    MACC[:, ti_ * 1024 + d_ * 512:
                                         ti_ * 1024 + (d_ + 1) * 512], PB[b][:, :]),
                                    waits=[mm, macc_war[ti]], tag="yinit")
                            elif shared:
                                t_acc = pr.op("vec", lambda e, ti_=ti, d_=dc, b=bk:
                                              e.tensor_tensor(
                                    MACC[:, ti_ * 1024 + d_ * 512:
                                         ti_ * 1024 + (d_ + 1) * 512],
                                    MACC[:, ti_ * 1024 + d_ * 512:
                                         ti_ * 1024 + (d_ + 1) * 512],
                                    PB[b][:, :], ALU.add),
                                    waits=[mm, macc_war[ti]], tag="yshacc")
                            else:
                                t_acc = pr.op("vec", lambda e, ti_=ti, d_=dc, b=bk,
                                              ee=e_, tt_=tt: e.scalar_tensor_tensor(
                                    MACC[:, ti_ * 1024 + d_ * 512:
                                         ti_ * 1024 + (d_ + 1) * 512],
                                    PB[b][:, :], W8[:, tt_ * 8 + ee:tt_ * 8 + ee + 1],
                                    MACC[:, ti_ * 1024 + d_ * 512:
                                         ti_ * 1024 + (d_ + 1) * 512],
                                    ALU.mult, ALU.add),
                                    waits=[mm, macc_war[ti], w8_done], tag="yacc2")
                            ybank[bk] = t_acc
                            macc_war[ti] = t_acc
                    first_acc = False
            for ti in range(8):
                tt = tb0 + ti
                t_res = pr.op("vec", lambda e, ti_=ti, tt_=tt: e.tensor_tensor(
                    self.RES[:, tt_ * D:(tt_ + 1) * D],
                    self.RES[:, tt_ * D:(tt_ + 1) * D],
                    MACC[:, ti_ * 1024:(ti_ + 1) * 1024], ALU.add),
                    waits=[macc_war[ti], res_last], tag="res3")
                macc_war[ti] = t_res
                res_last = t_res
        pr.barrier_all()


def kernel(taps=(), **inputs):
    in_maps = prep_host(inputs)
    mk = MK(taps=taps)
    nc = mk.build()
    res = run_bass_kernel_spmd(nc, in_maps, core_ids=list(range(8)))
    out = np.empty((4, 4096, 1024), np.float32)
    for c in range(8):
        b, h = c // 2, c % 2
        out[b, h * T:(h + 1) * T] = res.results[c]["y"]
    if taps:
        return out, res.results
    return out


# version 03 (final)
# speedup vs baseline: 1.5555x; 1.1538x over previous
import numpy as np
import ml_dtypes

import concourse.bass as bass
import concourse.mybir as mybir
from concourse.bass_utils import run_bass_kernel_spmd

F32 = mybir.dt.float32
BF16 = mybir.dt.bfloat16
ALU = mybir.AluOpType
ACT = mybir.ActivationFunctionType
AX = mybir.AxisListType

P = 128
T = 2048
NT = 16
D = 1024
DI = 2048
DS = 16
LN_EPS = 1e-5
NEG = -1.0e30
BF = ml_dtypes.bfloat16

ENGS = ["ten", "vec", "sca", "pol", "sp"]
DMA_SEM = {"sp": "dsp", "sca": "dact", "pol": "dpol"}
NDMASEM = 6


def f32(x):
    return np.ascontiguousarray(x, dtype=np.float32)


class Prog:

    def __init__(self, nc, sems, cnt, mk=None):
        self.nc = nc
        self.ops = []
        self.sems = sems
        self.cnt = cnt
        self.mk = mk

    CHAIN = {"ten": False, "vec": True, "sca": True, "pol": True, "sp": True}

    def op(self, eng, fn, waits=(), tag=""):
        sem = eng
        prev = self.cnt[sem]
        self.cnt[sem] += 1
        tok = (sem, self.cnt[sem])
        w = self._clean(waits, eng)
        if prev > 0 and self.CHAIN[eng]:
            w.append((sem, prev))
        self.ops.append((eng, fn, w, [(sem, 1)], tag))
        return tok

    def _dma_sem(self, eng):
        q = DMA_SEM[eng]
        idx = self.mk.dma_rr[q]
        self.mk.dma_rr[q] = (idx + 1) % NDMASEM
        return f"{q}{idx}"

    def dma(self, eng, dst, src, waits=(), tag=""):
        sem = self._dma_sem(eng)
        prev = self.cnt[sem]
        self.cnt[sem] += 16
        tok = (sem, self.cnt[sem])
        waits = list(waits) + [(sem, prev)]
        if self.cnt.get(eng, 0):
            waits.append((eng, self.cnt[eng]))

        def _dma(e, d=dst, s=src, t=tag):
            try:
                return e.dma_start(d, s)
            except Exception:
                print(f"DMA FAIL tag={t} dst={d} src={s}")
                raise
        self.ops.append((eng, _dma, self._clean(waits, eng), [(sem, 16)], tag))
        return tok

    def dma_hv(self, eng, dst_fn, src_fn, waits=(), tag=""):
        sem = self._dma_sem(eng)
        prev = self.cnt[sem]
        self.cnt[sem] += 16
        tok = (sem, self.cnt[sem])
        waits = list(waits) + [(sem, prev)]
        if self.cnt.get(eng, 0):
            waits.append((eng, self.cnt[eng]))
        mk = self.mk
        s_h = self.sems[sem]

        def _dma(e, t=tag):
            hreg = mk.half_reg(e, eng)
            try:
                with e.If_eq(hreg, 0):
                    e.dma_start(dst_fn(0), src_fn(0)).then_inc(s_h, 16).annotate(t + "_h0")
                with e.Else():
                    e.dma_start(dst_fn(1), src_fn(1)).then_inc(s_h, 16).annotate(t + "_h1")
            except Exception:
                print(f"DMAHV FAIL tag={t}")
                raise
            return None
        self.ops.append((eng, _dma, self._clean(waits, eng), [], tag))
        return tok

    def _clean(self, waits, eng):
        m = {}
        for w in waits:
            if w is None:
                continue
            s, v = w
            if s == eng:
                continue
            if m.get(s, -1) < v:
                m[s] = v
        return list(m.items())

    def barrier_all(self):
        snap = [(s, v) for s, v in self.cnt.items() if v > 0]
        toks = []
        for e in ENGS:
            toks.append(self.op(e, lambda eng: eng.nop(), waits=snap, tag="pbar"))
        return toks

    def emit(self):
        nc = self.nc
        with nc.Block() as blk:
            def mk_runner(ename):
                def run(e):
                    for eng, fn, waits, incs, tag in self.ops:
                        if eng != ename:
                            continue
                        for s, v in waits:
                            e.wait_ge(self.sems[s], v)
                        try:
                            inst = fn(e)
                        except Exception as ex:
                            import traceback; traceback.print_exc()
                            raise RuntimeError(f"emit fail tag={tag}") from ex
                        if inst is None and incs:
                            inst = e.nop()
                        for s, n in incs:
                            inst.then_inc(self.sems[s], n)
                return run
            blk.tensor(mk_runner("ten"))
            blk.vector(mk_runner("vec"))
            blk.scalar(mk_runner("sca"))
            blk.gpsimd(mk_runner("pol"))
            blk.sync(mk_runner("sp"))


class Arena:
    def __init__(self, ap_f32, ncols_f32):
        self.base = ap_f32
        self.n = ncols_f32
        self.off = 0

    def reset(self):
        self.off = 0

    def take(self, ncols, dtype=F32):
        if dtype == F32:
            a = self.base[:, self.off:self.off + ncols]
            self.off += ncols
        else:
            nf = (ncols + 1) // 2
            a = self.base[:, self.off:self.off + nf].bitcast(BF16)
            self.off += nf
        assert self.off <= self.n, f"arena overflow {self.off} > {self.n}"
        return a


def prep_host(inputs):
    x = np.asarray(inputs["L0_attn_x"], dtype=np.float32)
    c = {}
    c["w_in_T"] = f32(np.asarray(inputs["L0_attn_in_w"]).T)
    c["in_b_pc"] = f32(np.asarray(inputs["L0_attn_in_b"]).reshape(32, P).T)
    cw = np.asarray(inputs["L0_attn_conv_w"])[:, 0, :]
    c["conv_w_pc"] = f32(cw.reshape(16, P, 4).transpose(1, 0, 2).reshape(P, 64))
    c["conv_b_pc"] = f32(np.asarray(inputs["L0_attn_conv_b"]).reshape(16, P).T)
    c["xproj_T"] = f32(np.asarray(inputs["L0_attn_xproj_w"]).T)
    c["dt_w_row"] = f32(np.asarray(inputs["L0_attn_dt_w"])[:, 0].reshape(1, DI))
    c["dt_b_pc"] = f32(np.asarray(inputs["L0_attn_dt_b"]).reshape(16, P).T)
    c["out_w_T"] = f32(np.asarray(inputs["L0_attn_out_w"]).T)

    ipw = np.asarray(inputs["L0_ffn_in_proj_w"])
    ipb = np.asarray(inputs["L0_ffn_in_proj_b"])
    c["wq0_T"] = f32(ipw[0:D].T)
    c["wk0_T"] = f32(ipw[D:2 * D].T)
    c["wv0_T"] = f32(ipw[2 * D:3 * D].T)
    c["bq0_pc"] = f32(ipb[0:D].reshape(8, P).T)
    c["bk0_pc"] = f32(ipb[D:2 * D].reshape(8, P).T)
    c["bv0_pc"] = f32(ipb[2 * D:3 * D].reshape(8, P).T)
    c["wo0_T"] = f32(np.asarray(inputs["L0_ffn_out_proj_w"]).T)
    c["bo0_bc"] = f32(np.broadcast_to(np.asarray(inputs["L0_ffn_out_proj_b"]), (P, D)))
    c["gate_T"] = f32(np.asarray(inputs["L0_ffn_gate_w"]).T)
    c["gate_b_bc"] = f32(np.broadcast_to(np.asarray(inputs["L0_ffn_gate_b"]), (P, 8)))
    c["ew_T"] = f32(np.asarray(inputs["L0_ffn_expert_w"]).transpose(0, 2, 1))
    c["eb"] = f32(np.asarray(inputs["L0_ffn_expert_b"]))
    c["ln1_w_bc"] = f32(np.broadcast_to(np.asarray(inputs["L0_ffn_ln1_w"]), (P, D)))
    c["ln1_b_bc"] = f32(np.broadcast_to(np.asarray(inputs["L0_ffn_ln1_b"]), (P, D)))
    c["ln2_w_bc"] = f32(np.broadcast_to(np.asarray(inputs["L0_ffn_ln2_w"]), (P, D)))
    c["ln2_b_bc"] = f32(np.broadcast_to(np.asarray(inputs["L0_ffn_ln2_b"]), (P, D)))

    c["wq1"] = f32(np.asarray(inputs["L1_attn_Wq"]).reshape(D, D))
    c["wk1"] = f32(np.asarray(inputs["L1_attn_Wk"]).reshape(D, D))
    c["wv1"] = f32(np.asarray(inputs["L1_attn_Wv"]).reshape(D, D))
    c["wo1"] = f32(np.asarray(inputs["L1_attn_Wout"]).reshape(D, D))

    c["w_router"] = f32(np.asarray(inputs["L1_ffn_w_router"]))
    c["w1"] = f32(np.asarray(inputs["L1_ffn_w1"]))
    c["w3"] = f32(np.asarray(inputs["L1_ffn_w3"]))
    c["w2"] = f32(np.asarray(inputs["L1_ffn_w2"]))
    c["sw1"] = f32(np.asarray(inputs["L1_ffn_sw1"]))
    c["sw3"] = f32(np.asarray(inputs["L1_ffn_sw3"]))
    c["sw2"] = f32(np.asarray(inputs["L1_ffn_sw2"]))

    c["ident"] = f32(np.eye(P))
    c["eps_pc"] = np.full((P, 1), LN_EPS, np.float32)
    c["ones_row"] = np.ones((1, P), np.float32)
    mq = np.arange(P)[:, None]
    mk_ = np.arange(P)[None, :]
    c["diag_mask"] = f32(np.where(mk_ <= mq, 0.0, NEG))
    mp = np.arange(P)[:, None]
    mf = np.arange(512)[None, :]
    c["maskT"] = f32(np.concatenate(
        [np.where(mp + j * P > mf, NEG, 0.0) for j in range(4)], axis=1))

    in_maps = []
    for core in range(8):
        b, h = core // 2, core % 2
        d = dict(c)
        d["x_in"] = f32(x[b, h * T:(h + 1) * T])
        if h == 0:
            d["x_halo"] = np.zeros((16, D), np.float32)
            d["halo_b_pc"] = np.zeros((P, 16), np.float32)
        else:
            d["x_halo"] = f32(x[b, h * T - 16:h * T])
            d["halo_b_pc"] = f32(np.asarray(inputs["L0_attn_in_b"])[:DI].reshape(16, P).T)
        A = np.asarray(inputs["L0_attn_A_log"], dtype=np.float32)
        Asl = -np.exp(A[h * 1024:(h + 1) * 1024])
        d["A_sc"] = f32(Asl.reshape(8, P, DS).transpose(1, 0, 2).reshape(P, 8 * DS))
        Dv = np.asarray(inputs["L0_attn_D"], dtype=np.float32)[h * 1024:(h + 1) * 1024]
        d["D_sc"] = f32(Dv.reshape(8, P).T)
        pos = np.arange(h * T, (h + 1) * T, dtype=np.float32)
        inv = 1.0 / (10000.0 ** (np.arange(0, 64, 2, dtype=np.float32) / 64.0))
        fr = pos[:, None] * inv[None, :]
        d["rope_cos"] = f32(np.cos(fr).reshape(NT, P, 32).transpose(1, 0, 2).reshape(P, NT * 32))
        d["rope_sin"] = f32(np.sin(fr).reshape(NT, P, 32).transpose(1, 0, 2).reshape(P, NT * 32))
        in_maps.append(d)
    return in_maps


IN_SHAPES = {
    "x_in": (T, D), "x_halo": (16, D), "halo_b_pc": (P, 16),
    "w_in_T": (D, 2 * DI), "in_b_pc": (P, 32),
    "conv_w_pc": (P, 64), "conv_b_pc": (P, 16),
    "xproj_T": (DI, 33), "dt_w_row": (1, DI), "dt_b_pc": (P, 16),
    "out_w_T": (DI, D),
    "A_sc": (P, 128), "D_sc": (P, 8),
    "wq0_T": (D, D), "wk0_T": (D, D), "wv0_T": (D, D),
    "bq0_pc": (P, 8), "bk0_pc": (P, 8), "bv0_pc": (P, 8),
    "wo0_T": (D, D), "bo0_bc": (P, D),
    "gate_T": (D, 8), "gate_b_bc": (P, 8),
    "ew_T": (8, D, D), "eb": (8, D),
    "ln1_w_bc": (P, D), "ln1_b_bc": (P, D),
    "ln2_w_bc": (P, D), "ln2_b_bc": (P, D),
    "wq1": (D, D), "wk1": (D, D), "wv1": (D, D), "wo1": (D, D),
    "w_router": (D, 8), "w1": (8, D, DI), "w3": (8, D, DI),
    "w2": (8, DI, D), "sw1": (D, DI), "sw3": (D, DI), "sw2": (DI, D),
    "ident": (P, P), "diag_mask": (P, P), "maskT": (P, 4 * 512), "eps_pc": (P, 1),
    "ones_row": (1, P),
    "rope_cos": (P, NT * 32), "rope_sin": (P, NT * 32),
}
CONST_NAMES = ["eps_pc", "in_b_pc", "conv_w_pc", "conv_b_pc", "dt_b_pc", "halo_b_pc",
               "A_sc", "D_sc", "ident", "diag_mask", "bq0_pc", "bk0_pc",
               "bv0_pc", "gate_b_bc", "dt_w_row", "ones_row"]


def dslice(th, row0, nrows, ncols_total, col0, ncols):
    return bass.AP(th, row0 * ncols_total + col0,
                   [[ncols_total, nrows], [1, ncols]])


class MK:
    def __init__(self, taps=(), stages=("produce", "scan", "out", "l0ffn", "l1")):
        self.stages = set(stages)
        self.taps = set(taps)
        self.nc = bass.Bass("TRN2", target_bir_lowering=False, num_devices=8)
        self.inp = {}
        self.out = {}
        self.sems = {}
        self.cnt = {}
        names = ["ten", "vec", "sca", "pol", "sp"]
        for q in ["dsp", "dact", "dpol"]:
            names += [f"{q}{i}" for i in range(NDMASEM)]
        for n in names:
            self.sems[n] = self.nc.alloc_semaphore("s_" + n)
            self.cnt[n] = 0
        self.dma_rr = {"dsp": 0, "dact": 0, "dpol": 0}

    def newprog(self):
        return Prog(self.nc, self.sems, self.cnt, mk=self)

    def half_reg(self, e, eng):
        if not hasattr(self, "_eregs"):
            self._eregs = {}
        r = self._eregs.get(eng)
        if r is None:
            r = e.to_reg(e.compute_val(self.half))
            self._eregs[eng] = r
        return r

    def din(self, name, shape, dtype=F32):
        h = self.nc.dram_tensor(name, list(shape), dtype, kind="ExternalInput")
        self.inp[name] = h
        return h

    def dout(self, name, shape, dtype=F32):
        h = self.nc.dram_tensor(name, list(shape), dtype, kind="ExternalOutput")
        self.out[name] = h
        return h

    def tap(self, pr, name, sb_ap, waits):
        if name not in self.taps:
            return None
        shape = list(sb_ap.shape)
        h = self.dout("tap_" + name, shape, sb_ap.dtype)
        return pr.dma("sp", h.ap(), sb_ap, waits=waits, tag="tap_" + name)

    def build(self):
        nc = self.nc
        for n, s in IN_SHAPES.items():
            self.din(n, s, F32)
        self.dout("y", (T, D), F32)

        sh = lambda name, shape, dt: nc.dram_tensor(name, list(shape), dt,
                                                    kind="Internal", addr_space="Shared")
        self.XCb = sh("XCb", (2, DI, T), BF16)
        self.DLb = sh("DLb", (2, DI, T), F32)
        self.Bb = sh("Bbuf", (2, DS, T), F32)
        self.Cb = sh("Cbuf", (2, DS, T), F32)
        self.Yb = sh("Ybuf", (2, 1024, 2 * T), F32)
        self.Qb = [sh(f"Qbuf{l}", (2, D, T), BF16) for l in range(2)]
        self.Kb = [sh(f"Kbuf{l}", (2, D, T), BF16) for l in range(2)]
        self.Vb = [sh(f"Vbuf{l}", (2, D, T), BF16) for l in range(2)]
        self.Ob = [sh(f"Obuf{l}", (D, 2 * T), BF16) for l in range(2)]

        self.RES = nc.alloc_sbuf_tensor("RES", [P, NT * D], F32)
        self.CONST = {}
        for n in CONST_NAMES:
            self.CONST[n] = nc.alloc_sbuf_tensor("c_" + n, list(IN_SHAPES[n]), F32)
        self.IDB = nc.alloc_sbuf_tensor("IDB", [P, P], BF16)
        self.SML = nc.alloc_sbuf_tensor("SML", [P, 1024], F32)
        AR = nc.alloc_sbuf_tensor("AR", [P, 32 * 1024], F32)
        self.AR = Arena(AR[:, :], 32 * 1024)
        self.PB = [nc.alloc_psum_tensor(f"pb{i}", [P, 512], F32) for i in range(7)]
        self.PBF = nc.alloc_psum_tensor("pbf", [P, 1024], BF16)

        pid = nc.partition_id()
        self.half = pid % 2
        self.conds = [1 - self.half, self.half]

        pr = self.newprog()
        ld = []
        for n in CONST_NAMES:
            ld.append(pr.dma("sp", self.CONST[n][:, :], self.inp[n].ap()))
        ld.append(pr.dma("pol", self.IDB[:, :], self.inp["ident"].ap()))
        xin = self.inp["x_in"].ap().rearrange("(nt p) d -> nt p d", p=P)
        res_ld = [pr.dma("sp", self.RES[:, t * D:(t + 1) * D], xin[t])
                  for t in range(NT)]
        self.ld_const = ld
        self.mamba_produce(pr, res_ld)
        pr.emit()
        nc.all_core_barrier()

        if "scan" in self.stages:
            pr = self.newprog()
            self.mamba_scan(pr)
            pr.emit()
            nc.all_core_barrier()

        pr = self.newprog()
        if "out" in self.stages:
            self.mamba_out(pr)
        if "l0ffn" in self.stages or "l0heads" in self.stages or "l0prod" in self.stages:
            self.attn_produce(pr, 0)
            pr.emit()
            nc.all_core_barrier()
            pr = self.newprog()
            if "l0ffn" in self.stages or "l0heads" in self.stages:
                self.attn_heads(pr, 0)
                pr.emit()
                nc.all_core_barrier()
                pr = self.newprog()
            if "l0ffn" in self.stages:
                self.attn_out(pr, 0)
                self.moe_l0(pr)
        if "l1" in self.stages:
            self.attn_produce(pr, 1)
            pr.emit()
            nc.all_core_barrier()
            pr = self.newprog()
            self.attn_heads(pr, 1)
            pr.emit()
            nc.all_core_barrier()
            pr = self.newprog()
            self.attn_out(pr, 1)
            self.ffn_l1(pr)

        if "stage" in self.taps:
            d_xc = self.dout("tap_xc", (DI, T), BF16)
            d_dl = self.dout("tap_dl", (DI, T), F32)
            d_b = self.dout("tap_b", (DS, T), F32)
            d_c = self.dout("tap_c", (DS, T), F32)
            d_y = (self.dout("tap_y", (1024, 2 * T), F32)
                   if "scan" in self.stages else None)
            bar0 = pr.barrier_all()
            pr.dma_hv("pol", lambda h: d_xc.ap(),
                      lambda h: dslice(self.XCb, h * DI, DI, T, 0, T),
                      waits=bar0, tag="txc")
            pr.dma_hv("pol", lambda h: d_dl.ap(),
                      lambda h: dslice(self.DLb, h * DI, DI, T, 0, T), tag="tdl")
            pr.dma_hv("pol", lambda h: d_b.ap(),
                      lambda h: dslice(self.Bb, h * DS, DS, T, 0, T), tag="tb")
            pr.dma_hv("pol", lambda h: d_c.ap(),
                      lambda h: dslice(self.Cb, h * DS, DS, T, 0, T), tag="tc")
            if "scan" in self.stages:
                pr.dma_hv("pol", lambda h: d_y.ap(),
                          lambda h: dslice(self.Yb, h * 1024, 1024, 2 * T, 0, 2 * T),
                          tag="ty")

        fin = pr.barrier_all()
        yo = self.out["y"].ap().rearrange("(nt p) d -> nt p d", p=P)
        st = [pr.dma("sp", yo[t], self.RES[:, t * D:(t + 1) * D], waits=fin)
              for t in range(NT)]
        pr.op("sp", lambda e: None, waits=st, tag="final")
        pr.emit()
        return nc

    def ln_one(self, pr, x_ap, out_ap, tmp, st, wb=None, waits=(), rows=P,
               width=D, war=()):
        mean, var, inv, sumsq, sd = (st[:, 0:1], st[:, 1:2], st[:, 2:3],
                                     st[:, 3:4], st[:, 4:5])
        w_in = list(waits)
        t_sq = pr.op("sca", lambda e: e.activation(tmp, x_ap, ACT.Square,
                     accum_out=sumsq), waits=w_in + list(war), tag="ln_sq")
        pr.op("vec", lambda e: e.reduce_sum(mean, x_ap, axis=AX.X),
              waits=w_in, tag="ln_sum")
        pr.op("vec", lambda e: e.tensor_scalar(mean, mean, 1.0 / width, None,
              ALU.mult), tag="ln_mean")
        pr.op("vec", lambda e: e.tensor_tensor(sd, mean, mean, ALU.mult), tag="ln_m2")
        pr.op("vec", lambda e: e.tensor_scalar(var, sumsq, 1.0 / width, None,
              ALU.mult), waits=[t_sq], tag="ln_v0")
        t_var = pr.op("vec", lambda e: e.tensor_tensor(var, var, sd, ALU.subtract),
                      tag="ln_var")
        t_sd = pr.op("sca", lambda e: e.activation(sd, var, ACT.Sqrt,
                     bias=self.CONST["eps_pc"][0:var.shape[0], 0:1]),
                     waits=[t_var], tag="ln_sd")
        t_inv = pr.op("vec", lambda e: e.reciprocal(inv, sd), waits=[t_sd], tag="ln_inv")
        if wb is None:
            tk = pr.op("vec", lambda e: e.tensor_scalar(out_ap, x_ap, mean, inv,
                       ALU.subtract, ALU.mult), waits=list(war), tag="ln_app")
        else:
            w_bc, b_bc = wb
            pr.op("vec", lambda e: e.tensor_scalar(tmp, x_ap, mean, inv,
                  ALU.subtract, ALU.mult), waits=[t_sq] + list(war), tag="ln_a1")
            pr.op("vec", lambda e: e.tensor_tensor(tmp, tmp, w_bc, ALU.mult),
                  tag="ln_a2")
            tk = pr.op("vec", lambda e: e.tensor_tensor(out_ap, tmp, b_bc, ALU.add),
                       waits=list(war), tag="ln_a3")
        return tk

    def mamba_produce(self, pr, res_ld):
        nc = self.nc
        C = self.CONST
        A = self.AR
        A.reset()
        PB = self.PB

        ZFM = A.take(8 * T, BF16)
        LNT = A.take(D)
        LNSQ = A.take(D)
        XH = A.take(16 + T)
        REG1 = A.take(3 * T)
        ZCH = REG1[:, 0:T]
        SZ = REG1[:, T:2 * T]
        CA = REG1[:, 2 * T:3 * T]
        DLT = ZCH
        PRJ = SZ
        XCC = [A.take(T, BF16), A.take(T, BF16)]
        WIN = [A.take(2 * D, BF16), A.take(2 * D, BF16)]
        XPJ = A.take(16 * 33 + 16, BF16)
        HFM = A.take(8 * 16, BF16)
        HX = LNT

        half = self.half
        ident = C["ident"][:, :]

        zfm_ev = []
        for t in range(NT):
            tk = self.ln_one(pr, self.RES[:, t * D:(t + 1) * D], LNT, LNSQ,
                             self.SML[:, 8 * t:8 * t + 8],
                             waits=[res_ld[t]] + self.ld_const[:2],
                             war=[zfm_ev[-1]] if zfm_ev else [])
            for c in range(8):
                pb = PB[5 + (c % 2)]
                t_tr = pr.op("ten", lambda e, cc=c, b=pb: e.transpose(
                    b[:, 0:128], LNT[:, cc * 128:(cc + 1) * 128], ident),
                    waits=[tk, zfm_ev[-2] if len(zfm_ev) >= 2 else None], tag="zt")
                t_ev = pr.op("sca", lambda e, b=pb, cc=c, tt=t: e.copy(
                    ZFM[:, cc * T + tt * P: cc * T + (tt + 1) * P], b[:, 0:128]),
                    waits=[t_tr], tag="zte")
                zfm_ev.append(t_ev)
        zfm_done = zfm_ev[-1]

        t_hx = pr.dma("sp", HX[0:16, :], self.inp["x_halo"].ap(),
                      waits=[zfm_ev[-1]], tag="halo_ld")
        st = self.SML[0:16, 8 * NT:8 * NT + 8]
        t_hln = self.ln_one(pr, HX[0:16, :], HX[0:16, :], LNSQ[0:16, :], st,
                            waits=[t_hx], rows=16, war=[zfm_done])
        halo_fm = []
        for c in range(8):
            pb = PB[5 + c % 2]
            t_tr = pr.op("ten", lambda e, cc=c, b=pb: e.transpose(
                b[:, 0:16], HX[0:16, cc * 128:(cc + 1) * 128], ident[0:16, 0:16]),
                waits=[t_hln, zfm_ev[-1], halo_fm[-1] if halo_fm else None], tag="htr")
            t_ev = pr.op("sca", lambda e, b=pb, cc=c: e.copy(
                HFM[:, cc * 16:(cc + 1) * 16], b[:, 0:16]), waits=[t_tr], tag="htr_ev")
            halo_fm.append(t_ev)

        xp = self.inp["xproj_T"].ap().rearrange("(c p) n -> c p n", p=P)
        xpj_toks = [pr.dma("pol", XPJ[:, cc * 33:(cc + 1) * 33], xp[cc], tag="xpj")
                    for cc in range(16)]

        win_ap = self.inp["w_in_T"].ap().rearrange("(kc p) n -> p kc n", p=P)

        bank_tok = {4: zfm_ev[-1], 5: zfm_ev[-1], 6: halo_fm[-1], 7: None}
        prev_use = [None, None]
        xc_war = [None, None]
        xh_war = None
        sz_war = None
        xc_st = []
        for co in range(16):
            wbuf = WIN[co % 2]
            t_w1 = pr.dma("pol", wbuf[:, 0:1024].rearrange("p (kc n) -> p kc n", n=128),
                          win_ap[:, :, co * 128:(co + 1) * 128],
                          waits=[prev_use[co % 2]], tag="w_ld_x")
            t_w2 = pr.dma("pol", wbuf[:, 1024:2048].rearrange("p (kc n) -> p kc n", n=128),
                          win_ap[:, :, (16 + co) * 128:(17 + co) * 128], tag="w_ld_z")

            mm = None
            for kc in range(8):
                mm = pr.op("ten", lambda e, k=kc, wb=wbuf: e.matmul(
                    PB[6][:, 0:16], wb[:, k * 128:(k + 1) * 128],
                    HFM[:, k * 16:(k + 1) * 16], start=(k == 0), stop=(k == 7)),
                    waits=([t_w1, halo_fm[-1], bank_tok[6], xh_war] if kc == 0
                           else []), tag="hmm")
            t_he = pr.op("sca", lambda e, c_=co: e.activation(
                XH[:, 0:16], PB[6][:, 0:16], ACT.Identity,
                bias=C["halo_b_pc"][:, c_:c_ + 1]), waits=[mm], tag="hev")
            bank_tok[6] = t_he

            last_ev = {0: None, 1: None}
            for which, woff in ((0, 0), (1, 1024)):
                dst = XH if which == 0 else ZCH
                for tt in range(4):
                    bk = 4 + tt % 2
                    mm = None
                    for kc in range(8):
                        mm = pr.op("ten", lambda e, k=kc, wb=wbuf, o=woff, t_=tt, b=bk:
                                   e.matmul(PB[b][:, :],
                                            wb[:, o + k * 128:o + (k + 1) * 128],
                                            ZFM[:, k * T + t_ * 512:k * T + (t_ + 1) * 512],
                                            start=(k == 0), stop=(k == 7)),
                                   waits=([t_w1 if which == 0 else t_w2, zfm_done,
                                          bank_tok[bk], xh_war, sz_war] if kc == 0
                                          else []), tag="xzmm")
                    off = (16 + tt * 512) if which == 0 else tt * 512
                    t_ev = pr.op("sca", lambda e, d=dst, o=off, c_=co, wh=which, b=bk:
                                 e.activation(d[:, o:o + 512], PB[b][:, :], ACT.Identity,
                                 bias=C["in_b_pc"][:, c_ + 16 * wh:c_ + 16 * wh + 1]),
                                 waits=[mm], tag="xzev")
                    bank_tok[bk] = t_ev
                    last_ev[which] = t_ev
            t_xh, t_z = last_ev[0], last_ev[1]
            prev_use[co % 2] = t_z

            t_c = pr.op("vec", lambda e, c_=co: e.tensor_scalar(
                CA, XH[:, 13:13 + T], C["conv_w_pc"][:, 4 * c_:4 * c_ + 1], None,
                ALU.mult), waits=[t_xh, t_he], tag="cv0")
            for k in range(1, 4):
                t_c = pr.op("vec", lambda e, kk=k, c_=co:
                            e.scalar_tensor_tensor(CA, XH[:, 13 + kk:13 + kk + T],
                            C["conv_w_pc"][:, 4 * c_ + kk:4 * c_ + kk + 1], CA,
                            ALU.mult, ALU.add), tag=f"cv{k}")
            t_c = pr.op("vec", lambda e, c_=co: e.tensor_scalar(
                CA, CA, C["conv_b_pc"][:, c_:c_ + 1], None, ALU.add), tag="cvb")
            t_sz = pr.op("sca", lambda e: e.activation(
                SZ, ZCH, ACT.Silu), waits=[t_z, sz_war], tag="silu")
            t_xc = pr.op("vec", lambda e, o_=XCC[co % 2]:
                         e.tensor_tensor(o_, CA, SZ, ALU.mult),
                         waits=[t_c, t_sz] + list(xc_war[co % 2] or ()), tag="xcm")
            xh_war = t_xc
            sz_war = t_xc

            t_st = pr.dma_hv("pol",
                lambda h, c_=co: dslice(self.XCb, h * DI + c_ * P, P, T, 0, T),
                lambda h, c_=co: XCC[c_ % 2],
                waits=[t_xc], tag="xcst")
            xc_st.append(t_st)

            pj = None
            for tt in range(4):
                pj = pr.op("ten", lambda e, c_=co, t_=tt: e.matmul(
                    PB[t_][0:33, :], XPJ[:, c_ * 33:(c_ + 1) * 33],
                    XCC[c_ % 2][:, t_ * 512:(t_ + 1) * 512],
                    start=(c_ == 0), stop=(c_ == 15)),
                    waits=[t_xc, xpj_toks[co]], tag="pjmm")
            xc_war[co % 2] = (pj, t_st)

        prj_toks = []
        for tt in range(4):
            tk = pr.op("vec", lambda e, t_=tt: e.tensor_copy(
                PRJ[0:33, t_ * 512:(t_ + 1) * 512], PB[t_][0:33, :]),
                waits=[("ten", self.cnt["ten"])], tag="pjev")
            prj_toks.append(tk)
        tB = pr.dma_hv("pol", lambda h: dslice(self.Bb, h * DS, DS, T, 0, T),
                       lambda h: PRJ[1:17, :], waits=prj_toks, tag="Bst")
        tC = pr.dma_hv("pol", lambda h: dslice(self.Cb, h * DS, DS, T, 0, T),
                       lambda h: PRJ[17:33, :], waits=prj_toks, tag="Cst")

        bank_tok2 = {4: bank_tok[4], 5: bank_tok[5]}
        dst_tok = None
        for co in range(16):
            dts = []
            for tt in range(4):
                bk = 4 + tt % 2
                t_mm = pr.op("ten", lambda e, c_=co, t_=tt, b=bk: e.matmul(
                    PB[b][:, :], C["dt_w_row"][0:1, c_ * 128:(c_ + 1) * 128],
                    PRJ[0:1, t_ * 512:(t_ + 1) * 512], start=True, stop=True),
                    waits=prj_toks + [bank_tok2[bk]], tag="dtmm")
                t_e = pr.op("sca", lambda e, c_=co, t_=tt, b=bk:
                            e.activation(DLT[:, t_ * 512:(t_ + 1) * 512], PB[b][:, :],
                            ACT.Exp, bias=C["dt_b_pc"][:, c_:c_ + 1]),
                            waits=[t_mm, dst_tok], tag="dtex")
                bank_tok2[bk] = t_e
                dts.append(t_e)
            t_p1 = pr.op("vec", lambda e: e.tensor_scalar(
                DLT, DLT, 1.0, None, ALU.add), waits=dts, tag="dtp1")
            dst_tok = pr.dma_hv("pol",
                lambda h, c_=co: dslice(self.DLb, h * DI + c_ * P, P, T, 0, T),
                lambda h: DLT, waits=[t_p1], tag="dlst")

        self.tap(pr, "zfm", ZFM, [zfm_done])
        pr.barrier_all()

    def mamba_scan(self, pr):
        C = self.CONST
        A = self.AR
        A.reset()
        half = self.half
        TC = 1024

        DLs = A.take(8 * TC)
        XCs = A.take(8 * TC, BF16)
        Ws = A.take(8 * TC, BF16)
        Ya = A.take(8 * TC, BF16)
        BR = [A.take(TC, BF16), A.take(TC, BF16)]
        CR = [A.take(TC, BF16), A.take(TC, BF16)]
        AT = [A.take(TC), A.take(TC)]
        UT = [A.take(TC, BF16), A.take(TC, BF16)]
        HT = [A.take(TC, BF16), A.take(TC, BF16)]
        GT = [A.take(TC, BF16), A.take(TC, BF16)]
        YD = [A.take(TC), A.take(TC)]
        HC = self.SML[:, 0:128]

        dsl = lambda i: DLs[:, i * TC:(i + 1) * TC]
        xsl = lambda i: XCs[:, i * TC:(i + 1) * TC]
        wsl = lambda i: Ws[:, i * TC:(i + 1) * TC]
        ysl = lambda i: Ya[:, i * TC:(i + 1) * TC]

        bar = pr.op("pol", lambda e: e.nop(), tag="postbar")
        t_hc0 = pr.op("vec", lambda e: e.memset(HC, 0.0), tag="hc0")
        prev_store = [None, None]
        prev_scan = [None, None]
        prev_g = [None, None]
        prev_u_par = {0: None, 1: None}
        prev_g_par = {0: None, 1: None}
        yd_tok = [None] * 8
        ld_d_prev = [bar] * 8
        ld_x_prev = [None] * 8
        for tc in range(4):
            hseg = tc // 2
            coff = (tc % 2) * TC
            ld_d, ld_x = [], []
            for i in range(8):
                ld_d.append(pr.dma_hv("pol",
                    lambda h, ii=i: dsl(ii),
                    lambda h, ii=i, hs=hseg, cf=coff: dslice(
                        self.DLb, hs * DI + h * 1024 + ii * P, P, T, cf, TC),
                    waits=[ld_d_prev[i]], tag="dld"))
                ld_x.append(pr.dma_hv("pol",
                    lambda h, ii=i: xsl(ii),
                    lambda h, ii=i, hs=hseg, cf=coff: dslice(
                        self.XCb, hs * DI + h * 1024 + ii * P, P, T, cf, TC),
                    waits=[ld_x_prev[i]], tag="xld"))
            w_toks = []
            for i in range(8):
                w_toks.append(pr.op("vec", lambda e, ii=i: e.tensor_tensor(
                    wsl(ii), dsl(ii), xsl(ii), ALU.mult),
                    waits=[ld_d[i], ld_x[i], prev_store[i % 2]], tag="wmul"))
            y0 = [pr.op("pol", lambda e, ii=i: e.memset(ysl(ii), 0.0),
                        waits=[yd_tok[i]], tag="y0")
                  for i in range(8)]
            j = 0
            for s in range(DS):
                sp = s % 2
                t_br = pr.dma("pol", BR[sp],
                              dslice(self.Bb, hseg * DS + s, 1, T, coff, TC)
                              .to_broadcast((P, TC)),
                              waits=[prev_u_par[sp]], tag="brep")
                t_cr = pr.dma("pol", CR[sp],
                              dslice(self.Cb, hseg * DS + s, 1, T, coff, TC)
                              .to_broadcast((P, TC)),
                              waits=[prev_g_par[sp]], tag="crep")
                for i in range(8):
                    p_ = j % 2
                    col = i * DS + s
                    t_a = pr.op("sca", lambda e, ii=i, cc=col, pp=p_: e.activation(
                        AT[pp], dsl(ii), ACT.Exp, scale=C["A_sc"][:, cc:cc + 1]),
                        waits=[ld_d[i], prev_scan[p_]], tag="aexp")
                    t_u = pr.op("vec", lambda e, ii=i, pp=p_, ss=sp: e.tensor_tensor(
                        UT[pp], wsl(ii), BR[ss], ALU.mult),
                        waits=[w_toks[i], t_br], tag="umul")
                    t_h = pr.op("vec", lambda e, cc=col, pp=p_: e.tensor_tensor_scan(
                        HT[pp], AT[pp], UT[pp], HC[:, cc:cc + 1], ALU.mult, ALU.add),
                        waits=[t_a, t_u, prev_g[p_], t_hc0], tag="scan")
                    prev_scan[p_] = t_h
                    pr.op("vec", lambda e, cc=col, pp=p_: e.tensor_copy(
                        HC[:, cc:cc + 1], HT[pp][:, TC - 1:TC]), tag="carry")
                    t_g = pr.op("pol", lambda e, pp=p_, ss=sp: e.tensor_tensor(
                        GT[pp], HT[pp], CR[ss], ALU.mult),
                        waits=[t_h, t_cr, y0[i]], tag="gmul")
                    t_y = pr.op("pol", lambda e, ii=i, pp=p_: e.tensor_tensor(
                        ysl(ii), ysl(ii), GT[pp], ALU.add), tag="yacc")
                    y0[i] = t_y
                    prev_g[p_] = t_g
                    prev_u_par[sp] = t_u
                    prev_g_par[sp] = t_g
                    j += 1
            for i in range(8):
                t_yd = pr.op("vec", lambda e, ii=i: e.scalar_tensor_tensor(
                    YD[ii % 2], xsl(ii), C["D_sc"][:, ii:ii + 1], ysl(ii),
                    ALU.mult, ALU.add),
                    waits=[y0[i], ld_x[i], prev_store[i % 2]], tag="ytot")
                yd_tok[i] = t_yd
                prev_store[i % 2] = pr.dma_hv("pol",
                    lambda h, ii=i, t_=tc: dslice(
                        self.Yb, h * 1024 + ii * P, P, 2 * T, t_ * TC, TC),
                    lambda h, ii=i: YD[ii % 2], waits=[t_yd], tag=f"yst{tc}_{i}")
                ld_d_prev[i] = prev_store[i % 2]
                ld_x_prev[i] = prev_store[i % 2]
        pr.barrier_all()

    def mamba_out(self, pr):
        C = self.CONST
        A = self.AR
        A.reset()
        half = self.half
        YF = A.take(16 * T, BF16)
        WOB = A.take(16 * D, BF16)

        bar = pr.op("pol", lambda e: e.nop(), tag="postbar")
        ybuf = lambda kc: YF[:, kc * T:(kc + 1) * T]
        ld = []
        for kc in range(16):
            owner = kc // 8
            row = (kc % 8) * P
            ld.append(pr.dma_hv("pol",
                lambda h, k=kc: ybuf(k),
                lambda h, o=owner, r=row: dslice(
                    self.Yb, o * 1024 + r, P, 2 * T, h * T, T),
                waits=[bar], tag="yld"))
        ow = self.inp["out_w_T"].ap().rearrange("(kc p) n -> kc p n", p=P)
        lw = [pr.dma("pol", WOB[:, kc * D:(kc + 1) * D], ow[kc], tag="owld")
              for kc in range(16)]

        bank_tok = {0: None, 1: None}
        for tt in range(NT):
            for dc in range(2):
                mm = None
                for kc in range(16):
                    mm = pr.op("ten", lambda e, k=kc, t_=tt, d_=dc: e.matmul(
                        self.PB[d_][:, :], ybuf(k)[:, t_ * P:(t_ + 1) * P],
                        WOB[:, k * D + d_ * 512:k * D + (d_ + 1) * 512],
                        start=(k == 0), stop=(k == 15)),
                        waits=(ld + lw + [bank_tok[dc]] if kc == 0 else []),
                        tag="owmm")
                t_add = pr.op("vec", lambda e, t_=tt, d_=dc: e.tensor_tensor(
                    self.RES[:, t_ * D + d_ * 512:t_ * D + (d_ + 1) * 512],
                    self.RES[:, t_ * D + d_ * 512:t_ * D + (d_ + 1) * 512],
                    self.PB[d_][:, :], ALU.add), waits=[mm], tag="radd")
                bank_tok[dc] = t_add
        pr.barrier_all()
        return



    def make_fm(self, pr, zfm, xrf=None, lnt=None, lnsq=None, xrt=None,
                rope_cs=None, rope_toks=(), war=()):
        C = self.CONST
        PB = self.PB
        ident = C["ident"][:, :]
        evs = []
        last_rope = None
        for t in range(NT):
            tk = self.ln_one(pr, self.RES[:, t * D:(t + 1) * D], lnt, lnsq,
                             self.SML[:, 8 * t:8 * t + 8],
                             war=[evs[-1] if evs else None] + list(war))
            srcs = [(zfm, lnt, tk)]
            if xrf is not None:
                cos, sin = rope_cs
                cosb = cos[:, t * 32:(t + 1) * 32].unsqueeze(1).unsqueeze(3)\
                    .broadcast_to((P, 16, 32, 1))
                sinb = sin[:, t * 32:(t + 1) * 32].unsqueeze(1).unsqueeze(3)\
                    .broadcast_to((P, 16, 32, 1))
                zr = lnt.rearrange("p (h i two) -> p h i two", two=2, i=32)
                xr = xrt.rearrange("p (h i two) -> p h i two", two=2, i=32)
                z_e, z_o = zr[:, :, :, 0:1], zr[:, :, :, 1:2]
                x_e, x_o = xr[:, :, :, 0:1], xr[:, :, :, 1:2]
                tmp = self.SML[:, 256:768].rearrange(
                    "p (h i one) -> p h i one", h=16, i=32)
                pr.op("vec", lambda e, a=tmp, b=z_o, c=sinb: e.tensor_tensor(
                      a, b, c, ALU.mult),
                      waits=[tk] + list(rope_toks), tag="rp1")
                pr.op("vec", lambda e, a=x_e, b=z_e, c=cosb: e.tensor_tensor(
                      a, b, c, ALU.mult), waits=[last_rope], tag="rp2")
                pr.op("vec", lambda e, a=x_e, b=tmp: e.tensor_tensor(
                      a, a, b, ALU.subtract), tag="rp3")
                pr.op("vec", lambda e, a=tmp, b=z_o, c=cosb: e.tensor_tensor(
                      a, b, c, ALU.mult), tag="rp4")
                pr.op("vec", lambda e, a=x_o, b=z_e, c=sinb: e.tensor_tensor(
                      a, b, c, ALU.mult), tag="rp5")
                rt = pr.op("vec", lambda e, a=x_o, b=tmp: e.tensor_tensor(
                      a, a, b, ALU.add), tag="rp6")
                srcs.append((xrf, xrt, rt))
            for dst, srcbuf, tok in srcs:
                for c in range(8):
                    pb = PB[5 + (c % 2)]
                    t_tr = pr.op("ten", lambda e, cc=c, b=pb, s=srcbuf: e.transpose(
                        b[:, 0:128], s[:, cc * 128:(cc + 1) * 128], ident),
                        waits=[tok, evs[-2] if len(evs) >= 2 else None], tag="fmt")
                    t_ev = pr.op("sca", lambda e, b=pb, cc=c, tt=t, d=dst: e.copy(
                        d[:, cc * T + tt * P: cc * T + (tt + 1) * P], b[:, 0:128]),
                        waits=[t_tr], tag="fme")
                    evs.append(t_ev)
                if dst is xrf:
                    last_rope = evs[-1]
        return evs[-1]

    def attn_produce(self, pr, l):
        C = self.CONST
        PB = self.PB
        A = self.AR
        A.reset()
        ZFM = A.take(8 * T, BF16)
        XRF = A.take(8 * T, BF16) if l == 0 else None
        LNT = A.take(D)
        LNSQ = A.take(D)
        XRT = A.take(D) if l == 0 else None
        WB = [A.take(8 * 128, BF16), A.take(8 * 128, BF16)]
        EV = A.take(T, BF16)
        bar = pr.op("pol", lambda e: e.nop(), tag="postbar")
        if l == 0:
            RCOS = A.take(NT * 32)
            RSIN = A.take(NT * 32)
            t_rc = pr.dma("sp", RCOS, self.inp["rope_cos"].ap(), waits=[bar],
                          tag="rcld")
            t_rs = pr.dma("sp", RSIN, self.inp["rope_sin"].ap(), tag="rsld")
            war0 = [bar, t_rc, t_rs]
            rope_cs = (RCOS, RSIN)
        else:
            war0 = [bar]
            rope_cs = None
        fm_tok = self.make_fm(pr, ZFM, xrf=XRF, lnt=LNT, lnsq=LNSQ, xrt=XRT,
                              rope_cs=rope_cs, rope_toks=war0, war=war0)
        if l == 0:
            plan = [("wq0_T", self.Qb[0], "bq0_pc", XRF),
                    ("wk0_T", self.Kb[0], "bk0_pc", XRF),
                    ("wv0_T", self.Vb[0], "bv0_pc", ZFM)]
        else:
            plan = [("wq1", self.Qb[1], None, ZFM),
                    ("wk1", self.Kb[1], None, ZFM),
                    ("wv1", self.Vb[1], None, ZFM)]
        wprev = [None, None]
        ev_war = None
        bank_tok = {0: None, 1: None}
        for w_name, stage_buf, bias_col, src_fm in plan:
            w_ap = self.inp[w_name].ap().rearrange("(kc p) n -> p kc n", p=P)
            for co in range(8):
                wb = WB[co % 2]
                t_w = pr.dma("pol", wb.rearrange("p (kc n) -> p kc n", n=128),
                             w_ap[:, :, co * 128:(co + 1) * 128],
                             waits=[wprev[co % 2]], tag="qkv_wld")
                evs = []
                mm = None
                for tt in range(4):
                    bk = tt % 2
                    for kc in range(8):
                        mm = pr.op("ten", lambda e, k=kc, w=wb, t_=tt, b=bk, s=src_fm:
                                   e.matmul(PB[b][:, :],
                                            w[:, k * 128:(k + 1) * 128],
                                            s[:, k * T + t_ * 512:k * T + (t_ + 1) * 512],
                                            start=(k == 0), stop=(k == 7)),
                                   waits=([t_w, fm_tok, bank_tok[bk]] if kc == 0
                                          else []), tag="qkv_mm")
                    bias = C[bias_col][:, co:co + 1] if bias_col else 0.0
                    t_ev = pr.op("sca", lambda e, b=bk, t_=tt, bi=bias: e.activation(
                        EV[:, t_ * 512:(t_ + 1) * 512], PB[b][:, :], ACT.Identity,
                        bias=bi), waits=[mm, ev_war], tag="qkv_ev")
                    bank_tok[bk] = t_ev
                    evs.append(t_ev)
                wprev[co % 2] = mm
                t_st = pr.dma_hv("pol",
                    lambda h, c_=co, sb=stage_buf: dslice(sb, h * D + c_ * P, P, T, 0, T),
                    lambda h: EV[:, 0:T], waits=evs, tag="qkv_st")
                ev_war = t_st
        pr.barrier_all()

    def attn_heads(self, pr, l):
        C = self.CONST
        PB, PBF = self.PB, self.PBF
        A = self.AR
        A.reset()
        Q2 = A.take(4096, BF16)
        K2 = A.take(4096, BF16)
        QH = A.take(4096, BF16)
        VF = A.take(4096, BF16)
        VT2 = A.take(32 * 65 + 15, BF16)
        ON = A.take(4096, BF16)
        PT = [A.take(512, BF16) for _ in range(3)]
        OE = A.take(512, BF16)
        MSKT = A.take(4 * 512)
        CQ = self.SML[:, 128:160]
        MA = self.SML[:, 160:161]
        MT = self.SML[:, 161:162]
        CT = self.SML[0:32, 164:292]
        RR = self.SML[0:1, 300:812]
        Qb, Kb, Vb, Ob = self.Qb[l], self.Kb[l], self.Vb[l], self.Ob[l]

        bar = pr.op("pol", lambda e: e.nop(), tag="postbar")
        t_msk = pr.dma("sp", MSKT, self.inp["maskT"].ap(), waits=[bar], tag="mskld")
        t_o1 = pr.op("vec", lambda e: e.memset(K2[64:65, :], 1.0), waits=[bar],
                     tag="ones1")
        t_o2 = pr.op("vec", lambda e: e.memset(
            VT2[:, 0:2080].rearrange("p (kt c) -> p kt c", c=65)[:, :, 64:65], 1.0),
            tag="ones2")
        w_k2, w_qh, w_vf = [t_o1], [None], [None]
        w_q2, w_vt, w_on = [None], [t_o2], [None]
        p1bank = {0: None, 1: None}
        p2bank = {2: None, 3: None}
        exp_war = [None, None, None]
        av_start_war = []
        pb5_war = None
        ct_tok = None
        ctr_tok = None
        for hl in range(8):
            tk2, tqh, tvf = [], [], []
            for seg in range(2):
                tk2.append(pr.dma_hv("pol",
                    lambda h, s=seg: K2[0:64, s * T:(s + 1) * T],
                    lambda h, s=seg, hh=hl: dslice(Kb, s * D + h * 512 + hh * 64, 64, T, 0, T),
                    waits=w_k2, tag="k2ld"))
                tqh.append(pr.dma_hv("pol",
                    lambda h, s=seg: QH[0:64, s * T:(s + 1) * T],
                    lambda h, s=seg, hh=hl: dslice(Qb, s * D + h * 512 + hh * 64, 64, T, 0, T),
                    waits=w_qh, tag="qhld"))
                tvf.append(pr.dma_hv("pol",
                    lambda h, s=seg: VF[0:64, s * T:(s + 1) * T],
                    lambda h, s=seg, hh=hl: dslice(Vb, s * D + h * 512 + hh * 64, 64, T, 0, T),
                    waits=w_vf, tag="vfld"))
            tq2 = pr.op("sca", lambda e: e.activation(
                Q2[0:64, :], QH[0:64, :], ACT.Identity, scale=0.125),
                waits=tqh + w_q2, tag="qscale")
            tvt = None
            for kt in range(32):
                t_tr = pr.op("ten", lambda e, k=kt: e.transpose(
                    PBF[:, 0:64], VF[0:64, k * 128:(k + 1) * 128],
                    self.IDB[0:64, 0:64]),
                    waits=(tvf + w_vt) if kt == 0 else [tvt], tag="vtr")
                tvt = pr.op("vec", lambda e, k=kt: e.tensor_copy(
                    VT2[:, k * 65:k * 65 + 64], PBF[:, 0:64]),
                    waits=[t_tr], tag="vte")

            negm_toks = []
            for qg in range(32):
                kext = (qg + 1) * 128
                nch = (kext + 511) // 512
                red_tok = None
                for kcH in range(nch):
                    n = min(512, kext - kcH * 512)
                    bk = kcH % 2
                    mm = pr.op("ten", lambda e, k=kcH, q_=qg, b=bk, nn=n:
                               e.matmul(PB[b][:, 0:nn],
                                        QH[0:64, q_ * 128:(q_ + 1) * 128],
                                        K2[0:64, k * 512:k * 512 + nn],
                                        start=True, stop=True),
                               waits=(tk2 + tqh + [p1bank[bk]]) if True else [],
                               tag="s_mm")
                    if (kcH + 1) * 512 >= kext:
                        dcol = qg * 128 - kcH * 512
                        mm = pr.op("vec", lambda e, b=bk, dc=dcol: e.tensor_tensor(
                            PB[b][:, dc:dc + 128], PB[b][:, dc:dc + 128],
                            C["diag_mask"][:, :], ALU.add),
                            waits=[mm], tag="s_mask")
                    dst = MA if kcH == 0 else MT
                    rt = pr.op("vec", lambda e, b=bk, d=dst, nn=n: e.reduce_max(
                        d, PB[b][:, 0:nn], axis=AX.X),
                        waits=[mm, red_tok], tag="s_max")
                    if kcH > 0:
                        rt = pr.op("vec", lambda e: e.tensor_max(MA, MA, MT),
                                   tag="s_maxacc")
                    p1bank[bk] = rt
                    red_tok = rt
                negm_toks.append(pr.op("vec", lambda e, q_=qg: e.tensor_scalar(
                    CQ[:, q_:q_ + 1], MA, -0.125, None, ALU.mult),
                    waits=[red_tok, ctr_tok], tag="negm"))
            ctr_tok = pr.op("ten", lambda e: e.transpose(
                PB[4][0:32, 0:128], CQ, C["ident"][:, :]),
                waits=negm_toks + av_start_war, tag="ctr")
            t_cte = pr.op("vec", lambda e: e.tensor_copy(
                CT, PB[4][0:32, 0:128]), waits=[ctr_tok], tag="cte")
            t_fl = pr.dma("pol", Q2[64:65, :].rearrange("p (a b) -> p a b", a=32),
                          CT.unsqueeze(1), waits=[t_cte, ct_tok, w_q2[0]],
                          tag="cflat")
            ct_tok = t_fl
            av_tok = None
            st_last = None
            t_exp = None
            ons = []
            for qc in range(8):
                nkt = (qc + 1) * 4
                pend = []
                for kt in range(nkt):
                    bk = 2 + kt % 2
                    mm = pr.op("ten", lambda e, k=kt, q_=qc, b=bk: e.matmul(
                        PB[b][:, :], K2[0:65, k * 128:(k + 1) * 128],
                        Q2[0:65, q_ * 512:(q_ + 1) * 512], start=True, stop=True),
                        waits=(tk2 + [tq2, t_fl, p2bank[bk]]) if kt < 2 else
                              [p2bank[bk]], tag="st_mm")
                    st_last = mm
                    if kt >= qc * 4:
                        j = kt - qc * 4
                        mm = pr.op("vec", lambda e, b=bk, jj=j: e.tensor_tensor(
                            PB[b][:, :], PB[b][:, :],
                            MSKT[:, jj * 512:(jj + 1) * 512], ALU.add),
                            waits=[mm, t_msk], tag="st_mask")
                    pp = kt % 3
                    t_exp = pr.op("sca", lambda e, b=bk, p_=pp: e.activation(
                        PT[p_][:, :], PB[b][:, :], ACT.Exp),
                        waits=[mm, exp_war[pp]], tag="st_exp")
                    p2bank[bk] = t_exp
                    pend.append((kt, t_exp, pp))
                    if len(pend) >= 2:
                        k0, e0, pq = pend.pop(0)
                        av_tok = pr.op("ten", lambda e, k=k0, p_=pq,
                                       la=(k0 == nkt - 1): e.matmul(
                            PB[4][0:65, :], VT2[:, k * 65:(k + 1) * 65],
                            PT[p_][:, :], start=(k == 0), stop=la),
                            waits=[e0, tvt] + (av_start_war if k0 == 0 else []),
                            tag="av_mm")
                        exp_war[pq] = av_tok
                while pend:
                    k0, e0, pq = pend.pop(0)
                    av_tok = pr.op("ten", lambda e, k=k0, p_=pq,
                                   la=(k0 == nkt - 1): e.matmul(
                        PB[4][0:65, :], VT2[:, k * 65:(k + 1) * 65],
                        PT[p_][:, :], start=(k == 0), stop=la),
                        waits=[e0, tvt] + (av_start_war if k0 == 0 else []),
                        tag="av_mm")
                    exp_war[pq] = av_tok
                t_rr = pr.op("vec", lambda e: e.reciprocal(RR, PB[4][64:65, :]),
                             waits=[av_tok], tag="rrow")
                t_rb = pr.op("ten", lambda e: e.matmul(
                    PB[5][0:64, :], C["ones_row"][0:1, 0:64], RR,
                    start=True, stop=True), waits=[t_rr, pb5_war], tag="rbc")
                t_oe = pr.op("sca", lambda e: e.copy(OE[0:64, 0:512],
                             PB[4][0:64, :]), waits=[av_tok, t_rr], tag="o_ev")
                t_on = pr.op("vec", lambda e, q_=qc: e.tensor_tensor(
                    ON[0:64, q_ * 512:(q_ + 1) * 512], OE[0:64, 0:512],
                    PB[5][0:64, :], ALU.mult),
                    waits=[t_oe, t_rb] + w_on, tag="o_norm")
                ons.append(t_on)
                pb5_war = t_on
                av_start_war = [t_oe, t_rr]
            t_ost = pr.dma_hv("pol",
                lambda h, hh=hl: dslice(Ob, h * 512 + hh * 64, 64, 2 * T, 0, 2 * T),
                lambda h: ON[0:64, :], waits=ons, tag="onst")
            w_k2 = [st_last]
            w_qh = [st_last]
            w_q2 = [t_exp]
            w_vf = [av_tok]
            w_vt = [av_tok]
            w_on = [t_ost]
        pr.barrier_all()

    def attn_out(self, pr, l):
        C = self.CONST
        PB = self.PB
        A = self.AR
        A.reset()
        if l == 0:
            self.X1 = A.take(NT * D)
        OALL = A.take(8 * T, BF16)
        WOB = A.take(8 * 1024, BF16)
        Z0 = A.take(D)
        bar = pr.op("pol", lambda e: e.nop(), tag="postbar")
        cl = []
        if l == 0:
            BO = A.take(D)
            LW1 = A.take(D)
            LB1 = A.take(D)
            cl.append(pr.dma("sp", BO, self.inp["bo0_bc"].ap(), waits=[bar], tag="bold"))
            cl.append(pr.dma("sp", LW1, self.inp["ln1_w_bc"].ap(), tag="lw1"))
            cl.append(pr.dma("sp", LB1, self.inp["ln1_b_bc"].ap(), tag="lb1"))
        w_name = "wo0_T" if l == 0 else "wo1"
        Ob = self.Ob[l]
        ld = []
        for r in range(8):
            ld.append(pr.dma_hv("pol",
                lambda h, rr=r: OALL[:, rr * T:(rr + 1) * T],
                lambda h, rr=r: dslice(Ob, rr * P, P, 2 * T, h * T, T),
                waits=[bar], tag="oall"))
        w_ap = self.inp[w_name].ap().rearrange("(kc p) n -> kc p n", p=P)
        lw = [pr.dma("pol", WOB[:, kc * 1024:(kc + 1) * 1024], w_ap[kc],
                     tag="wold") for kc in range(8)]
        bank_tok = {0: None, 1: None}
        ln_war = None
        for tt in range(NT):
            outtoks = []
            for dc in range(2):
                mm = None
                for kc in range(8):
                    mm = pr.op("ten", lambda e, k=kc, t_=tt, d_=dc: e.matmul(
                        PB[d_][:, :], OALL[:, k * T + t_ * P:k * T + (t_ + 1) * P],
                        WOB[:, k * 1024 + d_ * 512:k * 1024 + (d_ + 1) * 512],
                        start=(k == 0), stop=(k == 7)),
                        waits=(ld + lw + [bank_tok[dc]] if kc == 0 else []),
                        tag="wo_mm")
                outtoks.append(mm)
            if l == 1:
                for dc in range(2):
                    t_add = pr.op("vec", lambda e, t_=tt, d_=dc: e.tensor_tensor(
                        self.RES[:, t_ * D + d_ * 512:t_ * D + (d_ + 1) * 512],
                        self.RES[:, t_ * D + d_ * 512:t_ * D + (d_ + 1) * 512],
                        PB[d_][:, :], ALU.add), waits=[outtoks[dc]], tag="res1")
                    bank_tok[dc] = t_add
            else:
                tz = self.ln_one(pr, self.RES[:, tt * D:(tt + 1) * D], Z0, Z0,
                                 self.SML[:, 8 * tt:8 * tt + 8],
                                 waits=cl, war=[ln_war])
                for dc in range(2):
                    t1 = pr.op("vec", lambda e, d_=dc: e.tensor_tensor(
                        Z0[:, d_ * 512:(d_ + 1) * 512],
                        Z0[:, d_ * 512:(d_ + 1) * 512], PB[d_][:, :], ALU.add),
                        waits=[outtoks[dc], tz], tag="z0a")
                    t2 = pr.op("vec", lambda e, d_=dc: e.tensor_tensor(
                        Z0[:, d_ * 512:(d_ + 1) * 512],
                        Z0[:, d_ * 512:(d_ + 1) * 512],
                        BO[:, d_ * 512:(d_ + 1) * 512], ALU.add), tag="boa")
                    bank_tok[dc] = t2
                ln_war = self.ln_one(
                    pr, Z0, self.X1[:, tt * D:(tt + 1) * D],
                    self.X1[:, tt * D:(tt + 1) * D],
                    self.SML[:, 264:272], wb=(LW1, LB1),
                    waits=[bank_tok[0], bank_tok[1]], war=[])
                bank_tok[0] = ln_war
                bank_tok[1] = ln_war
        pr.barrier_all()

    def moe_l0(self, pr):
        C = self.CONST
        PB = self.PB
        A = self.AR
        A.reset()
        A.off = NT * D
        X1 = self.X1
        X1FM = A.take(8 * 512, BF16)
        MACC = A.take(4 * 1024)
        EW = A.take(8 * 1024, BF16)
        GTW = A.take(64, BF16)
        EB = A.take(1024)
        W8 = A.take(NT * 8)
        W8T = A.take(4 * 128)
        LW2 = A.take(D)
        LB2 = A.take(D)
        TMP = A.take(D)
        G8 = self.SML[:, 0:8]
        T8 = self.SML[:, 8:16]
        M8 = self.SML[:, 16:24]
        SU = self.SML[:, 24:32]
        ident = C["ident"][:, :]

        bar = pr.op("pol", lambda e: e.nop(), tag="postbar")
        g_ap = self.inp["gate_T"].ap().rearrange("(kc p) n -> p kc n", p=P)
        t_gw = pr.dma("pol", GTW.rearrange("p (kc n) -> p kc n", n=8), g_ap,
                      waits=[bar], tag="gwld")
        t_eb = pr.dma("sp", EB[0:8, :], self.inp["eb"].ap(), waits=[bar], tag="ebld")
        t_l2 = [pr.dma("sp", LW2, self.inp["ln2_w_bc"].ap(), tag="lw2"),
                pr.dma("sp", LB2, self.inp["ln2_b_bc"].ap(), tag="lb2")]

        macc_war = [None] * 4
        res_done = []
        gexp_war = None
        ebank = {0: None, 1: None}
        ew_prev = None
        pb6_tok = None
        for half_t in range(4):
            t0 = half_t * 4
            fm_toks = []
            for ti in range(4):
                for c in range(8):
                    pb = PB[5 + (c % 2)]
                    t_tr = pr.op("ten", lambda e, cc=c, b=pb, t_=t0 + ti: e.transpose(
                        b[:, 0:128], X1[:, t_ * D + cc * 128:t_ * D + (cc + 1) * 128],
                        ident),
                        waits=[fm_toks[-2] if len(fm_toks) >= 2 else None,
                               ("vec", self.cnt["vec"]), ("sca", self.cnt["sca"]),
                               bar], tag="x1t")
                    t_ev = pr.op("sca", lambda e, b=pb, cc=c, ti_=ti: e.copy(
                        X1FM[:, cc * 512 + ti_ * P:cc * 512 + (ti_ + 1) * P],
                        b[:, 0:128]), waits=[t_tr], tag="x1e")
                    fm_toks.append(t_ev)
            fm_tok = fm_toks[-1]
            w8_toks = []
            gbank = None
            for ti in range(4):
                tt = t0 + ti
                mm = None
                for kc in range(8):
                    mm = pr.op("ten", lambda e, k=kc, ti_=ti: e.matmul(
                        PB[2][:, 0:8],
                        X1FM[:, k * 512 + ti_ * P:k * 512 + (ti_ + 1) * P],
                        GTW[:, k * 8:(k + 1) * 8], start=(k == 0), stop=(k == 7)),
                        waits=([t_gw, fm_tok, gbank] if kc == 0 else []),
                        tag="gmm")
                t_b = pr.op("vec", lambda e: e.tensor_tensor(
                    G8, PB[2][:, 0:8], C["gate_b_bc"][:, 0:8], ALU.add),
                    waits=[mm, gexp_war], tag="gbias")
                gbank = t_b
                t_m = pr.op("vec", lambda e: e.reduce_max(SU[:, 0:1], G8, axis=AX.X),
                            tag="gmax")
                t_nm = pr.op("vec", lambda e: e.tensor_scalar(
                    SU[:, 1:2], SU[:, 0:1], -1.0, None, ALU.mult), tag="gneg")
                t_e = pr.op("sca", lambda e: e.activation(
                    T8, G8, ACT.Exp, bias=SU[:, 1:2], accum_out=SU[:, 2:3]),
                    waits=[t_nm], tag="gexp")
                gexp_war = t_e
                t_r = pr.op("vec", lambda e: e.reciprocal(SU[:, 3:4], SU[:, 2:3]),
                            waits=[t_e], tag="grec")
                t_g = pr.op("vec", lambda e: e.tensor_scalar(
                    G8, T8, SU[:, 3:4], None, ALU.mult), tag="gnorm")
                t_m1 = pr.op("vec", lambda e: e.reduce_max(SU[:, 0:1], G8, axis=AX.X),
                             tag="gm1")
                pr.op("vec", lambda e: e.tensor_scalar(
                    M8, G8, SU[:, 0:1], None, ALU.is_lt), tag="gm2b")
                pr.op("vec", lambda e: e.tensor_tensor(T8, G8, M8, ALU.mult),
                      tag="gm2c")
                t_m2 = pr.op("vec", lambda e: e.reduce_max(SU[:, 1:2], T8,
                             axis=AX.X), tag="gm2")
                t_msk2 = pr.op("vec", lambda e: e.tensor_scalar(
                    M8, G8, SU[:, 1:2], None, ALU.is_ge), tag="gge")
                t_w8 = pr.op("vec", lambda e, tt_=tt: e.tensor_tensor(
                    W8[:, tt_ * 8:(tt_ + 1) * 8], G8, M8, ALU.mult), tag="gw8")
                w8_toks.append(t_w8)
            w8t = []
            for ti in range(4):
                tt = t0 + ti
                t_tr = pr.op("ten", lambda e, tt_=tt: e.transpose(
                    PB[3][0:8, 0:128], W8[:, tt_ * 8:(tt_ + 1) * 8], ident),
                    waits=[w8_toks[ti], w8t[-1] if w8t else None], tag="w8t")
                t_ev = pr.op("vec", lambda e, ti_=ti: e.tensor_copy(
                    W8T[0:8, ti_ * 128:(ti_ + 1) * 128], PB[3][0:8, 0:128]),
                    waits=[t_tr], tag="w8te")
                w8t.append(t_ev)
            for ti in range(4):
                for dc in range(2):
                    t_bm = pr.op("ten", lambda e, ti_=ti, d_=dc: e.matmul(
                        PB[6][:, :], W8T[0:8, ti_ * 128:(ti_ + 1) * 128],
                        EB[0:8, d_ * 512:(d_ + 1) * 512], start=True, stop=True),
                        waits=[w8t[ti], t_eb, fm_tok, macc_war[ti], pb6_tok],
                        tag="biasmm")
                    t_mc = pr.op("vec", lambda e, ti_=ti, d_=dc: e.tensor_copy(
                        MACC[:, ti_ * 1024 + d_ * 512:ti_ * 1024 + (d_ + 1) * 512],
                        PB[6][:, :]), waits=[t_bm, macc_war[ti]], tag="macc0")
                    macc_war[ti] = t_mc
                    pb6_tok = t_mc
            ew_ap = self.inp["ew_T"].ap()
            mm = None
            for e_ in range(8):
                t_ew = pr.dma("pol", EW.rearrange("p (kc n) -> p kc n", n=1024),
                              ew_ap[e_].rearrange("(kc p) n -> p kc n", p=P),
                              waits=[ew_prev], tag="ewld")
                for ti in range(4):
                    tt = t0 + ti
                    for dc in range(2):
                        bk = dc
                        for kc in range(8):
                            mm = pr.op("ten", lambda e, k=kc, ti_=ti, d_=dc, b=bk:
                                       e.matmul(PB[b][:, :],
                                                X1FM[:, k * 512 + ti_ * P:
                                                     k * 512 + (ti_ + 1) * P],
                                                EW[:, k * 1024 + d_ * 512:
                                                   k * 1024 + (d_ + 1) * 512],
                                                start=(k == 0), stop=(k == 7)),
                                       waits=([t_ew, fm_tok, ebank[bk]] if kc == 0
                                              else []), tag="emm")
                        t_acc = pr.op("vec", lambda e, ti_=ti, d_=dc, b=bk, ee=e_,
                                      tt_=tt: e.scalar_tensor_tensor(
                            MACC[:, ti_ * 1024 + d_ * 512:ti_ * 1024 + (d_ + 1) * 512],
                            PB[b][:, :], W8[:, tt_ * 8 + ee:tt_ * 8 + ee + 1],
                            MACC[:, ti_ * 1024 + d_ * 512:ti_ * 1024 + (d_ + 1) * 512],
                            ALU.mult, ALU.add),
                            waits=[mm, macc_war[ti]], tag="eacc")
                        ebank[bk] = t_acc
                        macc_war[ti] = t_acc
                ew_prev = mm
            for ti in range(4):
                tt = t0 + ti
                t_s = pr.op("vec", lambda e, ti_=ti, tt_=tt: e.tensor_tensor(
                    MACC[:, ti_ * 1024:(ti_ + 1) * 1024],
                    MACC[:, ti_ * 1024:(ti_ + 1) * 1024],
                    X1[:, tt_ * D:(tt_ + 1) * D], ALU.add),
                    waits=[macc_war[ti]], tag="x2sum")
                t_ln = self.ln_one(pr, MACC[:, ti * 1024:(ti + 1) * 1024], TMP,
                                   TMP, self.SML[:, 272:280], wb=(LW2, LB2),
                                   waits=t_l2 + [t_s],
                                   war=[res_done[-1] if res_done else None])
                t_res = pr.op("vec", lambda e, tt_=tt: e.tensor_tensor(
                    self.RES[:, tt_ * D:(tt_ + 1) * D],
                    self.RES[:, tt_ * D:(tt_ + 1) * D], TMP, ALU.add),
                    waits=[t_ln], tag="res2")
                res_done.append(t_res)
            macc_war = [res_done[-1]] * 4
        pr.barrier_all()

    def ffn_l1(self, pr):
        C = self.CONST
        PB = self.PB
        A = self.AR
        A.reset()
        ZFM = A.take(8 * T, BF16)
        GEF = A.take(8 * 1024, BF16)
        MACC = A.take(8 * 1024)
        W8 = A.take(NT * 8)
        WRT = A.take(64, BF16)
        W12 = [A.take(8 * 128, BF16), A.take(8 * 128, BF16)]
        W2F = A.take(8 * 1024, BF16)
        SG = [A.take(512, BF16), A.take(512, BF16)]
        LNT = A.take(D)
        LNSQ = A.take(D)
        G8 = self.SML[:, 0:8]
        T8 = self.SML[:, 8:16]
        M8 = self.SML[:, 16:24]
        SU = self.SML[:, 24:32]

        bar = pr.op("pol", lambda e: e.nop(), tag="postbar")
        fm_tok = self.make_fm(pr, ZFM, lnt=LNT, lnsq=LNSQ, war=[bar])
        wr_ap = self.inp["w_router"].ap().rearrange("(kc p) n -> p kc n", p=P)
        t_wr = pr.dma("pol", WRT.rearrange("p (kc n) -> p kc n", n=8), wr_ap,
                      waits=[bar], tag="wrld")
        w8_toks = []
        rbank = None
        rexp_war = None
        for tt in range(NT):
            mm = None
            for kc in range(8):
                mm = pr.op("ten", lambda e, k=kc, t_=tt: e.matmul(
                    PB[6][:, 0:8], ZFM[:, k * T + t_ * P:k * T + (t_ + 1) * P],
                    WRT[:, k * 8:(k + 1) * 8], start=(k == 0), stop=(k == 7)),
                    waits=([t_wr, fm_tok, rbank] if kc == 0 else []), tag="rmm")
            t_m = pr.op("vec", lambda e: e.reduce_max(SU[:, 0:1], PB[6][:, 0:8],
                        axis=AX.X), waits=[mm], tag="rmax")
            t_nm = pr.op("vec", lambda e: e.tensor_scalar(
                SU[:, 1:2], SU[:, 0:1], -1.0, None, ALU.mult), tag="rneg")
            t_e = pr.op("sca", lambda e: e.activation(
                T8, PB[6][:, 0:8], ACT.Exp, bias=SU[:, 1:2], accum_out=SU[:, 2:3]),
                waits=[t_nm, rexp_war], tag="rexp")
            rexp_war = t_e
            rbank = t_e
            t_r = pr.op("vec", lambda e: e.reciprocal(SU[:, 3:4], SU[:, 2:3]),
                        waits=[t_e], tag="rrec")
            t_g = pr.op("vec", lambda e: e.tensor_scalar(
                G8, T8, SU[:, 3:4], None, ALU.mult), tag="rnorm")
            t_m1 = pr.op("vec", lambda e: e.reduce_max(SU[:, 0:1], G8, axis=AX.X),
                         tag="rm1")
            pr.op("vec", lambda e: e.tensor_scalar(
                M8, G8, SU[:, 0:1], None, ALU.is_lt), tag="rm2b")
            pr.op("vec", lambda e: e.tensor_tensor(T8, G8, M8, ALU.mult),
                  tag="rm2c")
            t_m2 = pr.op("vec", lambda e: e.reduce_max(SU[:, 1:2], T8,
                         axis=AX.X), tag="rm2")
            t_sum = pr.op("vec", lambda e: e.tensor_tensor(
                SU[:, 4:5], SU[:, 0:1], SU[:, 1:2], ALU.add), tag="rsum12")
            t_rc = pr.op("vec", lambda e: e.reciprocal(SU[:, 5:6], SU[:, 4:5]),
                         tag="rrec2")
            t_msk2 = pr.op("vec", lambda e: e.tensor_scalar(
                M8, G8, SU[:, 1:2], None, ALU.is_ge), tag="rge")
            t_w = pr.op("vec", lambda e: e.tensor_tensor(M8, M8, G8, ALU.mult),
                        tag="rw")
            t_w8 = pr.op("vec", lambda e, tt_=tt: e.tensor_scalar(
                W8[:, tt_ * 8:(tt_ + 1) * 8], M8, SU[:, 5:6], None, ALU.mult),
                tag="rw8")
            w8_toks.append(t_w8)
        w8_done = w8_toks[-1]

        macc_war = [None] * 8
        res_last = None
        for tb in range(2):
            tb0 = tb * 8
            first_acc = True
            for e_ in [8] + list(range(8)):
                shared = (e_ == 8)
                if shared:
                    w1ap, w3ap, w2ap = (self.inp["sw1"].ap(), self.inp["sw3"].ap(),
                                        self.inp["sw2"].ap())
                else:
                    w1ap = self.inp["w1"].ap()[e_]
                    w3ap = self.inp["w3"].ap()[e_]
                    w2ap = self.inp["w2"].ap()[e_]
                for hg in range(2):
                    hg0 = hg * 8
                    g_toks = []
                    wld_prev = [None, None]
                    hbank = {0: None, 1: None, 2: None, 3: None}
                    gm_war = [None, None]
                    for hc8 in range(8):
                        hc = hg0 + hc8
                        t_w1 = pr.dma("pol",
                            W12[0].rearrange("p (kc n) -> p kc n", n=128),
                            w1ap.rearrange("(kc p) n -> p kc n", p=P)
                            [:, :, hc * 128:(hc + 1) * 128],
                            waits=[wld_prev[0]], tag="w1ld")
                        t_w3 = pr.dma("pol",
                            W12[1].rearrange("p (kc n) -> p kc n", n=128),
                            w3ap.rearrange("(kc p) n -> p kc n", p=P)
                            [:, :, hc * 128:(hc + 1) * 128],
                            waits=[wld_prev[1]], tag="w3ld")
                        mm1 = mm3 = None
                        for tc2 in range(2):
                            b1, b3 = tc2, 2 + tc2
                            for kc in range(8):
                                col = kc * T + tb0 * P + tc2 * 512
                                mm1 = pr.op("ten", lambda e, k=kc, cl=col, b=b1:
                                            e.matmul(PB[b][:, :],
                                                     W12[0][:, k * 128:(k + 1) * 128],
                                                     ZFM[:, cl:cl + 512],
                                                     start=(k == 0), stop=(k == 7)),
                                            waits=([t_w1, fm_tok, hbank[b1]]
                                                   if kc == 0 else []),
                                            tag="h1mm")
                            for kc in range(8):
                                col = kc * T + tb0 * P + tc2 * 512
                                mm3 = pr.op("ten", lambda e, k=kc, cl=col, b=b3:
                                            e.matmul(PB[b][:, :],
                                                     W12[1][:, k * 128:(k + 1) * 128],
                                                     ZFM[:, cl:cl + 512],
                                                     start=(k == 0), stop=(k == 7)),
                                            waits=([t_w3, fm_tok, hbank[b3]]
                                                   if kc == 0 else []),
                                            tag="h3mm")
                            t_si = pr.op("sca", lambda e, b=b1, t2=tc2: e.activation(
                                SG[t2], PB[b][:, :], ACT.Silu),
                                waits=[mm1, gm_war[tc2]], tag="silu1")
                            t_gm = pr.op("vec", lambda e, b=b3, h_=hc8, t2=tc2:
                                         e.tensor_tensor(
                                GEF[:, h_ * 1024 + t2 * 512:h_ * 1024 + (t2 + 1) * 512],
                                SG[t2], PB[b][:, :], ALU.mult),
                                waits=[t_si, mm3], tag="gmul2")
                            hbank[b1] = t_gm
                            hbank[b3] = t_gm
                            gm_war[tc2] = t_gm
                            g_toks.append(t_gm)
                        wld_prev = [mm1, mm3]
                    g_done = g_toks[-1]
                    t_w2 = pr.dma("pol", W2F.rearrange("p (hc n) -> p hc n", n=1024),
                                  w2ap.rearrange("(hcp p) n -> p hcp n", p=P)
                                  [:, hg0:hg0 + 8, :],
                                  waits=[wld_prev[0], wld_prev[1]], tag="w2ld")
                    ybank = {4: None, 5: None}
                    for ti in range(8):
                        tt = tb0 + ti
                        for dc in range(2):
                            bk = 4 + dc
                            mm = None
                            for hc8 in range(8):
                                mm = pr.op("ten", lambda e, h_=hc8, ti_=ti, d_=dc,
                                           b=bk: e.matmul(
                                    PB[b][:, :],
                                    GEF[:, h_ * 1024 + ti_ * P:h_ * 1024 + (ti_ + 1) * P],
                                    W2F[:, h_ * 1024 + d_ * 512:h_ * 1024 + (d_ + 1) * 512],
                                    start=(h_ == 0), stop=(h_ == 7)),
                                    waits=([t_w2, g_done, ybank[bk]] if hc8 == 0
                                           else []), tag="ymm")
                            if first_acc:
                                t_acc = pr.op("vec", lambda e, ti_=ti, d_=dc, b=bk:
                                              e.tensor_copy(
                                    MACC[:, ti_ * 1024 + d_ * 512:
                                         ti_ * 1024 + (d_ + 1) * 512], PB[b][:, :]),
                                    waits=[mm, macc_war[ti]], tag="yinit")
                            elif shared:
                                t_acc = pr.op("vec", lambda e, ti_=ti, d_=dc, b=bk:
                                              e.tensor_tensor(
                                    MACC[:, ti_ * 1024 + d_ * 512:
                                         ti_ * 1024 + (d_ + 1) * 512],
                                    MACC[:, ti_ * 1024 + d_ * 512:
                                         ti_ * 1024 + (d_ + 1) * 512],
                                    PB[b][:, :], ALU.add),
                                    waits=[mm, macc_war[ti]], tag="yshacc")
                            else:
                                t_acc = pr.op("vec", lambda e, ti_=ti, d_=dc, b=bk,
                                              ee=e_, tt_=tt: e.scalar_tensor_tensor(
                                    MACC[:, ti_ * 1024 + d_ * 512:
                                         ti_ * 1024 + (d_ + 1) * 512],
                                    PB[b][:, :], W8[:, tt_ * 8 + ee:tt_ * 8 + ee + 1],
                                    MACC[:, ti_ * 1024 + d_ * 512:
                                         ti_ * 1024 + (d_ + 1) * 512],
                                    ALU.mult, ALU.add),
                                    waits=[mm, macc_war[ti], w8_done], tag="yacc2")
                            ybank[bk] = t_acc
                            macc_war[ti] = t_acc
                    first_acc = False
            for ti in range(8):
                tt = tb0 + ti
                t_res = pr.op("vec", lambda e, ti_=ti, tt_=tt: e.tensor_tensor(
                    self.RES[:, tt_ * D:(tt_ + 1) * D],
                    self.RES[:, tt_ * D:(tt_ + 1) * D],
                    MACC[:, ti_ * 1024:(ti_ + 1) * 1024], ALU.add),
                    waits=[macc_war[ti], res_last], tag="res3")
                macc_war[ti] = t_res
                res_last = t_res
        pr.barrier_all()


def kernel(taps=(), **inputs):
    in_maps = prep_host(inputs)
    mk = MK(taps=taps)
    nc = mk.build()
    res = run_bass_kernel_spmd(nc, in_maps, core_ids=list(range(8)))
    out = np.empty((4, 4096, 1024), np.float32)
    for c in range(8):
        b, h = c // 2, c % 2
        out[b, h * T:(h + 1) * T] = res.results[c]["y"]
    if taps:
        return out, res.results
    return out
